# version 06
# baseline (speedup 1.0000x reference)
import numpy as np
import ml_dtypes
import concourse.bass as bass
import concourse.mybir as mybir
from concourse.bass_utils import run_bass_kernel_spmd
from concourse.aot_env import allow_ici_all_to_all
from contextlib import ExitStack

F32 = mybir.dt.float32
BF16 = mybir.dt.bfloat16
BF = ml_dtypes.bfloat16
AF = mybir.ActivationFunctionType
ALU = mybir.AluOpType

NCORES = 8
S, D, H, dH = 16384, 1024, 16, 64
C = 128
NCH = S // C
TPC = S // NCORES
INTER, E_EXP = 2048, 8
NEG = -1.0e30
PIECE = 128 * 2048
WCHUNK = 27 * PIECE

DEBUG = False
PHASE = "all"
RACE_DETECT = False
NCH_RUN = NCH
SCAN_PART = "all"


def _dummy_out(nc, E, es, y_out):
    with nc.sbuf_tensor("dummy0", [128, 1024], F32) as dz:
        tz = E.gpc(lambda g: g.memset(dz[:], 0.0))
        tl = None
        for i in range(16):
            tl = E.op("gp", lambda g, i=i: g.dma_start(
                y_out[i * 128:(i + 1) * 128, :], dz[:]), deps=[tz, tl])
        E.raw("gp", lambda g, tl=tl: g.wait_ge(E.sem["gp"], tl[1]))
        E.flush()


class Emit:
    ENGS = ("pe", "dv", "ac", "sp", "gp")

    def __init__(self, nc, es):
        self.nc = nc
        self.q = {k: [] for k in self.ENGS}
        self.sem = {k: es.enter_context(nc.semaphore(f"s_{k}")) for k in self.ENGS}
        self.sem["cc"] = es.enter_context(nc.semaphore("s_cc"))
        self.sem["gpc"] = es.enter_context(nc.semaphore("s_gpc"))
        self.cnt = {k: 0 for k in list(self.ENGS) + ["cc", "gpc"]}
        self.waited = {k: {} for k in self.ENGS}
        self.sep_fn = None
        self._last_rows = None

    def _deps(self, eng, deps):
        wl = []
        for d in deps:
            if d is None:
                continue
            if isinstance(d, list):
                for dd in d:
                    if dd is None:
                        continue
                    dsem, dval = dd
                    if dval > self.waited[eng].get(dsem, -1):
                        self.waited[eng][dsem] = dval
                        wl.append((dsem, dval))
            else:
                dsem, dval = d
                if dval > self.waited[eng].get(dsem, -1):
                    self.waited[eng][dsem] = dval
                    wl.append((dsem, dval))
        return tuple(wl)

    def addsem(self, es, nc, name):
        self.sem[name] = es.enter_context(nc.semaphore("s_" + name))
        self.cnt[name] = 0

    def op(self, eng, fn, deps=(), inc=True, rows=None, sem=None):
        step = 16 if eng in ("sp", "gp") else 1
        if eng == "pe":
            r = rows if rows is not None else (0, 128)
            last = getattr(self, "_last_rows", None)
            if (r != (0, 128) and last is not None and last != (0, 128)
                    and last != r and self.sep_fn is not None):
                sfn = self.sep_fn
                self.q["pe"].append(lambda e, sfn=sfn: sfn(e))
            self._last_rows = r
        wl = self._deps(eng, deps)
        tok = None
        key = sem if sem is not None else eng
        if inc:
            self.cnt[key] += step
            tok = (key, self.cnt[key])
        semmap = self.sem

        def run(e, fn=fn, wl=wl, inc=inc, step=step, key=key):
            for dsem, dval in wl:
                e.wait_ge(semmap[dsem], dval)
            inst = fn(e)
            if inc:
                inst.then_inc(semmap[key], step)
        self.q[eng].append(run)
        return tok

    def gpc(self, fn, deps=()):
        deps = list(deps) + ([("gpc", self.cnt["gpc"])] if self.cnt["gpc"] > 0 else [])
        wl = self._deps("gp", deps)
        self.cnt["gpc"] += 1
        tok = ("gpc", self.cnt["gpc"])
        semmap = self.sem

        def run(g, fn=fn, wl=wl):
            for dsem, dval in wl:
                g.wait_ge(semmap[dsem], dval)
            fn(g).then_inc(semmap["gpc"], 1)
        self.q["gp"].append(run)
        return tok

    def cc(self, fn, deps=()):
        wl = self._deps("gp", deps)
        self.cnt["cc"] += 1
        tok = ("cc", self.cnt["cc"])
        semmap = self.sem

        def run(g, fn=fn, wl=wl):
            for dsem, dval in wl:
                g.wait_ge(semmap[dsem], dval)
            fn(g).then_inc(semmap["cc"], 1)
        self.q["gp"].append(run)
        return tok

    def raw(self, eng, fn):
        self.q[eng].append(fn)

    def flush(self):
        nc = self.nc
        q = self.q
        with nc.Block() as blk:
            if q["pe"]:
                @blk.tensor
                def _(t, ops=tuple(q["pe"])):
                    for f in ops:
                        f(t)
            if q["dv"]:
                @blk.vector
                def _(v, ops=tuple(q["dv"])):
                    for f in ops:
                        f(v)
            if q["ac"]:
                @blk.scalar
                def _(a, ops=tuple(q["ac"])):
                    for f in ops:
                        f(a)
            if q["sp"]:
                @blk.sync
                def _(s, ops=tuple(q["sp"])):
                    for f in ops:
                        f(s)
            if q["gp"]:
                @blk.gpsimd
                def _(g, ops=tuple(q["gp"])):
                    for f in ops:
                        f(g)
        self.q = {k: [] for k in self.ENGS}


def build_program():
    nc = bass.Bass("TRN2", target_bir_lowering=False, detect_race_conditions=RACE_DETECT)

    def din(name, shape):
        return nc.dram_tensor(name, shape, F32, kind="ExternalInput").ap()

    x_rows = din("x_rows", [TPC, D])
    wq_s = din("wq_s", [D, 128])
    wk_s = din("wk_s", [D, 128])
    wv_s = din("wv_s", [D, 128])
    wrq_s = din("wrq_s", [D, 96])
    wrk_s = din("wrk_s", [D, 96])
    convq_s = din("convq_s", [128, 4])
    convk_s = din("convk_s", [128, 4])
    convv_s = din("convv_s", [128, 4])
    wf_s = din("wf_s", [D, 2])
    wl_s = din("wl_s", [D, 2])
    bfbl = din("bfbl", [4])
    gs_s = din("gs_s", [2])
    qn_w = din("qn_w", [dH])
    kn_w = din("kn_w", [dH])
    on_w = din("on_w", [D])
    wo = din("wo", [D, D])
    gate_w = din("gate_w", [E_EXP, D])
    gate_b = din("gate_b", [E_EXP])
    w1p = din("w1p", [E_EXP, INTER, 128])
    w3p = din("w3p", [E_EXP, INTER, 128])
    w2p = din("w2p", [E_EXP, D, 256])
    sw1p = din("sw1p", [INTER, 128])
    sw3p = din("sw3p", [INTER, 128])
    sw2p = din("sw2p", [D, 256])

    y_out = nc.dram_tensor("y", [TPC, D], F32, kind="ExternalOutput").ap()
    dbg = {}
    if DEBUG:
        for nm, shp in [("d_Qch", [128, S]), ("d_Kch", [128, S]), ("d_Ktok", [128, S]),
                        ("d_Vtok", [128, S]), ("d_lcum", [128, 256]), ("d_out", [128, S]),
                        ("d_x2", [TPC, D]), ("d_comb", [128, 128]), ("d_bt", [128, 256]),
                        ("d_zpk", [128, 256]), ("d_W", [128, 256]), ("d_EMT", [128, 256]),
                        ("d_NT", [128, 256]), ("d_exp", [128, 512]),
                        ("d_STb", [128, 64]), ("d_PT", [128, 128]), ("d_rstd", [1, 512]),
                        ("d_sqs", [1, 512])]:
            dt = F32 if nm in ("d_lcum", "d_x2", "d_comb", "d_bt", "d_zpk") else BF16
            dbg[nm] = nc.dram_tensor(nm, shp, dt, kind="ExternalOutput").ap()

    agX_in = nc.dram_tensor("agX_in", [D, TPC], BF16)
    agX_out = nc.dram_tensor("agX_out", [NCORES * D, TPC], BF16)
    agW_in = nc.dram_tensor("agW_in", [WCHUNK], BF16)
    agW_out = nc.dram_tensor("agW_out", [NCORES * WCHUNK], BF16)
    agO_in = nc.dram_tensor("agO_in", [NCORES, 128, TPC], BF16)
    agO_out = nc.dram_tensor("agO_out", [NCORES, 128, TPC], BF16)
    x2_scr = nc.dram_tensor("x2_scr", [TPC, D], F32)
    z_hbm = nc.dram_tensor("z_hbm", [4, S], F32)

    es = ExitStack()
    with es:
        E = Emit(nc, es)
        for _s in ("smq", "agxq", "wldq", "wstq", "p1wq", "xq0", "xq1", "zq", "rbq0",
                   "rbq1", "owq", "oq", "wobq", "g4q", "xrq", "x2wq", "x2q", "w1q",
                   "w3q", "w2q", "yq", "xbq"):
            E.addsem(es, nc, _s)
        sb = lambda n, shape, dt=F32: es.enter_context(nc.sbuf_tensor(n, shape, dt))

        identb = sb("identb", [128, 128], BF16)
        identf = sb("identf", [128, 128], F32)
        ones1f = sb("ones1f", [1, 128], F32)
        onesvb = sb("onesvb", [128, 1], BF16)
        qnw_c = sb("qnw_c", [128, 1], F32)
        knw_c = sb("knw_c", [128, 1], F32)
        lam_c = sb("lam_c", [128, 1], F32)
        onw_c = sb("onw_c", [128, 8], F32)
        cvq_c = sb("cvq_c", [128, 4], F32)
        cvk_c = sb("cvk_c", [128, 4], F32)
        cvv_c = sb("cvv_c", [128, 4], F32)
        bfbl4 = sb("bfbl4", [4, 1], F32)
        gsbuf = sb("gsbuf", [2, 1], F32)
        absgs = sb("absgs", [2, 1], F32)
        gbrow = sb("gbrow", [1, 8], F32)
        ceps6 = sb("ceps6", [128, 1], F32)
        ceps24 = sb("ceps24", [128, 1], F32)
        ceps5 = sb("ceps5", [128, 1], F32)

        cst = None
        for fn in [
            lambda g: g.memset(identb[:], 0.0),
            lambda g: g.affine_select(out=identb[:], in_=identb[:], compare_op=ALU.not_equal,
                                      fill=1.0, base=0, pattern=[[-1, 128]], channel_multiplier=1),
            lambda g: g.memset(identf[:], 0.0),
            lambda g: g.affine_select(out=identf[:], in_=identf[:], compare_op=ALU.not_equal,
                                      fill=1.0, base=0, pattern=[[-1, 128]], channel_multiplier=1),
            lambda g: g.memset(ones1f[:], 1.0),
            lambda g: g.memset(onesvb[:], 1.0),
            lambda g: g.memset(ceps6[:], 1e-6),
            lambda g: g.memset(ceps24[:], 1e-24),
            lambda g: g.memset(ceps5[:], 1e-5),
        ]:
            cst = E.gpc(fn)

        sm = []
        sm.append(E.op("gp", lambda g: g.dma_start(qnw_c[0:64, 0:1],
            qn_w.rearrange("(a b) -> a b", b=1), allow_slow_non_contiguous=True), sem="smq"))
        sm.append(E.op("gp", lambda g: g.dma_start(qnw_c[64:128, 0:1],
            qn_w.rearrange("(a b) -> a b", b=1), allow_slow_non_contiguous=True), sem="smq"))
        sm.append(E.op("gp", lambda g: g.dma_start(knw_c[0:64, 0:1],
            kn_w.rearrange("(a b) -> a b", b=1), allow_slow_non_contiguous=True), sem="smq"))
        sm.append(E.op("gp", lambda g: g.dma_start(knw_c[64:128, 0:1],
            kn_w.rearrange("(a b) -> a b", b=1), allow_slow_non_contiguous=True), sem="smq"))
        sm.append(E.op("gp", lambda g: g.dma_start(onw_c[:],
            on_w.rearrange("(b p) -> p b", p=128), allow_slow_non_contiguous=True), sem="smq"))
        sm.append(E.op("gp", lambda g: g.dma_start(cvq_c[:], convq_s), sem="smq"))
        sm.append(E.op("gp", lambda g: g.dma_start(cvk_c[:], convk_s), sem="smq"))
        sm.append(E.op("gp", lambda g: g.dma_start(cvv_c[:], convv_s), sem="smq"))
        sm.append(E.op("gp", lambda g: g.dma_start(bfbl4[0:4, 0:1],
            bfbl.rearrange("(a b) -> a b", b=1), allow_slow_non_contiguous=True), sem="smq"))
        sm.append(E.op("gp", lambda g: g.dma_start(gsbuf[:],
            gs_s.rearrange("(a b) -> a b", b=1), allow_slow_non_contiguous=True), sem="smq"))
        sm.append(E.op("gp", lambda g: g.dma_start(gbrow[0:1, :],
            gate_b.rearrange("(b a) -> b a", b=1)), sem="smq"))
        sm_last = sm[-1]
        t_ag = E.op("ac", lambda a: a.activation(absgs[:], gsbuf[:], AF.Abs), deps=[sm[9]])

        with ExitStack() as es0:
            sb0 = lambda n, shape, dt=BF16: es0.enter_context(nc.sbuf_tensor(n, shape, dt))
            xbf = sb0("xbf", [128, S])
            xts = sb0("xts", [128, S])
            wtb = sb0("wtb", [128, 2048])
            wts = sb0("wts", [128, 2048])
            psx = es0.enter_context(nc.psum_tensor("psx", [128, 1024], BF16))
            psw = es0.enter_context(nc.psum_tensor("psw", [128, 1024], BF16))

            t_xld = E.op("gp", lambda g: g.dma_start(xbf[:],
                x_rows.rearrange("(g p) m -> p g m", p=128)), sem="xbq")
            ev_last = None
            ev_all = []
            for g_i in range(16):
                tt = None
                for mb in range(8):
                    tt = E.op("pe", lambda t, g_i=g_i, mb=mb: t.transpose(
                        psx[:, mb * 128:(mb + 1) * 128],
                        xbf[:, g_i * 1024 + mb * 128: g_i * 1024 + (mb + 1) * 128],
                        identb[:]),
                        deps=[t_xld, cst, ev_last] if mb == 0 else (), inc=(mb == 7))
                ev_last = E.op("dv", lambda v, g_i=g_i: v.tensor_copy(
                    bass.AP(xts, g_i * 128, [[S, 128], [2048, 8], [1, 128]]),
                    psx[:, 0:1024]), deps=[tt])
                ev_all.append(ev_last)
            t_agxw = E.op("gp", lambda g: g.dma_start(
                agX_in.ap().rearrange("(b p) t -> p b t", p=128), xts[:]), deps=[ev_last],
                sem="agxq")
            ag1 = E.cc(lambda g: g.collective_compute(
                "AllGather", ALU.bypass, replica_groups=[list(range(NCORES))],
                ins=[agX_in.ap().opt()], outs=[agX_out.ap().opt()]), deps=[t_agxw])

            pieces = []
            for e in range(E_EXP):
                pieces.append((w1p[e, :, :], 16, e * 3 * PIECE))
                pieces.append((w3p[e, :, :], 16, e * 3 * PIECE + PIECE))
                pieces.append((w2p[e, :, 0:128], 8, e * 3 * PIECE + 2 * PIECE))
                pieces.append((w2p[e, :, 128:256], 8, e * 3 * PIECE + 2 * PIECE + 128 * 1024))
            base_sh = E_EXP * 3 * PIECE
            pieces.append((sw1p[:, :], 16, base_sh))
            pieces.append((sw3p[:, :], 16, base_sh + PIECE))
            pieces.append((sw2p[:, 0:128], 8, base_sh + 2 * PIECE))
            pieces.append((sw2p[:, 128:256], 8, base_sh + 2 * PIECE + 128 * 1024))

            prev_out = None
            last_wev = None
            last_tt = None
            for src, nblk, coff in pieces:
                t_ld = E.op("gp", lambda g, src=src, nblk=nblk: g.dma_start(
                    wtb[:, 0:nblk * 128], src.rearrange("(k p) m -> p k m", p=128)),
                    deps=[prev_out, last_tt], sem="wldq")
                evs = []
                for half in range((nblk + 7) // 8):
                    kk0 = half * 8
                    kn = min(8, nblk - kk0)
                    tt = None
                    for k in range(kn):
                        tt = E.op("pe", lambda t, k=k, kk0=kk0: t.transpose(
                            psw[:, k * 128:(k + 1) * 128],
                            wtb[:, (kk0 + k) * 128:(kk0 + k + 1) * 128], identb[:]),
                            deps=[t_ld, cst, last_wev] if k == 0 else (), inc=(k == kn - 1))
                    last_tt = tt
                    last_wev = E.op("dv", lambda v, kk0=kk0, kn=kn: v.tensor_copy(
                        wts[:, kk0 * 128:(kk0 + kn) * 128], psw[:, 0:kn * 128]),
                        deps=[tt, prev_out])
                    evs.append(last_wev)
                prev_out = E.op("gp", lambda g, nblk=nblk, coff=coff: g.dma_start(
                    bass.AP(agW_in, coff, [[nblk * 128, 128], [1, nblk * 128]]),
                    wts[:, 0:nblk * 128]), deps=evs, sem="wstq")
            ag2 = E.cc(lambda g: g.collective_compute(
                "AllGather", ALU.bypass, replica_groups=[list(range(NCORES))],
                ins=[agW_in.ap().opt()], outs=[agW_out.ap().opt()]), deps=[prev_out])

            E.flush()

        if PHASE == "p0":
            _dummy_out(nc, E, es, y_out)
            return nc

        with ExitStack() as es1:
            sb1 = lambda n, shape, dt=F32: es1.enter_context(nc.sbuf_tensor(n, shape, dt))
            psf1 = lambda n, shape: es1.enter_context(nc.psum_tensor(n, shape, F32))
            psb1 = lambda n, shape: es1.enter_context(nc.psum_tensor(n, shape, BF16))

            Kch = sb1("Kch", [128, S], BF16)
            Qch = sb1("Qch", [128, S], BF16)
            Ktok = sb1("Ktok", [128, S], BF16)
            Vtok = sb1("Vtok", [128, S], BF16)
            obuf0 = sb1("obuf0", [128, 128], BF16)
            obuf1 = sb1("obuf1", [128, 128], BF16)
            xt0 = sb1("xt0", [128, 4096], BF16)
            xt1 = sb1("xt1", [128, 4096], BF16)
            wqb8 = sb1("wqb8", [128, 1024], BF16)
            wkb8 = sb1("wkb8", [128, 1024], BF16)
            wvb8 = sb1("wvb8", [128, 1024], BF16)
            wrq8 = sb1("wrq8", [128, 8 * 48], BF16)
            wfl8 = sb1("wfl8", [128, 8 * 4], BF16)
            wrq8b = sb1("wrq8b", [128, 8 * 48], BF16)
            wrk8 = sb1("wrk8", [128, 8 * 48], BF16)
            wrk8b = sb1("wrk8b", [128, 8 * 48], BF16)
            qw = sb1("qw", [128, 516], BF16)
            kw = sb1("kw", [128, 516], BF16)
            vw = sb1("vw", [128, 516], BF16)
            cq = sb1("cq", [128, 512], BF16)
            ck = sb1("ck", [128, 512], BF16)
            cv = sb1("cv", [128, 512], BF16)
            ktmp = sb1("ktmp", [128, 512], BF16)
            rqsb = sb1("rqsb", [128, 512], BF16)
            rksb = sb1("rksb", [128, 512], BF16)
            q2t = sb1("q2t", [128, 512], BF16)
            k2t = sb1("k2t", [128, 512], BF16)
            flst = sb1("flst", [4, 512], F32)
            rr1 = sb1("rr1", [2, 512], F32)
            rr2 = sb1("rr2", [2, 512], F32)
            rr3 = sb1("rr3", [2, 512], F32)

            triu_f = sb1("triu_f", [128, 128], F32)
            inclm = sb1("inclm", [128, 256], F32)
            strictm = sb1("strictm", [128, 256], F32)
            bdones = sb1("bdones", [128, 2], BF16)
            self2f = sb1("self2f", [2, 128], F32)
            sel48 = sb1("sel48", [2, 128], F32)
            i64two = sb1("i64two", [128, 64], F32)

            pP1 = psf1("pP1", [128, 512])
            pP2 = psf1("pP2", [128, 512])
            pRED = psf1("pRED", [128, 512])
            pBC1 = psf1("pBC1", [128, 512])
            pBC2 = psf1("pBC2", [128, 512])
            pT1 = psb1("pT1", [128, 1024])
            pFL = psf1("pFL", [128, 512])

            twq = E.op("gp", lambda g: g.dma_start(wqb8[:],
                wq_s.rearrange("(b p) c -> p b c", p=128)), sem="p1wq")
            twk = E.op("gp", lambda g: g.dma_start(wkb8[:],
                wk_s.rearrange("(b p) c -> p b c", p=128)), sem="p1wq")
            twv = E.op("gp", lambda g: g.dma_start(wvb8[:],
                wv_s.rearrange("(b p) c -> p b c", p=128)), sem="p1wq")
            tw1 = E.op("gp", lambda g: g.dma_start(
                bass.AP(wrq8, 0, [[8 * 48, 128], [48, 8], [1, 48]]),
                wrq_s.rearrange("(b p) c -> p b c", p=128)[:, :, 0:48]), sem="p1wq")
            tw2 = E.op("gp", lambda g: g.dma_start(
                bass.AP(wfl8, 0, [[8 * 4, 128], [4, 8], [1, 2]]),
                wf_s.rearrange("(b p) c -> p b c", p=128)), sem="p1wq")
            tw3 = E.op("gp", lambda g: g.dma_start(
                bass.AP(wfl8, 2, [[8 * 4, 128], [4, 8], [1, 2]]),
                wl_s.rearrange("(b p) c -> p b c", p=128)), sem="p1wq")
            tw4 = E.op("gp", lambda g: g.dma_start(
                bass.AP(wrq8b, 0, [[8 * 48, 128], [48, 8], [1, 48]]),
                wrq_s.rearrange("(b p) c -> p b c", p=128)[:, :, 48:96]), sem="p1wq")
            tw5 = E.op("gp", lambda g: g.dma_start(
                bass.AP(wrk8, 0, [[8 * 48, 128], [48, 8], [1, 48]]),
                wrk_s.rearrange("(b p) c -> p b c", p=128)[:, :, 0:48]), sem="p1wq")
            tw6 = E.op("gp", lambda g: g.dma_start(
                bass.AP(wrk8b, 0, [[8 * 48, 128], [48, 8], [1, 48]]),
                wrk_s.rearrange("(b p) c -> p b c", p=128)[:, :, 48:96]), sem="p1wq")

            cst1 = None
            for fn in [
                lambda g: g.memset(triu_f[:], 0.0),
                lambda g: g.affine_select(out=triu_f[:], in_=triu_f[:], compare_op=ALU.is_gt,
                                          fill=1.0, base=0, pattern=[[-1, 128]], channel_multiplier=1),
                lambda g: g.memset(inclm[:], 0.0),
                lambda g: g.affine_select(out=inclm[:, 0:128], in_=inclm[:, 0:128], compare_op=ALU.is_ge,
                                          fill=NEG, base=0, pattern=[[1, 128]], channel_multiplier=-1),
                lambda g: g.affine_select(out=inclm[:, 128:256], in_=inclm[:, 128:256], compare_op=ALU.is_ge,
                                          fill=NEG, base=0, pattern=[[1, 128]], channel_multiplier=-1),
                lambda g: g.memset(strictm[:], 0.0),
                lambda g: g.affine_select(out=strictm[:, 0:128], in_=strictm[:, 0:128], compare_op=ALU.is_gt,
                                          fill=NEG, base=0, pattern=[[1, 128]], channel_multiplier=-1),
                lambda g: g.affine_select(out=strictm[:, 128:256], in_=strictm[:, 128:256], compare_op=ALU.is_gt,
                                          fill=NEG, base=0, pattern=[[1, 128]], channel_multiplier=-1),
                lambda g: g.memset(bdones[:], 0.0),
                lambda g: g.memset(bdones[0:64, 0:1], 1.0),
                lambda g: g.memset(bdones[64:128, 1:2], 1.0),
                lambda g: g.memset(self2f[:], 1.0),
                lambda g: g.affine_select(out=self2f[:], in_=self2f[:], compare_op=ALU.is_ge,
                                          fill=0.0, base=0, pattern=[[1, 128]], channel_multiplier=-64),
                lambda g: g.affine_select(out=self2f[:], in_=self2f[:], compare_op=ALU.is_ge,
                                          fill=0.0, base=63, pattern=[[-1, 128]], channel_multiplier=64),
                lambda g: g.memset(sel48[:], 1.0),
                lambda g: g.affine_select(out=sel48[:], in_=sel48[:], compare_op=ALU.is_ge,
                                          fill=0.0, base=0, pattern=[[1, 128]], channel_multiplier=-64),
                lambda g: g.affine_select(out=sel48[:], in_=sel48[:], compare_op=ALU.is_ge,
                                          fill=0.0, base=47, pattern=[[-1, 128]], channel_multiplier=64),
            ]:
                cst1 = E.gpc(fn)
            t_i64a = E.op("dv", lambda v: v.tensor_add(i64two[0:64, :], identf[0:64, 0:64],
                identf[0:64, 64:128]), deps=[cst1])
            t_i64b = E.op("dv", lambda v: v.tensor_add(i64two[64:128, :], identf[64:128, 0:64],
                identf[64:128, 64:128]), deps=[t_i64a])
            t_lam0 = E.op("pe", lambda t: t.matmul(pRED[:, 0:1], sel48[:], absgs[:],
                start=True, stop=True), deps=[t_ag, cst1])
            t_lam = E.op("dv", lambda v: v.tensor_copy(lam_c[:], pRED[:, 0:1]), deps=[t_lam0])

            h0 = E.gpc(lambda g: g.memset(qw[:, 0:3], 0.0))
            h1 = E.gpc(lambda g: g.memset(kw[:, 0:3], 0.0))
            h2 = E.gpc(lambda g: g.memset(vw[:, 0:3], 0.0))

            xtok = [None, None]
            xt_rd = [None, None]
            halo_q, halo_k, halo_v = [h0], [h1], [h2]
            evQp = evKp = evVp = None
            fcq = fck = fcv = None
            fFL = None
            fq2 = fk2 = None
            fktmp = None
            fpRED = t_lam
            bc1_free = bc2_free = None
            trK = trV = None
            zstage = None
            rr_free = [None, None, None]

            for g_i in range(32):
                xt = xt0 if g_i % 2 == 0 else xt1
                t0g = g_i * 512
                rank = t0g // TPC
                toff = t0g % TPC
                gsl = slice(g_i * 512, (g_i + 1) * 512)
                xtok[g_i % 2] = E.op("sp", lambda s, rank=rank, toff=toff, xt=xt:
                    s.dma_start(bass.AP(xt, 0, [[4096, 128], [512, 8], [1, 512]]),
                                bass.AP(agX_out, rank * D * TPC + toff,
                                        [[TPC, 128], [128 * TPC, 8], [1, 512]])),
                    deps=[xtok[g_i % 2], xt_rd[g_i % 2], ag1, tw6],
                    sem=("xq0" if g_i % 2 == 0 else "xq1"))
                t_x = xtok[g_i % 2]

                def proj(bank, wtile, ncols, deps, pstart=True):
                    tk = None
                    for mb in range(8):
                        tk = E.op("pe", lambda t, mb=mb, bank=bank, wtile=wtile, ncols=ncols, xt=xt:
                            t.matmul(bank[0:ncols, 0:512],
                                     wtile[:, mb * ncols:(mb + 1) * ncols],
                                     xt[:, mb * 512:(mb + 1) * 512],
                                     start=(mb == 0 and pstart), stop=(mb == 7)),
                            deps=deps if mb == 0 else (), inc=(mb == 7))
                    return tk

                tQ = proj(pP1, wqb8, 128, [t_x, twq, evQp])
                tK = proj(pP2, wkb8, 128, [t_x, twk, evKp])
                evQ = E.op("ac", lambda a: a.activation(qw[:, 3:515], pP1[:, 0:512], AF.Copy),
                           deps=[tQ, halo_q[-1]])
                tV = proj(pP1, wvb8, 128, [evQ, twv])
                evK = E.op("ac", lambda a: a.activation(kw[:, 3:515], pP2[:, 0:512], AF.Copy),
                           deps=[tK, halo_k[-1]])
                tRa = None
                for mb in range(8):
                    tRa = E.op("pe", lambda t, mb=mb, xt=xt: t.matmul(
                        pP2[0:48, 0:512], wrq8[:, mb * 48:(mb + 1) * 48],
                        xt[:, mb * 512:(mb + 1) * 512], start=(mb == 0), stop=(mb == 7)),
                        deps=[evK, tw6] if mb == 0 else (), inc=False)
                for mb in range(8):
                    tRa = E.op("pe", lambda t, mb=mb, xt=xt: t.matmul(
                        pP2[64:112, 0:512], wrq8b[:, mb * 48:(mb + 1) * 48],
                        xt[:, mb * 512:(mb + 1) * 512], start=(mb == 0), stop=(mb == 7)),
                        deps=[tw6] if mb == 0 else (), inc=(mb == 7))
                tFL = None
                for mb in range(8):
                    tFL = E.op("pe", lambda t, mb=mb, xt=xt: t.matmul(
                        pFL[0:4, 0:512], wfl8[:, mb * 4:(mb + 1) * 4],
                        xt[:, mb * 512:(mb + 1) * 512], start=(mb == 0), stop=(mb == 7)),
                        deps=[tw6, fFL] if mb == 0 else (), inc=(mb == 7))
                evV = E.op("ac", lambda a: a.activation(vw[:, 3:515], pP1[:, 0:512], AF.Copy),
                           deps=[tV, halo_v[-1]])
                evR1 = E.op("dv", lambda v: v.tensor_copy(rqsb[:, 0:512], pP2[:, 0:512]),
                            deps=[tRa])
                evFL = E.op("dv", lambda v: v.tensor_scalar(flst[0:4, 0:512],
                    pFL[0:4, 0:512], bfbl4[0:4, 0:1], None, op0=ALU.add),
                    deps=[tFL, zstage, sm_last])
                fFL = evFL
                zstage = E.op("gp", lambda g, g_i=g_i: g.dma_start(
                    bass.AP(z_hbm, g_i * 512, [[S, 4], [1, 512]]), flst[0:4, :]),
                    deps=[evFL], sem="zq")
                tRk = None
                for mb in range(8):
                    tRk = E.op("pe", lambda t, mb=mb, xt=xt: t.matmul(
                        pP1[0:48, 0:512], wrk8[:, mb * 48:(mb + 1) * 48],
                        xt[:, mb * 512:(mb + 1) * 512], start=(mb == 0), stop=(mb == 7)),
                        deps=[evV, tw6] if mb == 0 else (), inc=False)
                for mb in range(8):
                    tRk = E.op("pe", lambda t, mb=mb, xt=xt: t.matmul(
                        pP1[64:112, 0:512], wrk8b[:, mb * 48:(mb + 1) * 48],
                        xt[:, mb * 512:(mb + 1) * 512], start=(mb == 0), stop=(mb == 7)),
                        deps=[tw6] if mb == 0 else (), inc=(mb == 7))
                evR2 = E.op("dv", lambda v: v.tensor_copy(rksb[:, 0:512], pP1[0:128, 0:512]),
                            deps=[tRk])
                xt_rd[g_i % 2] = tRk
                evQp = evR2
                evKp = [evR1, evFL]

                def conv(dst, src, wcol, dep0, dfree=None):
                    t1 = E.op("dv", lambda v: v.tensor_scalar(dst[:, 0:512], src[:, 0:512],
                        wcol[:, 0:1], None, op0=ALU.mult), deps=[dep0, sm_last, dfree])
                    for j in range(1, 4):
                        t1 = E.op("dv", lambda v, j=j: v.scalar_tensor_tensor(
                            dst[:, 0:512], src[:, j:512 + j], wcol[:, j:j + 1], dst[:, 0:512],
                            op0=ALU.mult, op1=ALU.add), deps=[t1])
                    return t1
                tcq = conv(cq, qw, cvq_c, evQ, fcq)
                tck = conv(ck, kw, cvk_c, evK, fck)
                tcv = conv(cv, vw, cvv_c, evV, fcv)
                halo_q.append(E.op("dv", lambda v: v.tensor_copy(qw[:, 0:3], qw[:, 512:515]),
                                   deps=[tcq]))
                halo_k.append(E.op("dv", lambda v: v.tensor_copy(kw[:, 0:3], kw[:, 512:515]),
                                   deps=[tck]))
                halo_v.append(E.op("dv", lambda v: v.tensor_copy(vw[:, 0:3], vw[:, 512:515]),
                                   deps=[tcv]))

                tq2 = E.op("ac", lambda a: a.activation(q2t[:], cq[:], AF.Square),
                           deps=[tcq, fq2])
                trq = E.op("pe", lambda t: t.matmul(pRED[0:2, 0:512], bdones[:], q2t[:],
                    start=True, stop=True), deps=[tq2, cst1, fpRED])
                tl1 = E.op("ac", lambda a: a.activation(rr1[:], pRED[0:2, 0:512], AF.Ln,
                    scale=1.0 / dH, bias=ceps6[0:2, 0:1]), deps=[trq, rr_free[0]])
                tl2 = E.op("ac", lambda a: a.activation(rr1[:], rr1[:], AF.Exp, scale=-0.5),
                           deps=[tl1])
                tbq = E.op("pe", lambda t: t.matmul(pBC1[:, 0:512], self2f[:], rr1[:],
                    start=True, stop=True), deps=[tl2, bc1_free])
                rr_free[0] = tbq
                tqa = E.op("dv", lambda v, gsl=gsl: v.scalar_tensor_tensor(
                    Qch[:, gsl], cq[:], qnw_c[:, 0:1], pBC1[:, 0:512],
                    op0=ALU.mult, op1=ALU.mult), deps=[tbq])
                tk2 = E.op("ac", lambda a: a.activation(k2t[:], ck[:], AF.Square),
                           deps=[tck, fk2])
                trk = E.op("pe", lambda t: t.matmul(pRED[32:34, 0:512], bdones[:], k2t[:],
                    start=True, stop=True), deps=[tk2, tl1], inc=True)
                tl3 = E.op("ac", lambda a: a.activation(rr2[:], pRED[32:34, 0:512], AF.Ln,
                    scale=1.0 / dH, bias=ceps6[0:2, 0:1]), deps=[trk, rr_free[1]])
                tl4 = E.op("ac", lambda a: a.activation(rr2[:], rr2[:], AF.Exp, scale=-0.5),
                           deps=[tl3])
                tbk = E.op("pe", lambda t: t.matmul(pBC2[:, 0:512], self2f[:], rr2[:],
                    start=True, stop=True), deps=[tl4, bc2_free])
                rr_free[1] = tbk
                tka = E.op("dv", lambda v: v.scalar_tensor_tensor(
                    ktmp[:], ck[:], knw_c[:, 0:1], pBC2[:, 0:512],
                    op0=ALU.mult, op1=ALU.mult), deps=[tbk, fktmp])
                bc2_free = tka
                tqb1 = E.op("dv", lambda v, gsl=gsl: v.scalar_tensor_tensor(
                    Qch[0:48, gsl], rqsb[0:48, :], lam_c[0:48, 0:1], Qch[0:48, gsl],
                    op0=ALU.mult, op1=ALU.add), deps=[tqa, evR1, t_lam])
                tqb2 = E.op("dv", lambda v, gsl=gsl: v.scalar_tensor_tensor(
                    Qch[64:112, gsl], rqsb[64:112, :], lam_c[64:112, 0:1], Qch[64:112, gsl],
                    op0=ALU.mult, op1=ALU.add), deps=[tqb1])
                tkb1 = E.op("dv", lambda v: v.scalar_tensor_tensor(
                    ktmp[0:48, :], rksb[0:48, :], lam_c[0:48, 0:1], ktmp[0:48, :],
                    op0=ALU.mult, op1=ALU.add), deps=[tka, evR2])
                tkb2 = E.op("dv", lambda v: v.scalar_tensor_tensor(
                    ktmp[64:112, :], rksb[64:112, :], lam_c[64:112, 0:1], ktmp[64:112, :],
                    op0=ALU.mult, op1=ALU.add), deps=[tkb1])
                tk2b = E.op("ac", lambda a: a.activation(k2t[:], ktmp[:], AF.Square),
                            deps=[tkb2, trk])
                trk2 = E.op("pe", lambda t: t.matmul(pRED[64:66, 0:512], bdones[:], k2t[:],
                    start=True, stop=True), deps=[tk2b, tl3], inc=True)
                tl5 = E.op("ac", lambda a: a.activation(rr3[:], pRED[64:66, 0:512], AF.Ln,
                    scale=1.0, bias=ceps24[0:2, 0:1]), deps=[trk2, rr_free[2]])
                tl6 = E.op("ac", lambda a: a.activation(rr3[:], rr3[:], AF.Exp, scale=-0.5),
                           deps=[tl5])
                tbk2 = E.op("pe", lambda t: t.matmul(pBC1[:, 0:512], self2f[:], rr3[:],
                    start=True, stop=True), deps=[tl6, tqa])
                rr_free[2] = tbk2
                tkn = E.op("dv", lambda v, gsl=gsl: v.tensor_mul(Kch[:, gsl], ktmp[:],
                    pBC1[:, 0:512]), deps=[tbk2])
                bc1_free = tkn
                ttk = None
                for j in range(4):
                    ttk = E.op("pe", lambda t, j=j, g_i=g_i: t.transpose(
                        pT1[:, j * 128:(j + 1) * 128],
                        Kch[:, g_i * 512 + j * 128: g_i * 512 + (j + 1) * 128], identb[:]),
                        deps=[tkn, trK, trV] if j == 0 else (), inc=False)
                for j in range(4):
                    ttk = E.op("pe", lambda t, j=j: t.transpose(
                        pT1[:, 512 + j * 128: 512 + (j + 1) * 128],
                        cv[:, j * 128:(j + 1) * 128], identb[:]),
                        deps=[tcv] if j == 0 else (), inc=(j == 3))
                trK = E.op("dv", lambda v, gsl=gsl: v.tensor_copy(Ktok[:, gsl], pT1[:, 0:512]),
                           deps=[ttk])
                trV = E.op("dv", lambda v, gsl=gsl: v.tensor_copy(Vtok[:, gsl],
                    pT1[:, 512:1024]), deps=[ttk])
                fcq = tq2
                fck = tk2
                fcv = ttk
                fq2 = trq
                fk2 = trk2
                fktmp = tk2b
                fpRED = tl5

            spc = sb1("spc", [128, 256], F32)
            spzlc = sb1("spzlc", [128, 256], F32)
            sp_t = sb1("sp_t", [128, 256], F32)
            spzl_t = sb1("spzl_t", [128, 256], F32)
            lcum_t = sb1("lcum_t", [128, 256], F32)
            lcum_c = sb1("lcum_c", [128, 256], F32)
            npos_c = sb1("npos_c", [128, 256], F32)
            b_t = sb1("b_t", [128, 256], F32)
            c2_t = sb1("c2_t", [128, 256], F32)
            d2_t = sb1("d2_t", [128, 256], F32)
            gC2 = sb1("gC2", [128, 128], F32)
            tmp_t = sb1("tmp_t", [128, 256], F32)
            rbufL = sb1("rbufL", [1, 256], F32)

            tr_sp = []
            for r, dst, hcol in ((0, spc, 0), (1, spc, 1), (2, spzlc, 0), (3, spzlc, 1)):
                tr_sp.append(E.op("gp", lambda g, r=r, dst=dst, hcol=hcol: g.dma_start(
                    dst[:, hcol * 128:(hcol + 1) * 128],
                    bass.AP(z_hbm, r * S, [[128, 128], [1, 128]])),
                    deps=[zstage]))
            t_spa = E.op("ac", lambda a: a.activation(spc[:], spc[:], AF.Exp, scale=-1.0),
                         deps=tr_sp)
            t_spb = E.op("ac", lambda a: a.activation(spc[:], spc[:], AF.Ln, bias=1.0),
                         deps=[t_spa])
            t_spc2 = E.op("ac", lambda a: a.activation(spzlc[:], spzlc[:], AF.Exp, scale=-1.0),
                          deps=[t_spb])
            t_sp2 = E.op("ac", lambda a: a.activation(spzlc[:], spzlc[:], AF.Ln, bias=1.0),
                         deps=[t_spc2])
            tt1 = E.op("pe", lambda t: t.transpose(pBC1[:, 0:128], spc[:, 0:128], identf[:]),
                       deps=[t_spb, bc1_free, tkn], inc=False)
            tt2 = E.op("pe", lambda t: t.transpose(pBC1[:, 128:256], spc[:, 128:256], identf[:]),
                       inc=True)
            t_spt = E.op("dv", lambda v: v.tensor_copy(sp_t[:], pBC1[:, 0:256]), deps=[tt2])
            tt3 = E.op("pe", lambda t: t.transpose(pBC1[:, 256:384], spzlc[:, 0:128], identf[:]),
                       deps=[t_spt, t_sp2], inc=False)
            tt4 = E.op("pe", lambda t: t.transpose(pBC1[:, 384:512], spzlc[:, 128:256], identf[:]),
                       inc=True)
            t_szt = E.op("dv", lambda v: v.tensor_copy(spzl_t[:], pBC1[:, 256:512]), deps=[tt4])
            t_lc0 = E.op("pe", lambda t: t.matmul(pBC2[:, 0:256], triu_f[:], sp_t[:],
                start=True, stop=True), deps=[t_spt, bc2_free, cst1])
            t_lct = E.op("dv", lambda v: v.tensor_copy(lcum_t[:], pBC2[:, 0:256]), deps=[t_lc0])
            tt5 = E.op("pe", lambda t: t.transpose(pBC1[:, 0:128], lcum_t[:, 0:128], identf[:]),
                       deps=[t_lct, t_spt, t_szt], inc=False)
            tt6 = E.op("pe", lambda t: t.transpose(pBC1[:, 128:256], lcum_t[:, 128:256], identf[:]),
                       inc=True)
            t_lcc = E.op("dv", lambda v: v.tensor_copy(lcum_c[:], pBC1[:, 0:256]), deps=[tt6])
            t_np1 = E.op("dv", lambda v: v.tensor_sub(npos_c[:], lcum_c[:], spc[:]), deps=[t_lcc])
            t_np2 = E.op("dv", lambda v: v.tensor_add(npos_c[:], npos_c[:], spzlc[:]),
                         deps=[t_np1])
            t_bt = E.op("ac", lambda a: a.activation(b_t[:], spzl_t[:], AF.Exp, scale=-1.0),
                        deps=[t_szt])
            t_c21 = E.op("dv", lambda v: v.tensor_sub(tmp_t[:], sp_t[:], spzl_t[:]),
                         deps=[t_szt, t_spt])
            t_c22 = E.op("dv", lambda v: v.tensor_sub(tmp_t[:], tmp_t[:], lcum_t[:]),
                         deps=[t_c21, t_lct])
            t_c2t = E.op("ac", lambda a: a.activation(c2_t[:], tmp_t[:], AF.Exp), deps=[t_c22])
            t_rb = E.op("sp", lambda s: s.dma_start(rbufL[0:1, :], lcum_t[127:128, 0:256]),
                        deps=[t_lct])
            t_bcL = E.op("pe", lambda t: t.matmul(pBC2[:, 0:256], ones1f[:], rbufL[0:1, :],
                start=True, stop=True), deps=[t_rb, t_lcc])
            t_d21 = E.op("dv", lambda v: v.tensor_sub(tmp_t[:], lcum_t[:], pBC2[:, 0:256]),
                         deps=[t_bcL, t_c2t])
            t_d2t = E.op("ac", lambda a: a.activation(d2_t[:], tmp_t[:], AF.Exp), deps=[t_d21])
            t_gcb = E.op("ac", lambda a: a.activation(tmp_t[:], pBC2[:, 0:256], AF.Exp,
                scale=-1.0), deps=[t_d21, t_d2t])
            t_gc1 = E.op("dv", lambda v: v.tensor_copy(gC2[0:64, :], tmp_t[0:64, 0:128]),
                         deps=[t_gcb])
            t_gc2 = E.op("dv", lambda v: v.tensor_copy(gC2[64:128, :], tmp_t[64:128, 128:256]),
                         deps=[t_gcb])
            if DEBUG:
                E.op("gp", lambda g: g.dma_start(dbg["d_lcum"], lcum_t[:]), deps=[t_lct])
                E.op("gp", lambda g: g.dma_start(dbg["d_bt"], b_t[:]), deps=[t_bt])
                E.op("gp", lambda g: g.dma_start(dbg["d_zpk"], spc[:]), deps=[t_spb])
                E.op("gp", lambda g: g.dma_start(dbg["d_Qch"], Qch[:]), deps=[tqb2])
                E.op("gp", lambda g: g.dma_start(dbg["d_Kch"], Kch[:]), deps=[tkn])
                E.op("gp", lambda g: g.dma_start(dbg["d_Ktok"], Ktok[:]), deps=[trK])
                E.op("gp", lambda g: g.dma_start(dbg["d_Vtok"], Vtok[:]), deps=[trV])
            E.flush()

            if PHASE == "p1":
                _dummy_out(nc, E, es, y_out)
                return nc

            NTs = sb1("NTs", [128, 256], BF16)
            Ms = sb1("Ms", [128, 256], BF16)
            Wsol = sb1("Wsol", [128, 256], BF16)
            EMT = sb1("EMT", [128, 256], BF16)
            expsb = sb1("expsb", [128, 512], BF16)
            gbsb = sb1("gbsb", [128, 256], BF16)
            argb = sb1("argb", [128, 256], F32)
            argb2 = sb1("argb2", [128, 256], F32)
            d2buf = sb1("d2buf", [128, 256], BF16)
            ptmp = sb1("ptmp", [128, 128], BF16)
            PTs = sb1("PTs", [128, 128], BF16)
            STf = sb1("STf", [128, 64], F32)
            STb = sb1("STb", [128, 64], BF16)
            Mch = sb1("Mch", [128, 64], F32)
            rbA0 = sb1("rbA0", [1, 256], F32)
            rbA1 = sb1("rbA1", [1, 256], F32)
            rbB0 = sb1("rbB0", [1, 256], F32)
            rbB1 = sb1("rbB1", [1, 256], F32)

            bkA, bkB, bkC, bkD, bkE = pP1, pP2, pRED, pBC1, pBC2
            bkFO = psf1("bkFO", [128, 512])
            bkF = bkFO
            pO = pFL
            bkT = pT1

            E.sep_fn = lambda t: t.ldweights(identb[:])
            t_s0 = E.gpc(lambda g: g.memset(STf[:], 0.0))
            t_s1 = E.gpc(lambda g: g.memset(STb[:], 0.0))
            tok_ST, tok_STb = t_s0, t_s1

            fA = fB = fC = fD = fE = fF = fO = fT = None
            fNT = fMs = fW = fEMT = fgb = farg = [None]
            farg_e = farg_n = fexp_e = fexp_n = None
            fd2 = fpt = fmc = None
            rb_free = [None, None]
            ow_tok = [None, None]
            t_oev = None

            for c in range(NCH_RUN):
                par = c % 2
                rbA = rbA0 if par == 0 else rbA1
                rbB = rbB0 if par == 0 else rbB1
                csl = slice(c * 128, (c + 1) * 128)
                if SCAN_PART != "g2":
                    t_rba = E.op("sp", lambda s, c=c, rbA=rbA: s.dma_start(rbA[0:1, :],
                        lcum_c[c:c + 1, 0:256]), deps=[t_lcc, rb_free[par]])
                    t_rbb = E.op("sp", lambda s, c=c, rbB=rbB: s.dma_start(rbB[0:1, :],
                        npos_c[c:c + 1, 0:256]), deps=[t_np2])
                else:
                    t_rba = t_rbb = None
                tG = None
                glist = [] if SCAN_PART == "g3" else [
                    (Kch, Kch, 0), (Kch, Kch, 128), (Kch, Qch, 256), (Kch, Qch, 384)]
                for i, (srcL, srcR, col) in enumerate(glist):
                    h = i % 2
                    hsl = slice(h * 64, (h + 1) * 64)
                    tG = E.op("pe", lambda t, srcL=srcL, srcR=srcR, col=col, hsl=hsl, csl=csl:
                        t.matmul(bkA[:, col:col + 128], srcL[hsl, csl], srcR[hsl, csl],
                                 start=True, stop=True),
                        deps=[fA, tkn, tqb2] if i == 0 else (), inc=(i == 3),
                        rows=(h * 64, 64))
                tBC = None
                for i in range(4 if SCAN_PART != "g2" else 0):
                    h = i % 2
                    rb = rbA if i < 2 else rbB
                    col = i * 128
                    tBC = E.op("pe", lambda t, h=h, rb=rb, col=col: t.matmul(
                        bkB[:, col:col + 128], ones1f[:], rb[0:1, h * 128:(h + 1) * 128],
                        start=True, stop=True),
                        deps=[t_rba, t_rbb, fB] if i == 0 else (), inc=(i == 3),
                        rows=(0, 1))
                if SCAN_PART in ("g", "g2", "g3"):
                    fO = tG if tG else tBC
                    ow_tok[par] = tBC if tBC else tG
                    fA = tG
                    fB = tBC
                    continue
                t_ae = E.op("dv", lambda v: v.tensor_sub(argb[:], inclm[:], bkB[:, 0:256]),
                            deps=[tBC, farg_e])
                t_an = E.op("dv", lambda v: v.tensor_sub(argb2[:], strictm[:], bkB[:, 256:512]),
                            deps=[tBC, farg_n])
                t_e = None
                for h in range(2):
                    t_e = E.op("ac", lambda a, h=h, c=c: a.activation(
                        expsb[:, h * 128:(h + 1) * 128], argb[:, h * 128:(h + 1) * 128],
                        AF.Exp, bias=lcum_t[:, h * 128 + c: h * 128 + c + 1]),
                        deps=[t_ae, fexp_e] if h == 0 else (), inc=(h == 1))
                farg_e = t_e
                t_en = None
                for h in range(2):
                    t_en = E.op("ac", lambda a, h=h, c=c: a.activation(
                        expsb[:, 256 + h * 128:256 + (h + 1) * 128],
                        argb2[:, h * 128:(h + 1) * 128], AF.Exp,
                        bias=lcum_t[:, h * 128 + c: h * 128 + c + 1]),
                        deps=[t_an, fexp_n] if h == 0 else (), inc=(h == 1))
                farg_n = t_en
                t_gb = E.op("ac", lambda a: a.activation(gbsb[:], bkB[:, 0:256], AF.Exp,
                    scale=-1.0), deps=[tBC, fgb])
                fB = [t_ae, t_an, t_gb]
                if SCAN_PART == "arg":
                    fO = t_gb
                    ow_tok[par] = t_gb
                    fA = tG
                    fexp_e = t_e
                    fexp_n = t_en
                    continue
                t_emt = E.op("dv", lambda v: v.tensor_mul(EMT[:], bkA[:, 256:512],
                    expsb[:, 0:256]), deps=[tG, t_e, fEMT])
                fexp_e = t_emt
                t_nt = E.op("dv", lambda v: v.scalar_tensor_tensor(NTs[:], bkA[:, 0:256],
                    -1.0, expsb[:, 256:512], op0=ALU.mult, op1=ALU.mult),
                    deps=[tG, t_en, fNT])
                fexp_n = t_nt
                fA = t_nt
                t_w = E.op("dv", lambda v, c=c, csl=csl: v.tensor_scalar(Wsol[:, 0:64],
                    Vtok[:, csl][:, 0:64], b_t[:, c:c + 1], None, op0=ALU.mult),
                    deps=[t_bt, fW, trV])
                t_w = E.op("dv", lambda v, c=c, csl=csl: v.tensor_scalar(Wsol[:, 128:192],
                    Vtok[:, csl][:, 64:128], b_t[:, 128 + c:129 + c], None, op0=ALU.mult),
                    deps=[t_w])
                t_w = E.op("dv", lambda v, c=c, csl=csl: v.tensor_scalar(Wsol[:, 64:128],
                    Ktok[:, csl][:, 0:64], c2_t[:, c:c + 1], None, op0=ALU.mult),
                    deps=[t_w, t_c2t, trK])
                t_w = E.op("dv", lambda v, c=c, csl=csl: v.tensor_scalar(Wsol[:, 192:256],
                    Ktok[:, csl][:, 64:128], c2_t[:, 128 + c:129 + c], None, op0=ALU.mult),
                    deps=[t_w])
                tok_W = t_w
                if SCAN_PART == "rhs":
                    fO = t_w
                    ow_tok[par] = t_w
                    fW = t_w
                    continue
                t_m0 = None
                for h in range(2):
                    t_m0 = E.op("pe", lambda t, h=h: t.transpose(
                        bkT[:, h * 128:(h + 1) * 128], NTs[:, h * 128:(h + 1) * 128], identb[:]),
                        deps=[t_nt, fT] if h == 0 else (), inc=(h == 1))
                tok_Ms = E.op("dv", lambda v: v.tensor_copy(Ms[:], bkT[:, 0:256]),
                              deps=[t_m0, fMs])
                fT = tok_Ms
                tok_NT = t_nt
                if SCAN_PART == "pro":
                    fO = tok_Ms
                    ow_tok[par] = tok_Ms
                    continue

                tdw = None
                for j in range(7):
                    for h in range(2):
                        tdw = E.op("pe", lambda t, h=h: t.matmul(
                            bkC[:, h * 128:(h + 1) * 128], NTs[:, h * 128:(h + 1) * 128],
                            Wsol[:, h * 128:(h + 1) * 128], start=True, stop=True),
                            deps=[tok_NT, tok_W, fC] if h == 0 else (), inc=(h == 1))
                    if j < 6:
                        tsq = None
                        for h in range(2):
                            tsq = E.op("pe", lambda t, h=h: t.matmul(
                                bkD[:, h * 128:(h + 1) * 128], NTs[:, h * 128:(h + 1) * 128],
                                Ms[:, h * 128:(h + 1) * 128], start=True, stop=True),
                                deps=[tok_Ms, fD] if h == 0 else (), inc=False)
                        for h in range(2):
                            tsq = E.op("pe", lambda t, h=h: t.matmul(
                                bkD[:, 256 + h * 128:256 + (h + 1) * 128],
                                Ms[:, h * 128:(h + 1) * 128], NTs[:, h * 128:(h + 1) * 128],
                                start=True, stop=True), deps=(), inc=(h == 1))
                        tok_Ms = E.op("ac", lambda a: a.activation(Ms[:], bkD[:, 0:256],
                            AF.Copy), deps=[tsq])
                        tok_NT = E.op("ac", lambda a: a.activation(NTs[:], bkD[:, 256:512],
                            AF.Copy), deps=[tsq])
                        fD = tok_NT
                    tok_W = E.op("dv", lambda v: v.tensor_add(Wsol[:], Wsol[:], bkC[:, 0:256]),
                                 deps=[tdw, tok_W])
                    fC = tok_W
                fNT = tdw
                fMs = tok_Ms
                if DEBUG and c == 0:
                    E.op("gp", lambda g: g.dma_start(dbg["d_W"], Wsol[:]), deps=[tok_W])
                    E.op("gp", lambda g: g.dma_start(dbg["d_EMT"], EMT[:]), deps=[t_emt])
                    E.op("gp", lambda g: g.dma_start(dbg["d_NT"], NTs[:]), deps=[t_nt])
                    E.op("gp", lambda g: g.dma_start(dbg["d_exp"], expsb[:]), deps=[t_en, t_e])
                if SCAN_PART == "solve":
                    fO = tok_W
                    ow_tok[par] = tok_W
                    continue

                t_d = E.op("dv", lambda v, c=c: v.tensor_scalar(d2buf[:, 0:64], Wsol[:, 0:64],
                    d2_t[:, c:c + 1], None, op0=ALU.mult), deps=[tok_W, t_d2t, fd2])
                t_d = E.op("dv", lambda v, c=c: v.tensor_scalar(d2buf[:, 64:128],
                    Wsol[:, 128:192], d2_t[:, 128 + c:129 + c], None, op0=ALU.mult),
                    deps=[t_d])
                t_d = E.op("dv", lambda v, c=c, csl=csl: v.tensor_scalar(d2buf[:, 128:192],
                    Ktok[:, csl][:, 0:64], d2_t[:, c:c + 1], None, op0=ALU.mult), deps=[t_d])
                t_d = E.op("dv", lambda v, c=c, csl=csl: v.tensor_scalar(d2buf[:, 192:256],
                    Ktok[:, csl][:, 64:128], d2_t[:, 128 + c:129 + c], None, op0=ALU.mult),
                    deps=[t_d])
                tG2 = None
                for h in range(2):
                    tG2 = E.op("pe", lambda t, h=h: t.matmul(
                        bkE[h * 64:(h + 1) * 64, 0:64], Wsol[:, h * 128 + 64:h * 128 + 128],
                        d2buf[:, 128 + h * 64:128 + (h + 1) * 64], start=True, stop=True),
                        deps=[t_d, fE] if h == 0 else (), inc=(h == 1))
                tZq = None
                for h in range(2):
                    tZq = E.op("pe", lambda t, h=h: t.matmul(
                        bkE[h * 64:(h + 1) * 64, 64:192], Wsol[:, h * 128 + 64:h * 128 + 128],
                        EMT[:, h * 128:(h + 1) * 128], start=True, stop=True),
                        deps=[t_emt] if h == 0 else (), inc=(h == 1))

                t_mc = E.op("dv", lambda v, c=c: v.scalar_tensor_tensor(Mch[:], i64two[:],
                    gC2[:, c:c + 1], bkE[:, 0:64], op0=ALU.mult, op1=ALU.subtract),
                    deps=[tG2, tZq, t_gc1, t_gc2, t_i64b, fmc])
                t_p = E.op("dv", lambda v, csl=csl: v.tensor_mul(ptmp[0:64, :],
                    Qch[0:64, csl], gbsb[0:64, 0:128]), deps=[t_gb, fpt])
                t_p = E.op("dv", lambda v, csl=csl: v.tensor_mul(ptmp[64:128, :],
                    Qch[64:128, csl], gbsb[64:128, 128:256]), deps=[t_p])
                t_pt = E.op("dv", lambda v: v.tensor_sub(PTs[:], ptmp[:], bkE[:, 64:192]),
                            deps=[t_p, tZq])
                fgb = t_pt
                fE = t_pt
                if SCAN_PART == "epi1":
                    fO = t_pt
                    ow_tok[par] = t_pt
                    fF = [t_mc]
                    fd2 = tG1
                    fmc = t_mc
                    continue
                tch = None
                tG1 = None
                for h in range(2):
                    hsl = slice(h * 64, (h + 1) * 64)
                    tG1 = E.op("pe", lambda t, h=h, hsl=hsl, csl=csl: t.matmul(
                        bkF[hsl, 0:64], Ktok[:, csl][:, h * 64:(h + 1) * 64],
                        d2buf[:, h * 64:(h + 1) * 64], start=True, stop=False),
                        deps=[fF, t_mc, tok_ST] if h == 0 else (), inc=False)
                    tch = E.op("pe", lambda t, hsl=hsl: t.matmul(
                        bkF[hsl, 0:64], Mch[hsl, :], STf[hsl, :], start=False, stop=True),
                        deps=(), inc=(h == 1), rows=(h * 64, 64))
                fmc = tch
                fd2 = tch
                if DEBUG and c == 0:
                    E.op("gp", lambda g: g.dma_start(dbg["d_STb"], STb[:]), deps=[tok_STb, t_pt])
                    E.op("gp", lambda g: g.dma_start(dbg["d_PT"], PTs[:]), deps=[t_pt])
                tout = None
                for h in range(2):
                    hsl = slice(h * 64, (h + 1) * 64)
                    tout = E.op("pe", lambda t, h=h, hsl=hsl: t.matmul(
                        pO[hsl, 0:128], STb[hsl, :], PTs[hsl, :],
                        start=True, stop=False),
                        deps=[t_pt, tok_STb, fO] if h == 0 else (), inc=False,
                        rows=(h * 64, 64))
                    tout = E.op("pe", lambda t, h=h, hsl=hsl: t.matmul(
                        pO[hsl, 0:128], Wsol[:, h * 128:h * 128 + 64],
                        EMT[:, h * 128:(h + 1) * 128], start=False, stop=True),
                        deps=(), inc=(h == 1))
                fEMT = tout
                fW = tout
                fpt = tout
                obuf = obuf0 if par == 0 else obuf1
                t_oev = E.op("dv", lambda v, obuf=obuf: v.tensor_copy(obuf[:],
                    pO[:, 0:128]), deps=[tout, ow_tok[par]])
                rdest = c // 16
                ow_tok[par] = E.op("gp", lambda g, c=c, rdest=rdest, obuf=obuf: g.dma_start(
                    bass.AP(agO_in, rdest * 128 * TPC + (c % 16) * 128,
                            [[TPC, 128], [1, 128]]), obuf[:]), deps=[t_oev], sem="owq")
                if DEBUG:
                    ow_tok[par] = E.op("gp", lambda g, csl=csl, obuf=obuf: g.dma_start(
                        dbg["d_out"][:, csl], obuf[:]), deps=[t_oev], sem="owq")
                tok_ST = E.op("dv", lambda v: v.tensor_copy(STf[:], bkF[:, 0:64]),
                              deps=[tch, tout])
                tok_STb = E.op("ac", lambda a: a.activation(STb[:], bkF[:, 0:64], AF.Copy),
                               deps=[tch, tout])
                fF = [tok_ST, tok_STb]
                fO = t_oev
                rb_free[par] = tBC

            def a2a(g):
                with allow_ici_all_to_all():
                    return g.collective_compute(
                        "AllToAll", ALU.bypass, replica_groups=[list(range(NCORES))],
                        ins=[agO_in.ap().opt()], outs=[agO_out.ap().opt()])
            if PHASE == "p2x":
                ag3 = None
            else:
                ag3 = E.cc(a2a, deps=[ow_tok[0], ow_tok[1]])
            E.flush()

        E.sep_fn = None
        if PHASE in ("p2", "p2x"):
            _dummy_out(nc, E, es, y_out)
            return nc

        with ExitStack() as es4:
            sb4 = lambda n, shape, dt=F32: es4.enter_context(nc.sbuf_tensor(n, shape, dt))

            x2Thi = sb4("x2Thi", [128, 8 * TPC], BF16)
            comb = sb4("comb", [128, 128], F32)
            tg_last = None

            with ExitStack() as es4b:
                sb4b = lambda n, shape, dt=F32: es4b.enter_context(nc.sbuf_tensor(n, shape, dt))
                psf4 = lambda n, shape: es4b.enter_context(nc.psum_tensor(n, shape, F32))
                psb4 = lambda n, shape: es4b.enter_context(nc.psum_tensor(n, shape, BF16))
                OUTSl = sb4b("OUTSl", [128, 8 * TPC], BF16)
                x2Tlo = sb4b("x2Tlo", [128, 8 * TPC], BF16)
                wob = sb4b("wob", [128, 8 * 1024], BF16)
                normed = sb4b("normed", [128, 4096], BF16)
                sqb = sb4b("sqb", [128, 512], BF16)
                rstd4 = sb4b("rstd4", [1, 512], F32)
                xrow = sb4b("xrow", [128, 1024], F32)
                x2t4 = sb4b("x2t4", [128, 1024], F32)
                x2bf = sb4b("x2bf", [128, 1024], BF16)
                x2lo = sb4b("x2lo", [128, 1024], BF16)
                lnc1 = sb4b("lnc1", [128, 1], F32)
                lnc2 = sb4b("lnc2", [128, 1], F32)
                lnsq = sb4b("lnsq", [128, 1024], F32)
                gwf = sb4b("gwf", [128, 64], F32)
                gwhi = sb4b("gwhi", [128, 64], BF16)
                gwlo = sb4b("gwlo", [128, 64], BF16)
                gbb = sb4b("gbb", [128, 8], F32)
                glog = sb4b("glog", [128, 8], F32)
                gsc = sb4b("gsc", [128, 8], F32)
                gtmp = sb4b("gtmp", [128, 8], F32)
                gm = sb4b("gm", [128, 1], F32)
                gs2 = sb4b("gs2", [128, 1], F32)
                gwsb = sb4b("gwsb", [8, 1024], F32)

                pM1 = psf4("pM1", [128, 1024])
                pR4 = psf4("pR4", [128, 512])
                pB4 = psf4("pB4", [128, 512])
                pG4 = psf4("pG4", [128, 512])
                pT4 = psb4("pT4", [128, 1024])
                pT5 = psb4("pT5", [128, 1024])

                t_og = []
                for r in range(NCORES):
                    t_og.append(E.op("sp", lambda s, r=r: s.dma_start(
                        OUTSl[:, r * TPC:(r + 1) * TPC],
                        bass.AP(agO_out, r * 128 * TPC, [[TPC, 128], [1, TPC]])),
                        deps=[ag3], sem="oq"))
                t_wo = E.op("gp", lambda g: g.dma_start(wob[:],
                    wo.rearrange("(b p) c -> p b c", p=128)), sem="wobq")
                t_gw = E.op("gp", lambda g: g.dma_start(gwsb[:], gate_w), sem="g4q")
                t_gt = None
                for mb in range(8):
                    t_gt = E.op("pe", lambda t, mb=mb: t.transpose(
                        pG4[:, mb * 8:(mb + 1) * 8], gwsb[0:8, mb * 128:(mb + 1) * 128],
                        identf[0:8, 0:8]), deps=[t_gw] if mb == 0 else (), inc=(mb == 7))
                t_gwf = E.op("dv", lambda v: v.tensor_copy(gwf[:], pG4[:, 0:64]), deps=[t_gt])
                t_ghi = E.op("dv", lambda v: v.tensor_copy(gwhi[:], gwf[:]), deps=[t_gwf])
                t_glo = E.op("dv", lambda v: v.scalar_tensor_tensor(gwlo[:], gwhi[:], -1.0,
                    gwf[:], op0=ALU.mult, op1=ALU.add), deps=[t_ghi])
                t_gbb0 = E.op("pe", lambda t: t.matmul(pG4[:, 64:72], ones1f[:], gbrow[0:1, :],
                    start=True, stop=True), deps=[t_gt, sm_last], rows=(0, 1))
                t_gbbc = E.op("dv", lambda v: v.tensor_copy(gbb[:], pG4[:, 64:72]),
                              deps=[t_gbb0])

                fR4 = fB4 = None
                fM = [None, None]
                fth1 = fxbf = fxlo = fg3 = None
                fT4 = fT5 = None
                fnorm = None
                fx2 = None
                fglog = None
                frstd = None
                fsqb = None
                x2w_toks = []

                for tt in range(4):
                    tred = None
                    for mb in range(8):
                        tqq = E.op("ac", lambda a, mb=mb, tt=tt: a.activation(sqb[:],
                            OUTSl[:, mb * TPC + tt * 512: mb * TPC + (tt + 1) * 512],
                            AF.Square), deps=[t_og[-1], fsqb])
                        tred = E.op("pe", lambda t, mb=mb: t.matmul(pR4[0:1, 0:512],
                            onesvb[:], sqb[:], start=(mb == 0), stop=(mb == 7)),
                            deps=[tqq, fR4] if mb == 0 else [tqq], inc=True)
                        fsqb = tred
                    if DEBUG and tt == 0:
                        dsq = sb4b("dsq", [1, 512], F32)
                        tdsq = E.op("dv", lambda v: v.tensor_copy(dsq[0:1, :], pR4[0:1, 0:512]),
                                    deps=[tred])
                        E.op("gp", lambda g: g.dma_start(dbg["d_sqs"], dsq[0:1, :]), deps=[tdsq])
                    tln = E.op("ac", lambda a: a.activation(rstd4[0:1, :], pR4[0:1, 0:512],
                        AF.Ln, scale=1.0 / D, bias=ceps6[0:1, 0:1]), deps=[tred, frstd, (tdsq if (DEBUG and tt == 0) else None)])
                    tex = E.op("ac", lambda a: a.activation(rstd4[0:1, :], rstd4[0:1, :],
                        AF.Exp, scale=-0.5), deps=[tln])
                    if DEBUG and tt == 0:
                        E.op("gp", lambda g: g.dma_start(dbg["d_rstd"], rstd4[0:1, :]), deps=[tex])
                    fR4 = [tex, (tdsq if (DEBUG and tt == 0) else None)]
                    tbc = E.op("pe", lambda t: t.matmul(pB4[:, 0:512], ones1f[:],
                        rstd4[0:1, :], start=True, stop=True), deps=[tex, fB4])
                    frstd = tbc
                    tnm = None
                    for mb in range(8):
                        tnm = E.op("dv", lambda v, mb=mb, tt=tt: v.scalar_tensor_tensor(
                            normed[:, mb * 512:(mb + 1) * 512],
                            OUTSl[:, mb * TPC + tt * 512: mb * TPC + (tt + 1) * 512],
                            onw_c[:, mb:mb + 1], pB4[:, 0:512], op0=ALU.mult, op1=ALU.mult),
                            deps=[tbc, fnorm] if mb == 0 else (), inc=(mb == 7))
                    fB4 = tnm

                    for ts_i in range(4):
                        gt = tt * 4 + ts_i
                        tmix = None
                        for ch in range(2):
                            for mb in range(8):
                                tmix = E.op("pe", lambda t, ch=ch, mb=mb, ts_i=ts_i:
                                    t.matmul(pM1[:, ch * 512:(ch + 1) * 512],
                                             normed[:, mb * 512 + ts_i * 128: mb * 512 + (ts_i + 1) * 128],
                                             wob[:, mb * 1024 + ch * 512: mb * 1024 + (ch + 1) * 512],
                                             start=(mb == 0), stop=(mb == 7)),
                                    deps=[tnm, t_wo, fM[ch]] if mb == 0 else (),
                                    inc=(mb == 7))
                                fmix = tmix
                        fnorm = tmix
                        txl = E.op("sp", lambda s, gt=gt: s.dma_start(xrow[:],
                            x_rows[gt * 128:(gt + 1) * 128, :]), deps=[fx2], sem="xrq")
                        ta = E.op("dv", lambda v: v.tensor_add(x2t4[:, 0:512], xrow[:, 0:512],
                            pM1[:, 0:512]), deps=[tmix, txl, fth1])
                        fM[0] = ta
                        ta = E.op("dv", lambda v: v.tensor_add(x2t4[:, 512:1024],
                            xrow[:, 512:1024], pM1[:, 512:1024]), deps=[ta])
                        fM[1] = ta
                        u1 = E.op("dv", lambda v: v.tensor_reduce(lnc1[:, 0:1], x2t4[:],
                            axis=mybir.AxisListType.X, op=ALU.add, negate=True), deps=[ta])
                        u2 = E.op("dv", lambda v: v.tensor_scalar_mul(lnc1[:, 0:1],
                            lnc1[:, 0:1], 1.0 / D), deps=[u1])
                        u3 = E.op("dv", lambda v: v.tensor_scalar(x2t4[:], x2t4[:],
                            lnc1[:, 0:1], None, op0=ALU.add), deps=[u2])
                        u4 = E.op("ac", lambda a: a.activation(lnsq[:], x2t4[:], AF.Square),
                                  deps=[u3])
                        u5 = E.op("dv", lambda v: v.tensor_reduce(lnc2[:, 0:1], lnsq[:],
                            axis=mybir.AxisListType.X, op=ALU.add), deps=[u4])
                        u6 = E.op("ac", lambda a: a.activation(lnc2[:, 0:1], lnc2[:, 0:1],
                            AF.Ln, scale=1.0 / D, bias=ceps5[:, 0:1]), deps=[u5])
                        u7 = E.op("ac", lambda a: a.activation(lnc2[:, 0:1], lnc2[:, 0:1],
                            AF.Exp, scale=-0.5), deps=[u6])
                        u8 = E.op("dv", lambda v: v.tensor_scalar(x2t4[:], x2t4[:],
                            lnc2[:, 0:1], None, op0=ALU.mult), deps=[u7])
                        fx2 = E.op("sp", lambda s, gt=gt: s.dma_start(
                            x2_scr.ap()[gt * 128:(gt + 1) * 128, :], x2t4[:]), deps=[u8],
                            sem="x2wq")
                        x2w_toks.append(fx2)
                        th1 = E.op("ac", lambda a: a.activation(x2bf[:], x2t4[:], AF.Copy),
                                   deps=[u8, fxbf])
                        th2 = E.op("dv", lambda v: v.scalar_tensor_tensor(x2lo[:], x2bf[:],
                            -1.0, x2t4[:], op0=ALU.mult, op1=ALU.add), deps=[th1, fxlo])
                        tq1 = None
                        for mb in range(8):
                            tq1 = E.op("pe", lambda t, mb=mb: t.transpose(
                                pT4[:, mb * 128:(mb + 1) * 128],
                                x2bf[:, mb * 128:(mb + 1) * 128], identb[:]),
                                deps=[th1, fT4] if mb == 0 else (), inc=(mb == 7))
                        fT4 = E.op("dv", lambda v, gt=gt: v.tensor_copy(
                            bass.AP(x2Thi, gt * 128, [[8 * TPC, 128], [TPC, 8], [1, 128]]),
                            pT4[:, 0:1024]), deps=[tq1])
                        hi_ev = fT4
                        fth1 = th1
                        fxbf = tq1
                        tq2 = None
                        for mb in range(8):
                            tq2 = E.op("pe", lambda t, mb=mb: t.transpose(
                                pT5[:, mb * 128:(mb + 1) * 128],
                                x2lo[:, mb * 128:(mb + 1) * 128], identb[:]),
                                deps=[th2, fT5] if mb == 0 else (), inc=(mb == 7))
                        fT5 = E.op("dv", lambda v, gt=gt: v.tensor_copy(
                            bass.AP(x2Tlo, gt * 128, [[8 * TPC, 128], [TPC, 8], [1, 128]]),
                            pT5[:, 0:1024]), deps=[tq2])
                        lo_ev = fT5
                        fxlo = tq2
                        tgl = None
                        for mb in range(8):
                            tgl = E.op("pe", lambda t, mb=mb, gt=gt: t.matmul(
                                pG4[:, 128:136],
                                bass.AP(x2Thi, mb * TPC + gt * 128, [[8 * TPC, 128], [1, 128]]),
                                gwhi[:, mb * 8:(mb + 1) * 8], start=(mb == 0), stop=False),
                                deps=[hi_ev, t_ghi, t_gbbc, fglog] if mb == 0 else (), inc=False)
                        for mb in range(8):
                            tgl = E.op("pe", lambda t, mb=mb, gt=gt: t.matmul(
                                pG4[:, 128:136],
                                bass.AP(x2Thi, mb * TPC + gt * 128, [[8 * TPC, 128], [1, 128]]),
                                gwlo[:, mb * 8:(mb + 1) * 8], start=False, stop=False),
                                deps=[t_glo] if mb == 0 else (), inc=False)
                        for mb in range(8):
                            tgl = E.op("pe", lambda t, mb=mb, gt=gt: t.matmul(
                                pG4[:, 128:136],
                                bass.AP(x2Tlo, mb * TPC + gt * 128, [[8 * TPC, 128], [1, 128]]),
                                gwhi[:, mb * 8:(mb + 1) * 8], start=False, stop=(mb == 7)),
                                deps=[lo_ev] if mb == 0 else (), inc=(mb == 7))
                        tg1 = E.op("dv", lambda v: v.tensor_copy(glog[:], pG4[:, 128:136]),
                                   deps=[tgl, fg3])
                        fglog = tg1
                        tg2 = E.op("dv", lambda v: v.tensor_reduce(gm[:, 0:1], glog[:],
                            axis=mybir.AxisListType.X, op=ALU.max, negate=True), deps=[tg1])
                        tg3 = E.op("ac", lambda a: a.activation(gsc[:], glog[:], AF.Exp,
                            bias=gm[:, 0:1]), deps=[tg2])
                        fg3 = tg3
                        tg4 = E.op("dv", lambda v: v.tensor_reduce(gm[:, 0:1], gsc[:],
                            axis=mybir.AxisListType.X, op=ALU.add), deps=[tg3])
                        tg5 = E.op("dv", lambda v: v.reciprocal(gs2[:, 0:1], gm[:, 0:1]),
                                   deps=[tg4])
                        tg6 = E.op("dv", lambda v: v.tensor_scalar(gsc[:], gsc[:],
                            gs2[:, 0:1], None, op0=ALU.mult), deps=[tg5])
                        tg7 = E.op("dv", lambda v: v.tensor_add(gtmp[:], gsc[:], gbb[:]),
                                   deps=[tg6])
                        tg8 = E.op("dv", lambda v: v.tensor_reduce(gm[:, 0:1], gtmp[:],
                            axis=mybir.AxisListType.X, op=ALU.max), deps=[tg7])
                        tg9 = E.op("dv", lambda v: v.tensor_scalar(glog[:], gtmp[:],
                            gm[:, 0:1], None, op0=ALU.is_ge), deps=[tg8])
                        tg10 = E.op("dv", lambda v: v.scalar_tensor_tensor(glog[:], glog[:],
                            NEG, gtmp[:], op0=ALU.mult, op1=ALU.add), deps=[tg9])
                        tg11 = E.op("dv", lambda v: v.tensor_reduce(gm[:, 0:1], glog[:],
                            axis=mybir.AxisListType.X, op=ALU.max), deps=[tg10])
                        tg12 = E.op("dv", lambda v: v.tensor_scalar(gtmp[:], gtmp[:],
                            gm[:, 0:1], None, op0=ALU.is_ge), deps=[tg11])
                        tg13 = E.op("dv", lambda v, gt=gt: v.tensor_mul(
                            comb[:, gt * 8:(gt + 1) * 8], gtmp[:], gsc[:]), deps=[tg12])
                        tg_last = tg13
                if DEBUG:
                    E.op("gp", lambda g: g.dma_start(dbg["d_x2"], x2_scr.ap()),
                         deps=[x2w_toks[-1]])
                    E.op("gp", lambda g: g.dma_start(dbg["d_comb"], comb[:]), deps=[tg_last])
                E.flush()

            with ExitStack() as es5:
                sb5 = lambda n, shape, dt=F32: es5.enter_context(nc.sbuf_tensor(n, shape, dt))
                psf5 = lambda n, shape: es5.enter_context(nc.psum_tensor(n, shape, F32))
                yacc = sb5("yacc", [128, 16 * D], F32)
                es5b = es5.enter_context(ExitStack())
                sb5b = lambda n, shape, dt=F32: es5b.enter_context(nc.sbuf_tensor(n, shape, dt))
                h_e = sb5b("h_e", [128, 16 * TPC], BF16)
                w1b0 = sb5b("w1b0", [128, 1024], BF16)
                w1b1 = sb5b("w1b1", [128, 1024], BF16)
                w3b0 = sb5b("w3b0", [128, 1024], BF16)
                w3b1 = sb5b("w3b1", [128, 1024], BF16)
                w2full = sb5b("w2full", [128, 16 * 1024], BF16)
                silu_t = sb5b("silu_t", [128, 512], BF16)
                ytmp = sb5b("ytmp", [128, 512], F32)
                pH1a = psf5("pH1a", [128, 512])
                pH1b = psf5("pH1b", [128, 512])
                pH3a = psf5("pH3a", [128, 512])
                pH3b = psf5("pH3b", [128, 512])
                pY1 = psf5("pY1", [128, 512])
                pY2 = psf5("pY2", [128, 512])

                def w1_off(e, which):
                    return (e * 3 * PIECE if e < E_EXP else base_sh) + which * PIECE

                def w2_off(e):
                    return (e * 3 * PIECE if e < E_EXP else base_sh) + 2 * PIECE

                experts = [E_EXP] + list(range(E_EXP))
                w1_rd = {}
                w3_rd = {}
                w1_ld = {}
                w3_ld = {}
                prev_y_last = None
                y_free = [None, None]
                fsilu = None
                fytmp = None
                yt = {}
                ph_free = {"1a": None, "1b": None, "3a": None, "3b": None}

                for ei, e in enumerate(experts):
                    w1b = w1b0
                    w3b = w3b0
                    w2tk = []
                    for ib in range(16):
                        rk = ib // 2
                        w2tk.append(E.op("sp", lambda s, e=e, ib=ib, rk=rk: s.dma_start(
                            w2full[:, ib * 1024:(ib + 1) * 1024],
                            bass.AP(agW_out, rk * WCHUNK + w2_off(e) + (ib % 2) * 128 * 1024,
                                    [[1024, 128], [1, 1024]])),
                            deps=[ag2, prev_y_last] if ib == 0 else (), sem="w2q"))
                    hA_last = None
                    for ib in range(16):
                        prev1 = w1_rd.get((ei, ib - 1), w1_rd.get((ei - 1, 15)))
                        prev3 = w3_rd.get((ei, ib - 1), w3_rd.get((ei - 1, 15)))
                        tl1 = E.op("sp", lambda s, e=e, ib=ib, w1b=w1b: s.dma_start(
                            bass.AP(w1b, 0, [[1024, 128], [128, 8], [1, 128]]),
                            bass.AP(agW_out, w1_off(e, 0) + ib * 128,
                                    [[2048, 128], [WCHUNK, 8], [1, 128]])),
                            deps=[ag2, prev1], sem="w1q")
                        w1_ld[(ei, ib)] = tl1
                        tl3 = E.op("sp", lambda s, e=e, ib=ib, w3b=w3b: s.dma_start(
                            bass.AP(w3b, 0, [[1024, 128], [128, 8], [1, 128]]),
                            bass.AP(agW_out, w1_off(e, 1) + ib * 128,
                                    [[2048, 128], [WCHUNK, 8], [1, 128]])),
                            deps=[prev3], sem="w3q")
                        w3_ld[(ei, ib)] = tl3
                        for tg in range(4):
                            pH1 = pH1a if tg % 2 == 0 else pH1b
                            pH3 = pH3a if tg % 2 == 0 else pH3b
                            k1 = "1a" if tg % 2 == 0 else "1b"
                            k3 = "3a" if tg % 2 == 0 else "3b"
                            tm1 = None
                            for mb in range(8):
                                tm1 = E.op("pe", lambda t, mb=mb, tg=tg, pH1=pH1, w1b=w1b:
                                    t.matmul(pH1[:, 0:512], w1b[:, mb * 128:(mb + 1) * 128],
                                             x2Thi[:, mb * TPC + tg * 512: mb * TPC + (tg + 1) * 512],
                                             start=(mb == 0), stop=(mb == 7)),
                                    deps=[tl1, ph_free[k1], prev_y_last, hi_ev] if mb == 0 else (),
                                    inc=(mb == 7))
                            w1_rd[(ei, ib)] = tm1
                            tm3 = None
                            for mb in range(8):
                                tm3 = E.op("pe", lambda t, mb=mb, tg=tg, pH3=pH3, w3b=w3b:
                                    t.matmul(pH3[:, 0:512], w3b[:, mb * 128:(mb + 1) * 128],
                                             x2Thi[:, mb * TPC + tg * 512: mb * TPC + (tg + 1) * 512],
                                             start=(mb == 0), stop=(mb == 7)),
                                    deps=[tl3, ph_free[k3]] if mb == 0 else (), inc=(mb == 7))
                            w3_rd[(ei, ib)] = tm3
                            tsl = E.op("ac", lambda a, pH1=pH1: a.activation(silu_t[:],
                                pH1[:, 0:512], AF.Silu), deps=[tm1, fsilu])
                            ph_free[k1] = tsl
                            tmu = E.op("dv", lambda v, ib=ib, tg=tg, pH3=pH3: v.tensor_mul(
                                h_e[:, ib * TPC + tg * 512: ib * TPC + (tg + 1) * 512],
                                silu_t[:], pH3[:, 0:512]), deps=[tsl, tm3])
                            fsilu = tmu
                            ph_free[k3] = tmu
                            yt[(ib, tg)] = tmu
                            hA_last = tmu
                    for ts_i in range(16):
                        for ch in range(2):
                            par = (ts_i * 2 + ch) % 2
                            pY = pY1 if par == 0 else pY2
                            ty = None
                            for ib in range(16):
                                ty = E.op("pe", lambda t, ib=ib, ts_i=ts_i, ch=ch, pY=pY:
                                    t.matmul(pY[:, 0:512],
                                             h_e[:, ib * TPC + ts_i * 128:ib * TPC + (ts_i + 1) * 128],
                                             w2full[:, ib * 1024 + ch * 512: ib * 1024 + (ch + 1) * 512],
                                             start=(ib == 0), stop=(ib == 15)),
                                    deps=[yt[(ib, ts_i // 4)], w2tk[-1], y_free[par]]
                                         if ib == 0 else [yt[(ib, ts_i // 4)]],
                                    inc=(ib == 15))
                            if e == E_EXP:
                                tya = E.op("ac", lambda a, ts_i=ts_i, ch=ch, pY=pY:
                                    a.activation(
                                        yacc[:, ts_i * D + ch * 512: ts_i * D + (ch + 1) * 512],
                                        pY[:, 0:512], AF.Copy), deps=[ty])
                                y_free[par] = tya
                                yt[("acc", ts_i, ch)] = tya
                            else:
                                tya = E.op("ac", lambda a, ts_i=ts_i, ch=ch, pY=pY, e=e:
                                    a.activation(ytmp[:], pY[:, 0:512], AF.Copy,
                                                 scale=comb[:, ts_i * 8 + e: ts_i * 8 + e + 1]),
                                    deps=[ty, tg_last, fytmp])
                                y_free[par] = tya
                                tyb = E.op("dv", lambda v, ts_i=ts_i, ch=ch: v.tensor_add(
                                    yacc[:, ts_i * D + ch * 512: ts_i * D + (ch + 1) * 512],
                                    yacc[:, ts_i * D + ch * 512: ts_i * D + (ch + 1) * 512],
                                    ytmp[:]), deps=[tya, yt[("acc", ts_i, ch)]])
                                fytmp = tyb
                                yt[("acc", ts_i, ch)] = tyb
                            prev_y_last = ty
                E.flush()
                es5b.close()

                x2r = sb5("x2r", [128, 1024], F32)
                o6 = sb5("o6", [128, 1024], F32)
                l6a = sb5("l6a", [128, 1], F32)
                l6b = sb5("l6b", [128, 1], F32)
                sq6 = sb5("sq6", [128, 1024], F32)
                fx2r = None
                out_toks = []
                for ts_i in range(16):
                    tx = E.op("sp", lambda s, ts_i=ts_i: s.dma_start(x2r[:],
                        x2_scr.ap()[ts_i * 128:(ts_i + 1) * 128, :]),
                        deps=[x2w_toks[-1], fx2r], sem="x2q")
                    tadd = E.op("dv", lambda v, ts_i=ts_i: v.tensor_add(o6[:], x2r[:],
                        yacc[:, ts_i * D:(ts_i + 1) * D]),
                        deps=[tx, yt[("acc", ts_i, 0)], yt[("acc", ts_i, 1)]])
                    u1 = E.op("dv", lambda v: v.tensor_reduce(l6a[:, 0:1], o6[:],
                        axis=mybir.AxisListType.X, op=ALU.add, negate=True), deps=[tadd])
                    u2 = E.op("dv", lambda v: v.tensor_scalar_mul(l6a[:, 0:1], l6a[:, 0:1],
                        1.0 / D), deps=[u1])
                    u3 = E.op("dv", lambda v: v.tensor_scalar(o6[:], o6[:], l6a[:, 0:1],
                        None, op0=ALU.add), deps=[u2])
                    u4 = E.op("ac", lambda a: a.activation(sq6[:], o6[:], AF.Square), deps=[u3])
                    u5 = E.op("dv", lambda v: v.tensor_reduce(l6b[:, 0:1], sq6[:],
                        axis=mybir.AxisListType.X, op=ALU.add), deps=[u4])
                    u6 = E.op("ac", lambda a: a.activation(l6b[:, 0:1], l6b[:, 0:1], AF.Ln,
                        scale=1.0 / D, bias=ceps5[:, 0:1]), deps=[u5])
                    u7 = E.op("ac", lambda a: a.activation(l6b[:, 0:1], l6b[:, 0:1], AF.Exp,
                        scale=-0.5), deps=[u6])
                    u8 = E.op("dv", lambda v: v.tensor_scalar(o6[:], o6[:], l6b[:, 0:1],
                        None, op0=ALU.mult), deps=[u7])
                    tout6 = E.op("sp", lambda s, ts_i=ts_i: s.dma_start(
                        y_out[ts_i * 128:(ts_i + 1) * 128, :], o6[:]), deps=[u8], sem="yq")
                    out_toks.append(tout6)
                    fx2r = tout6

                fin = out_toks[-1]
                E.raw("gp", lambda g, fin=fin: g.wait_ge(E.sem["yq"], fin[1]))
                E.raw("sp", lambda s, fin=fin: s.wait_ge(E.sem["yq"], fin[1]))
                E.flush()

    return nc


def kernel(**inputs):
    x = np.asarray(inputs["L0_attn_x"])[0]
    get = lambda n: np.asarray(inputs[n])
    in_maps = []
    for c in range(NCORES):
        hs = slice(2 * c, 2 * c + 2)
        m = {
            "x_rows": np.ascontiguousarray(x[c * TPC:(c + 1) * TPC]),
            "wq_s": np.ascontiguousarray(get("L0_attn_Wq")[:, c * 128:(c + 1) * 128]),
            "wk_s": np.ascontiguousarray(get("L0_attn_Wk")[:, c * 128:(c + 1) * 128]),
            "wv_s": np.ascontiguousarray(get("L0_attn_Wv")[:, c * 128:(c + 1) * 128]),
            "wrq_s": np.ascontiguousarray(get("L0_attn_Wrq")[:, c * 96:(c + 1) * 96]),
            "wrk_s": np.ascontiguousarray(get("L0_attn_Wrk")[:, c * 96:(c + 1) * 96]),
            "convq_s": np.ascontiguousarray(get("L0_attn_conv_q")[c * 128:(c + 1) * 128]),
            "convk_s": np.ascontiguousarray(get("L0_attn_conv_k")[c * 128:(c + 1) * 128]),
            "convv_s": np.ascontiguousarray(get("L0_attn_conv_v")[c * 128:(c + 1) * 128]),
            "wf_s": np.ascontiguousarray(get("L0_attn_Wf")[:, hs]),
            "wl_s": np.ascontiguousarray(get("L0_attn_Wl")[:, hs]),
            "bfbl": np.ascontiguousarray(np.concatenate(
                [get("L0_attn_bf")[hs], get("L0_attn_bl")[hs]])),
            "gs_s": np.ascontiguousarray(get("L0_attn_graph_scale")[hs]),
            "qn_w": np.ascontiguousarray(get("L0_attn_qn_w")),
            "kn_w": np.ascontiguousarray(get("L0_attn_kn_w")),
            "on_w": np.ascontiguousarray(get("L0_attn_on_w")),
            "wo": np.ascontiguousarray(get("L0_attn_Wo")),
            "gate_w": np.ascontiguousarray(get("L0_ffn_gate_w")),
            "gate_b": np.ascontiguousarray(get("L0_ffn_gate_bias")),
            "w1p": np.ascontiguousarray(get("L0_ffn_w1")[:, :, c * 128:(c + 1) * 128]),
            "w3p": np.ascontiguousarray(get("L0_ffn_w3")[:, :, c * 128:(c + 1) * 128]),
            "w2p": np.ascontiguousarray(get("L0_ffn_w2")[:, :, c * 256:(c + 1) * 256]),
            "sw1p": np.ascontiguousarray(get("L0_ffn_sw1")[:, c * 128:(c + 1) * 128]),
            "sw3p": np.ascontiguousarray(get("L0_ffn_sw3")[:, c * 128:(c + 1) * 128]),
            "sw2p": np.ascontiguousarray(get("L0_ffn_sw2")[:, c * 256:(c + 1) * 256]),
        }
        in_maps.append(m)
    nc = build_program()
    res = run_bass_kernel_spmd(nc, in_maps, core_ids=list(range(NCORES)))
    out = np.concatenate([np.asarray(res.results[c]["y"]) for c in range(NCORES)], axis=0)
    return out.reshape(1, S, D).astype(np.float32)




# version 07
# speedup vs baseline: 1.0001x; 1.0001x over previous
import numpy as np
import ml_dtypes
import concourse.bass as bass
import concourse.mybir as mybir
from concourse.bass_utils import run_bass_kernel_spmd
from concourse.aot_env import allow_ici_all_to_all
from contextlib import ExitStack

F32 = mybir.dt.float32
BF16 = mybir.dt.bfloat16
BF = ml_dtypes.bfloat16
AF = mybir.ActivationFunctionType
ALU = mybir.AluOpType

NCORES = 8
S, D, H, dH = 16384, 1024, 16, 64
C = 128
NCH = S // C
TPC = S // NCORES
INTER, E_EXP = 2048, 8
NEG = -1.0e30
PIECE = 128 * 2048
WCHUNK = 27 * PIECE

DEBUG = False
PHASE = "all"
RACE_DETECT = False
NCH_RUN = NCH
SCAN_PART = "all"


def _dummy_out(nc, E, es, y_out):
    with nc.sbuf_tensor("dummy0", [128, 1024], F32) as dz:
        tz = E.gpc(lambda g: g.memset(dz[:], 0.0))
        tl = None
        for i in range(16):
            tl = E.op("gp", lambda g, i=i: g.dma_start(
                y_out[i * 128:(i + 1) * 128, :], dz[:]), deps=[tz, tl])
        E.raw("gp", lambda g, tl=tl: g.wait_ge(E.sem["gp"], tl[1]))
        E.flush()


class Emit:
    ENGS = ("pe", "dv", "ac", "sp", "gp")

    def __init__(self, nc, es):
        self.nc = nc
        self.q = {k: [] for k in self.ENGS}
        self.sem = {k: es.enter_context(nc.semaphore(f"s_{k}")) for k in self.ENGS}
        self.sem["cc"] = es.enter_context(nc.semaphore("s_cc"))
        self.sem["gpc"] = es.enter_context(nc.semaphore("s_gpc"))
        self.cnt = {k: 0 for k in list(self.ENGS) + ["cc", "gpc"]}
        self.waited = {k: {} for k in self.ENGS}
        self.sep_fn = None
        self._last_rows = None

    def _deps(self, eng, deps):
        wl = []
        for d in deps:
            if d is None:
                continue
            if isinstance(d, list):
                for dd in d:
                    if dd is None:
                        continue
                    dsem, dval = dd
                    if dval > self.waited[eng].get(dsem, -1):
                        self.waited[eng][dsem] = dval
                        wl.append((dsem, dval))
            else:
                dsem, dval = d
                if dval > self.waited[eng].get(dsem, -1):
                    self.waited[eng][dsem] = dval
                    wl.append((dsem, dval))
        return tuple(wl)

    def addsem(self, es, nc, name):
        self.sem[name] = es.enter_context(nc.semaphore("s_" + name))
        self.cnt[name] = 0

    def op(self, eng, fn, deps=(), inc=True, rows=None, sem=None):
        step = 16 if eng in ("sp", "gp") else 1
        if eng == "pe":
            r = rows if rows is not None else (0, 128)
            last = getattr(self, "_last_rows", None)
            if (r != (0, 128) and last is not None and last != (0, 128)
                    and last != r and self.sep_fn is not None):
                sfn = self.sep_fn
                self.q["pe"].append(lambda e, sfn=sfn: sfn(e))
            self._last_rows = r
        wl = self._deps(eng, deps)
        tok = None
        key = sem if sem is not None else eng
        if inc:
            self.cnt[key] += step
            tok = (key, self.cnt[key])
        semmap = self.sem

        def run(e, fn=fn, wl=wl, inc=inc, step=step, key=key):
            for dsem, dval in wl:
                e.wait_ge(semmap[dsem], dval)
            inst = fn(e)
            if inc:
                inst.then_inc(semmap[key], step)
        self.q[eng].append(run)
        return tok

    def gpc(self, fn, deps=()):
        deps = list(deps) + ([("gpc", self.cnt["gpc"])] if self.cnt["gpc"] > 0 else [])
        wl = self._deps("gp", deps)
        self.cnt["gpc"] += 1
        tok = ("gpc", self.cnt["gpc"])
        semmap = self.sem

        def run(g, fn=fn, wl=wl):
            for dsem, dval in wl:
                g.wait_ge(semmap[dsem], dval)
            fn(g).then_inc(semmap["gpc"], 1)
        self.q["gp"].append(run)
        return tok

    def cc(self, fn, deps=()):
        wl = self._deps("gp", deps)
        self.cnt["cc"] += 1
        tok = ("cc", self.cnt["cc"])
        semmap = self.sem

        def run(g, fn=fn, wl=wl):
            for dsem, dval in wl:
                g.wait_ge(semmap[dsem], dval)
            fn(g).then_inc(semmap["cc"], 1)
        self.q["gp"].append(run)
        return tok

    def raw(self, eng, fn):
        self.q[eng].append(fn)

    def flush(self):
        nc = self.nc
        q = self.q
        with nc.Block() as blk:
            if q["pe"]:
                @blk.tensor
                def _(t, ops=tuple(q["pe"])):
                    for f in ops:
                        f(t)
            if q["dv"]:
                @blk.vector
                def _(v, ops=tuple(q["dv"])):
                    for f in ops:
                        f(v)
            if q["ac"]:
                @blk.scalar
                def _(a, ops=tuple(q["ac"])):
                    for f in ops:
                        f(a)
            if q["sp"]:
                @blk.sync
                def _(s, ops=tuple(q["sp"])):
                    for f in ops:
                        f(s)
            if q["gp"]:
                @blk.gpsimd
                def _(g, ops=tuple(q["gp"])):
                    for f in ops:
                        f(g)
        self.q = {k: [] for k in self.ENGS}


def build_program():
    nc = bass.Bass("TRN2", target_bir_lowering=False, detect_race_conditions=RACE_DETECT)

    def din(name, shape):
        return nc.dram_tensor(name, shape, F32, kind="ExternalInput").ap()

    x_rows = din("x_rows", [TPC, D])
    wq_s = din("wq_s", [D, 128])
    wk_s = din("wk_s", [D, 128])
    wv_s = din("wv_s", [D, 128])
    wrq_s = din("wrq_s", [D, 96])
    wrk_s = din("wrk_s", [D, 96])
    convq_s = din("convq_s", [128, 4])
    convk_s = din("convk_s", [128, 4])
    convv_s = din("convv_s", [128, 4])
    wf_s = din("wf_s", [D, 2])
    wl_s = din("wl_s", [D, 2])
    bfbl = din("bfbl", [4])
    gs_s = din("gs_s", [2])
    qn_w = din("qn_w", [dH])
    kn_w = din("kn_w", [dH])
    on_w = din("on_w", [D])
    wo = din("wo", [D, D])
    gate_w = din("gate_w", [E_EXP, D])
    gate_b = din("gate_b", [E_EXP])
    w1p = din("w1p", [E_EXP, INTER, 128])
    w3p = din("w3p", [E_EXP, INTER, 128])
    w2p = din("w2p", [E_EXP, D, 256])
    sw1p = din("sw1p", [INTER, 128])
    sw3p = din("sw3p", [INTER, 128])
    sw2p = din("sw2p", [D, 256])

    y_out = nc.dram_tensor("y", [TPC, D], F32, kind="ExternalOutput").ap()
    dbg = {}
    if DEBUG:
        for nm, shp in [("d_Qch", [128, S]), ("d_Kch", [128, S]), ("d_Ktok", [128, S]),
                        ("d_Vtok", [128, S]), ("d_lcum", [128, 256]), ("d_out", [128, S]),
                        ("d_x2", [TPC, D]), ("d_comb", [128, 128]), ("d_bt", [128, 256]),
                        ("d_zpk", [128, 256]), ("d_W", [128, 256]), ("d_EMT", [128, 256]),
                        ("d_NT", [128, 256]), ("d_exp", [128, 512]),
                        ("d_STb", [128, 64]), ("d_PT", [128, 128]), ("d_rstd", [1, 512]),
                        ("d_sqs", [1, 512])]:
            dt = F32 if nm in ("d_lcum", "d_x2", "d_comb", "d_bt", "d_zpk") else BF16
            dbg[nm] = nc.dram_tensor(nm, shp, dt, kind="ExternalOutput").ap()

    agX_in = nc.dram_tensor("agX_in", [D, TPC], BF16)
    agX_out = nc.dram_tensor("agX_out", [NCORES * D, TPC], BF16)
    agW_in = nc.dram_tensor("agW_in", [WCHUNK], BF16)
    agW_out = nc.dram_tensor("agW_out", [NCORES * WCHUNK], BF16)
    agO_in = nc.dram_tensor("agO_in", [NCORES, 128, TPC], BF16)
    agO_out = nc.dram_tensor("agO_out", [NCORES, 128, TPC], BF16)
    x2_scr = nc.dram_tensor("x2_scr", [TPC, D], F32)
    z_hbm = nc.dram_tensor("z_hbm", [4, S], F32)

    es = ExitStack()
    with es:
        E = Emit(nc, es)
        for _s in ("smq", "agxq", "wldq", "wstq", "p1wq", "xq0", "xq1", "zq",
                   "owq", "oq", "wobq", "g4q", "xrq", "x2wq", "x2q", "w1q",
                   "w3q", "w2q", "yq", "xbq") + tuple(f"rbq{i}" for i in range(8)):
            E.addsem(es, nc, _s)
        sb = lambda n, shape, dt=F32: es.enter_context(nc.sbuf_tensor(n, shape, dt))

        identb = sb("identb", [128, 128], BF16)
        identf = sb("identf", [128, 128], F32)
        ones1f = sb("ones1f", [1, 128], F32)
        onesvb = sb("onesvb", [128, 1], BF16)
        qnw_c = sb("qnw_c", [128, 1], F32)
        knw_c = sb("knw_c", [128, 1], F32)
        lam_c = sb("lam_c", [128, 1], F32)
        onw_c = sb("onw_c", [128, 8], F32)
        cvq_c = sb("cvq_c", [128, 4], F32)
        cvk_c = sb("cvk_c", [128, 4], F32)
        cvv_c = sb("cvv_c", [128, 4], F32)
        bfbl4 = sb("bfbl4", [4, 1], F32)
        gsbuf = sb("gsbuf", [2, 1], F32)
        absgs = sb("absgs", [2, 1], F32)
        gbrow = sb("gbrow", [1, 8], F32)
        ceps6 = sb("ceps6", [128, 1], F32)
        ceps24 = sb("ceps24", [128, 1], F32)
        ceps5 = sb("ceps5", [128, 1], F32)

        cst = None
        for fn in [
            lambda g: g.memset(identb[:], 0.0),
            lambda g: g.affine_select(out=identb[:], in_=identb[:], compare_op=ALU.not_equal,
                                      fill=1.0, base=0, pattern=[[-1, 128]], channel_multiplier=1),
            lambda g: g.memset(identf[:], 0.0),
            lambda g: g.affine_select(out=identf[:], in_=identf[:], compare_op=ALU.not_equal,
                                      fill=1.0, base=0, pattern=[[-1, 128]], channel_multiplier=1),
            lambda g: g.memset(ones1f[:], 1.0),
            lambda g: g.memset(onesvb[:], 1.0),
            lambda g: g.memset(ceps6[:], 1e-6),
            lambda g: g.memset(ceps24[:], 1e-24),
            lambda g: g.memset(ceps5[:], 1e-5),
        ]:
            cst = E.gpc(fn)

        sm = []
        sm.append(E.op("gp", lambda g: g.dma_start(qnw_c[0:64, 0:1],
            qn_w.rearrange("(a b) -> a b", b=1), allow_slow_non_contiguous=True), sem="smq"))
        sm.append(E.op("gp", lambda g: g.dma_start(qnw_c[64:128, 0:1],
            qn_w.rearrange("(a b) -> a b", b=1), allow_slow_non_contiguous=True), sem="smq"))
        sm.append(E.op("gp", lambda g: g.dma_start(knw_c[0:64, 0:1],
            kn_w.rearrange("(a b) -> a b", b=1), allow_slow_non_contiguous=True), sem="smq"))
        sm.append(E.op("gp", lambda g: g.dma_start(knw_c[64:128, 0:1],
            kn_w.rearrange("(a b) -> a b", b=1), allow_slow_non_contiguous=True), sem="smq"))
        sm.append(E.op("gp", lambda g: g.dma_start(onw_c[:],
            on_w.rearrange("(b p) -> p b", p=128), allow_slow_non_contiguous=True), sem="smq"))
        sm.append(E.op("gp", lambda g: g.dma_start(cvq_c[:], convq_s), sem="smq"))
        sm.append(E.op("gp", lambda g: g.dma_start(cvk_c[:], convk_s), sem="smq"))
        sm.append(E.op("gp", lambda g: g.dma_start(cvv_c[:], convv_s), sem="smq"))
        sm.append(E.op("gp", lambda g: g.dma_start(bfbl4[0:4, 0:1],
            bfbl.rearrange("(a b) -> a b", b=1), allow_slow_non_contiguous=True), sem="smq"))
        sm.append(E.op("gp", lambda g: g.dma_start(gsbuf[:],
            gs_s.rearrange("(a b) -> a b", b=1), allow_slow_non_contiguous=True), sem="smq"))
        sm.append(E.op("gp", lambda g: g.dma_start(gbrow[0:1, :],
            gate_b.rearrange("(b a) -> b a", b=1)), sem="smq"))
        sm_last = sm[-1]
        t_ag = E.op("ac", lambda a: a.activation(absgs[:], gsbuf[:], AF.Abs), deps=[sm[9]])

        with ExitStack() as es0:
            sb0 = lambda n, shape, dt=BF16: es0.enter_context(nc.sbuf_tensor(n, shape, dt))
            xbf = sb0("xbf", [128, S])
            xts = sb0("xts", [128, S])
            wtb = sb0("wtb", [128, 2048])
            wts = sb0("wts", [128, 2048])
            psx = es0.enter_context(nc.psum_tensor("psx", [128, 1024], BF16))
            psw = es0.enter_context(nc.psum_tensor("psw", [128, 1024], BF16))

            t_xld = E.op("gp", lambda g: g.dma_start(xbf[:],
                x_rows.rearrange("(g p) m -> p g m", p=128)), sem="xbq")
            ev_last = None
            ev_all = []
            for g_i in range(16):
                tt = None
                for mb in range(8):
                    tt = E.op("pe", lambda t, g_i=g_i, mb=mb: t.transpose(
                        psx[:, mb * 128:(mb + 1) * 128],
                        xbf[:, g_i * 1024 + mb * 128: g_i * 1024 + (mb + 1) * 128],
                        identb[:]),
                        deps=[t_xld, cst, ev_last] if mb == 0 else (), inc=(mb == 7))
                ev_last = E.op("dv", lambda v, g_i=g_i: v.tensor_copy(
                    bass.AP(xts, g_i * 128, [[S, 128], [2048, 8], [1, 128]]),
                    psx[:, 0:1024]), deps=[tt])
                ev_all.append(ev_last)
            t_agxw = E.op("gp", lambda g: g.dma_start(
                agX_in.ap().rearrange("(b p) t -> p b t", p=128), xts[:]), deps=[ev_last],
                sem="agxq")
            ag1 = E.cc(lambda g: g.collective_compute(
                "AllGather", ALU.bypass, replica_groups=[list(range(NCORES))],
                ins=[agX_in.ap().opt()], outs=[agX_out.ap().opt()]), deps=[t_agxw])

            pieces = []
            for e in range(E_EXP):
                pieces.append((w1p[e, :, :], 16, e * 3 * PIECE))
                pieces.append((w3p[e, :, :], 16, e * 3 * PIECE + PIECE))
                pieces.append((w2p[e, :, 0:128], 8, e * 3 * PIECE + 2 * PIECE))
                pieces.append((w2p[e, :, 128:256], 8, e * 3 * PIECE + 2 * PIECE + 128 * 1024))
            base_sh = E_EXP * 3 * PIECE
            pieces.append((sw1p[:, :], 16, base_sh))
            pieces.append((sw3p[:, :], 16, base_sh + PIECE))
            pieces.append((sw2p[:, 0:128], 8, base_sh + 2 * PIECE))
            pieces.append((sw2p[:, 128:256], 8, base_sh + 2 * PIECE + 128 * 1024))

            prev_out = None
            last_wev = None
            last_tt = None
            for src, nblk, coff in pieces:
                t_ld = E.op("gp", lambda g, src=src, nblk=nblk: g.dma_start(
                    wtb[:, 0:nblk * 128], src.rearrange("(k p) m -> p k m", p=128)),
                    deps=[prev_out, last_tt], sem="wldq")
                evs = []
                for half in range((nblk + 7) // 8):
                    kk0 = half * 8
                    kn = min(8, nblk - kk0)
                    tt = None
                    for k in range(kn):
                        tt = E.op("pe", lambda t, k=k, kk0=kk0: t.transpose(
                            psw[:, k * 128:(k + 1) * 128],
                            wtb[:, (kk0 + k) * 128:(kk0 + k + 1) * 128], identb[:]),
                            deps=[t_ld, cst, last_wev] if k == 0 else (), inc=(k == kn - 1))
                    last_tt = tt
                    last_wev = E.op("dv", lambda v, kk0=kk0, kn=kn: v.tensor_copy(
                        wts[:, kk0 * 128:(kk0 + kn) * 128], psw[:, 0:kn * 128]),
                        deps=[tt, prev_out])
                    evs.append(last_wev)
                prev_out = E.op("gp", lambda g, nblk=nblk, coff=coff: g.dma_start(
                    bass.AP(agW_in, coff, [[nblk * 128, 128], [1, nblk * 128]]),
                    wts[:, 0:nblk * 128]), deps=evs, sem="wstq")
            ag2 = E.cc(lambda g: g.collective_compute(
                "AllGather", ALU.bypass, replica_groups=[list(range(NCORES))],
                ins=[agW_in.ap().opt()], outs=[agW_out.ap().opt()]), deps=[prev_out])

            E.flush()

        if PHASE == "p0":
            _dummy_out(nc, E, es, y_out)
            return nc

        with ExitStack() as es1:
            sb1 = lambda n, shape, dt=F32: es1.enter_context(nc.sbuf_tensor(n, shape, dt))
            psf1 = lambda n, shape: es1.enter_context(nc.psum_tensor(n, shape, F32))
            psb1 = lambda n, shape: es1.enter_context(nc.psum_tensor(n, shape, BF16))

            Kch = sb1("Kch", [128, S], BF16)
            Qch = sb1("Qch", [128, S], BF16)
            Ktok = sb1("Ktok", [128, S], BF16)
            Vtok = sb1("Vtok", [128, S], BF16)
            obuf0 = sb1("obuf0", [128, 128], BF16)
            obuf1 = sb1("obuf1", [128, 128], BF16)
            xt0 = sb1("xt0", [128, 4096], BF16)
            xt1 = sb1("xt1", [128, 4096], BF16)
            wqb8 = sb1("wqb8", [128, 1024], BF16)
            wkb8 = sb1("wkb8", [128, 1024], BF16)
            wvb8 = sb1("wvb8", [128, 1024], BF16)
            wrq8 = sb1("wrq8", [128, 8 * 48], BF16)
            wfl8 = sb1("wfl8", [128, 8 * 4], BF16)
            wrq8b = sb1("wrq8b", [128, 8 * 48], BF16)
            wrk8 = sb1("wrk8", [128, 8 * 48], BF16)
            wrk8b = sb1("wrk8b", [128, 8 * 48], BF16)
            qw = sb1("qw", [128, 516], BF16)
            kw = sb1("kw", [128, 516], BF16)
            vw = sb1("vw", [128, 516], BF16)
            cq = sb1("cq", [128, 512], BF16)
            ck = sb1("ck", [128, 512], BF16)
            cv = sb1("cv", [128, 512], BF16)
            ktmp = sb1("ktmp", [128, 512], BF16)
            rqsb = sb1("rqsb", [128, 512], BF16)
            rksb = sb1("rksb", [128, 512], BF16)
            q2t = sb1("q2t", [128, 512], BF16)
            k2t = sb1("k2t", [128, 512], BF16)
            flst = sb1("flst", [4, 512], F32)
            rr1 = sb1("rr1", [2, 512], F32)
            rr2 = sb1("rr2", [2, 512], F32)
            rr3 = sb1("rr3", [2, 512], F32)

            triu_f = sb1("triu_f", [128, 128], F32)
            inclm = sb1("inclm", [128, 256], F32)
            strictm = sb1("strictm", [128, 256], F32)
            bdones = sb1("bdones", [128, 2], BF16)
            self2f = sb1("self2f", [2, 128], F32)
            sel48 = sb1("sel48", [2, 128], F32)
            i64two = sb1("i64two", [128, 64], F32)

            pP1 = psf1("pP1", [128, 512])
            pP2 = psf1("pP2", [128, 512])
            pRED = psf1("pRED", [128, 512])
            pBC1 = psf1("pBC1", [128, 512])
            pBC2 = psf1("pBC2", [128, 512])
            pT1 = psb1("pT1", [128, 1024])
            pFL = psf1("pFL", [128, 512])

            twq = E.op("gp", lambda g: g.dma_start(wqb8[:],
                wq_s.rearrange("(b p) c -> p b c", p=128)), sem="p1wq")
            twk = E.op("gp", lambda g: g.dma_start(wkb8[:],
                wk_s.rearrange("(b p) c -> p b c", p=128)), sem="p1wq")
            twv = E.op("gp", lambda g: g.dma_start(wvb8[:],
                wv_s.rearrange("(b p) c -> p b c", p=128)), sem="p1wq")
            tw1 = E.op("gp", lambda g: g.dma_start(
                bass.AP(wrq8, 0, [[8 * 48, 128], [48, 8], [1, 48]]),
                wrq_s.rearrange("(b p) c -> p b c", p=128)[:, :, 0:48]), sem="p1wq")
            tw2 = E.op("gp", lambda g: g.dma_start(
                bass.AP(wfl8, 0, [[8 * 4, 128], [4, 8], [1, 2]]),
                wf_s.rearrange("(b p) c -> p b c", p=128)), sem="p1wq")
            tw3 = E.op("gp", lambda g: g.dma_start(
                bass.AP(wfl8, 2, [[8 * 4, 128], [4, 8], [1, 2]]),
                wl_s.rearrange("(b p) c -> p b c", p=128)), sem="p1wq")
            tw4 = E.op("gp", lambda g: g.dma_start(
                bass.AP(wrq8b, 0, [[8 * 48, 128], [48, 8], [1, 48]]),
                wrq_s.rearrange("(b p) c -> p b c", p=128)[:, :, 48:96]), sem="p1wq")
            tw5 = E.op("gp", lambda g: g.dma_start(
                bass.AP(wrk8, 0, [[8 * 48, 128], [48, 8], [1, 48]]),
                wrk_s.rearrange("(b p) c -> p b c", p=128)[:, :, 0:48]), sem="p1wq")
            tw6 = E.op("gp", lambda g: g.dma_start(
                bass.AP(wrk8b, 0, [[8 * 48, 128], [48, 8], [1, 48]]),
                wrk_s.rearrange("(b p) c -> p b c", p=128)[:, :, 48:96]), sem="p1wq")

            cst1 = None
            for fn in [
                lambda g: g.memset(triu_f[:], 0.0),
                lambda g: g.affine_select(out=triu_f[:], in_=triu_f[:], compare_op=ALU.is_gt,
                                          fill=1.0, base=0, pattern=[[-1, 128]], channel_multiplier=1),
                lambda g: g.memset(inclm[:], 0.0),
                lambda g: g.affine_select(out=inclm[:, 0:128], in_=inclm[:, 0:128], compare_op=ALU.is_ge,
                                          fill=NEG, base=0, pattern=[[1, 128]], channel_multiplier=-1),
                lambda g: g.affine_select(out=inclm[:, 128:256], in_=inclm[:, 128:256], compare_op=ALU.is_ge,
                                          fill=NEG, base=0, pattern=[[1, 128]], channel_multiplier=-1),
                lambda g: g.memset(strictm[:], 0.0),
                lambda g: g.affine_select(out=strictm[:, 0:128], in_=strictm[:, 0:128], compare_op=ALU.is_gt,
                                          fill=NEG, base=0, pattern=[[1, 128]], channel_multiplier=-1),
                lambda g: g.affine_select(out=strictm[:, 128:256], in_=strictm[:, 128:256], compare_op=ALU.is_gt,
                                          fill=NEG, base=0, pattern=[[1, 128]], channel_multiplier=-1),
                lambda g: g.memset(bdones[:], 0.0),
                lambda g: g.memset(bdones[0:64, 0:1], 1.0),
                lambda g: g.memset(bdones[64:128, 1:2], 1.0),
                lambda g: g.memset(self2f[:], 1.0),
                lambda g: g.affine_select(out=self2f[:], in_=self2f[:], compare_op=ALU.is_ge,
                                          fill=0.0, base=0, pattern=[[1, 128]], channel_multiplier=-64),
                lambda g: g.affine_select(out=self2f[:], in_=self2f[:], compare_op=ALU.is_ge,
                                          fill=0.0, base=63, pattern=[[-1, 128]], channel_multiplier=64),
                lambda g: g.memset(sel48[:], 1.0),
                lambda g: g.affine_select(out=sel48[:], in_=sel48[:], compare_op=ALU.is_ge,
                                          fill=0.0, base=0, pattern=[[1, 128]], channel_multiplier=-64),
                lambda g: g.affine_select(out=sel48[:], in_=sel48[:], compare_op=ALU.is_ge,
                                          fill=0.0, base=47, pattern=[[-1, 128]], channel_multiplier=64),
            ]:
                cst1 = E.gpc(fn)
            t_i64a = E.op("dv", lambda v: v.tensor_add(i64two[0:64, :], identf[0:64, 0:64],
                identf[0:64, 64:128]), deps=[cst1])
            t_i64b = E.op("dv", lambda v: v.tensor_add(i64two[64:128, :], identf[64:128, 0:64],
                identf[64:128, 64:128]), deps=[t_i64a])
            t_lam0 = E.op("pe", lambda t: t.matmul(pRED[:, 0:1], sel48[:], absgs[:],
                start=True, stop=True), deps=[t_ag, cst1])
            t_lam = E.op("dv", lambda v: v.tensor_copy(lam_c[:], pRED[:, 0:1]), deps=[t_lam0])

            h0 = E.gpc(lambda g: g.memset(qw[:, 0:3], 0.0))
            h1 = E.gpc(lambda g: g.memset(kw[:, 0:3], 0.0))
            h2 = E.gpc(lambda g: g.memset(vw[:, 0:3], 0.0))

            xtok = [None, None]
            xt_rd = [None, None]
            halo_q, halo_k, halo_v = [h0], [h1], [h2]
            evQp = evKp = evVp = None
            fcq = fck = fcv = None
            fFL = None
            fq2 = fk2 = None
            fktmp = None
            fpRED = t_lam
            bc1_free = bc2_free = None
            trK = trV = None
            zstage = None
            rr_free = [None, None, None]

            for g_i in range(32):
                xt = xt0 if g_i % 2 == 0 else xt1
                t0g = g_i * 512
                rank = t0g // TPC
                toff = t0g % TPC
                gsl = slice(g_i * 512, (g_i + 1) * 512)
                xtok[g_i % 2] = E.op("sp", lambda s, rank=rank, toff=toff, xt=xt:
                    s.dma_start(bass.AP(xt, 0, [[4096, 128], [512, 8], [1, 512]]),
                                bass.AP(agX_out, rank * D * TPC + toff,
                                        [[TPC, 128], [128 * TPC, 8], [1, 512]])),
                    deps=[xtok[g_i % 2], xt_rd[g_i % 2], ag1, tw6],
                    sem=("xq0" if g_i % 2 == 0 else "xq1"))
                t_x = xtok[g_i % 2]

                def proj(bank, wtile, ncols, deps, pstart=True):
                    tk = None
                    for mb in range(8):
                        tk = E.op("pe", lambda t, mb=mb, bank=bank, wtile=wtile, ncols=ncols, xt=xt:
                            t.matmul(bank[0:ncols, 0:512],
                                     wtile[:, mb * ncols:(mb + 1) * ncols],
                                     xt[:, mb * 512:(mb + 1) * 512],
                                     start=(mb == 0 and pstart), stop=(mb == 7)),
                            deps=deps if mb == 0 else (), inc=(mb == 7))
                    return tk

                tQ = proj(pP1, wqb8, 128, [t_x, twq, evQp])
                tK = proj(pP2, wkb8, 128, [t_x, twk, evKp])
                evQ = E.op("ac", lambda a: a.activation(qw[:, 3:515], pP1[:, 0:512], AF.Copy),
                           deps=[tQ, halo_q[-1]])
                tV = proj(pP1, wvb8, 128, [evQ, twv])
                evK = E.op("ac", lambda a: a.activation(kw[:, 3:515], pP2[:, 0:512], AF.Copy),
                           deps=[tK, halo_k[-1]])
                tRa = None
                for mb in range(8):
                    tRa = E.op("pe", lambda t, mb=mb, xt=xt: t.matmul(
                        pP2[0:48, 0:512], wrq8[:, mb * 48:(mb + 1) * 48],
                        xt[:, mb * 512:(mb + 1) * 512], start=(mb == 0), stop=(mb == 7)),
                        deps=[evK, tw6] if mb == 0 else (), inc=False)
                for mb in range(8):
                    tRa = E.op("pe", lambda t, mb=mb, xt=xt: t.matmul(
                        pP2[64:112, 0:512], wrq8b[:, mb * 48:(mb + 1) * 48],
                        xt[:, mb * 512:(mb + 1) * 512], start=(mb == 0), stop=(mb == 7)),
                        deps=[tw6] if mb == 0 else (), inc=(mb == 7))
                tFL = None
                for mb in range(8):
                    tFL = E.op("pe", lambda t, mb=mb, xt=xt: t.matmul(
                        pFL[0:4, 0:512], wfl8[:, mb * 4:(mb + 1) * 4],
                        xt[:, mb * 512:(mb + 1) * 512], start=(mb == 0), stop=(mb == 7)),
                        deps=[tw6, fFL] if mb == 0 else (), inc=(mb == 7))
                evV = E.op("ac", lambda a: a.activation(vw[:, 3:515], pP1[:, 0:512], AF.Copy),
                           deps=[tV, halo_v[-1]])
                evR1 = E.op("dv", lambda v: v.tensor_copy(rqsb[:, 0:512], pP2[:, 0:512]),
                            deps=[tRa])
                evFL = E.op("dv", lambda v: v.tensor_scalar(flst[0:4, 0:512],
                    pFL[0:4, 0:512], bfbl4[0:4, 0:1], None, op0=ALU.add),
                    deps=[tFL, zstage, sm_last])
                fFL = evFL
                zstage = E.op("gp", lambda g, g_i=g_i: g.dma_start(
                    bass.AP(z_hbm, g_i * 512, [[S, 4], [1, 512]]), flst[0:4, :]),
                    deps=[evFL], sem="zq")
                tRk = None
                for mb in range(8):
                    tRk = E.op("pe", lambda t, mb=mb, xt=xt: t.matmul(
                        pP1[0:48, 0:512], wrk8[:, mb * 48:(mb + 1) * 48],
                        xt[:, mb * 512:(mb + 1) * 512], start=(mb == 0), stop=(mb == 7)),
                        deps=[evV, tw6] if mb == 0 else (), inc=False)
                for mb in range(8):
                    tRk = E.op("pe", lambda t, mb=mb, xt=xt: t.matmul(
                        pP1[64:112, 0:512], wrk8b[:, mb * 48:(mb + 1) * 48],
                        xt[:, mb * 512:(mb + 1) * 512], start=(mb == 0), stop=(mb == 7)),
                        deps=[tw6] if mb == 0 else (), inc=(mb == 7))
                evR2 = E.op("dv", lambda v: v.tensor_copy(rksb[:, 0:512], pP1[0:128, 0:512]),
                            deps=[tRk])
                xt_rd[g_i % 2] = tRk
                evQp = evR2
                evKp = [evR1, evFL]

                def conv(dst, src, wcol, dep0, dfree=None):
                    t1 = E.op("dv", lambda v: v.tensor_scalar(dst[:, 0:512], src[:, 0:512],
                        wcol[:, 0:1], None, op0=ALU.mult), deps=[dep0, sm_last, dfree])
                    for j in range(1, 4):
                        t1 = E.op("dv", lambda v, j=j: v.scalar_tensor_tensor(
                            dst[:, 0:512], src[:, j:512 + j], wcol[:, j:j + 1], dst[:, 0:512],
                            op0=ALU.mult, op1=ALU.add), deps=[t1])
                    return t1
                tcq = conv(cq, qw, cvq_c, evQ, fcq)
                tck = conv(ck, kw, cvk_c, evK, fck)
                tcv = conv(cv, vw, cvv_c, evV, fcv)
                halo_q.append(E.op("dv", lambda v: v.tensor_copy(qw[:, 0:3], qw[:, 512:515]),
                                   deps=[tcq]))
                halo_k.append(E.op("dv", lambda v: v.tensor_copy(kw[:, 0:3], kw[:, 512:515]),
                                   deps=[tck]))
                halo_v.append(E.op("dv", lambda v: v.tensor_copy(vw[:, 0:3], vw[:, 512:515]),
                                   deps=[tcv]))

                tq2 = E.op("ac", lambda a: a.activation(q2t[:], cq[:], AF.Square),
                           deps=[tcq, fq2])
                trq = E.op("pe", lambda t: t.matmul(pRED[0:2, 0:512], bdones[:], q2t[:],
                    start=True, stop=True), deps=[tq2, cst1, fpRED])
                tl1 = E.op("ac", lambda a: a.activation(rr1[:], pRED[0:2, 0:512], AF.Ln,
                    scale=1.0 / dH, bias=ceps6[0:2, 0:1]), deps=[trq, rr_free[0]])
                tl2 = E.op("ac", lambda a: a.activation(rr1[:], rr1[:], AF.Exp, scale=-0.5),
                           deps=[tl1])
                tbq = E.op("pe", lambda t: t.matmul(pBC1[:, 0:512], self2f[:], rr1[:],
                    start=True, stop=True), deps=[tl2, bc1_free])
                rr_free[0] = tbq
                tqa = E.op("dv", lambda v, gsl=gsl: v.scalar_tensor_tensor(
                    Qch[:, gsl], cq[:], qnw_c[:, 0:1], pBC1[:, 0:512],
                    op0=ALU.mult, op1=ALU.mult), deps=[tbq])
                tk2 = E.op("ac", lambda a: a.activation(k2t[:], ck[:], AF.Square),
                           deps=[tck, fk2])
                trk = E.op("pe", lambda t: t.matmul(pRED[32:34, 0:512], bdones[:], k2t[:],
                    start=True, stop=True), deps=[tk2, tl1], inc=True)
                tl3 = E.op("ac", lambda a: a.activation(rr2[:], pRED[32:34, 0:512], AF.Ln,
                    scale=1.0 / dH, bias=ceps6[0:2, 0:1]), deps=[trk, rr_free[1]])
                tl4 = E.op("ac", lambda a: a.activation(rr2[:], rr2[:], AF.Exp, scale=-0.5),
                           deps=[tl3])
                tbk = E.op("pe", lambda t: t.matmul(pBC2[:, 0:512], self2f[:], rr2[:],
                    start=True, stop=True), deps=[tl4, bc2_free])
                rr_free[1] = tbk
                tka = E.op("dv", lambda v: v.scalar_tensor_tensor(
                    ktmp[:], ck[:], knw_c[:, 0:1], pBC2[:, 0:512],
                    op0=ALU.mult, op1=ALU.mult), deps=[tbk, fktmp])
                bc2_free = tka
                tqb1 = E.op("dv", lambda v, gsl=gsl: v.scalar_tensor_tensor(
                    Qch[0:48, gsl], rqsb[0:48, :], lam_c[0:48, 0:1], Qch[0:48, gsl],
                    op0=ALU.mult, op1=ALU.add), deps=[tqa, evR1, t_lam])
                tqb2 = E.op("dv", lambda v, gsl=gsl: v.scalar_tensor_tensor(
                    Qch[64:112, gsl], rqsb[64:112, :], lam_c[64:112, 0:1], Qch[64:112, gsl],
                    op0=ALU.mult, op1=ALU.add), deps=[tqb1])
                tkb1 = E.op("dv", lambda v: v.scalar_tensor_tensor(
                    ktmp[0:48, :], rksb[0:48, :], lam_c[0:48, 0:1], ktmp[0:48, :],
                    op0=ALU.mult, op1=ALU.add), deps=[tka, evR2])
                tkb2 = E.op("dv", lambda v: v.scalar_tensor_tensor(
                    ktmp[64:112, :], rksb[64:112, :], lam_c[64:112, 0:1], ktmp[64:112, :],
                    op0=ALU.mult, op1=ALU.add), deps=[tkb1])
                tk2b = E.op("ac", lambda a: a.activation(k2t[:], ktmp[:], AF.Square),
                            deps=[tkb2, trk])
                trk2 = E.op("pe", lambda t: t.matmul(pRED[64:66, 0:512], bdones[:], k2t[:],
                    start=True, stop=True), deps=[tk2b, tl3], inc=True)
                tl5 = E.op("ac", lambda a: a.activation(rr3[:], pRED[64:66, 0:512], AF.Ln,
                    scale=1.0, bias=ceps24[0:2, 0:1]), deps=[trk2, rr_free[2]])
                tl6 = E.op("ac", lambda a: a.activation(rr3[:], rr3[:], AF.Exp, scale=-0.5),
                           deps=[tl5])
                tbk2 = E.op("pe", lambda t: t.matmul(pBC1[:, 0:512], self2f[:], rr3[:],
                    start=True, stop=True), deps=[tl6, tqa])
                rr_free[2] = tbk2
                tkn = E.op("dv", lambda v, gsl=gsl: v.tensor_mul(Kch[:, gsl], ktmp[:],
                    pBC1[:, 0:512]), deps=[tbk2])
                bc1_free = tkn
                ttk = None
                for j in range(4):
                    ttk = E.op("pe", lambda t, j=j, g_i=g_i: t.transpose(
                        pT1[:, j * 128:(j + 1) * 128],
                        Kch[:, g_i * 512 + j * 128: g_i * 512 + (j + 1) * 128], identb[:]),
                        deps=[tkn, trK, trV] if j == 0 else (), inc=False)
                for j in range(4):
                    ttk = E.op("pe", lambda t, j=j: t.transpose(
                        pT1[:, 512 + j * 128: 512 + (j + 1) * 128],
                        cv[:, j * 128:(j + 1) * 128], identb[:]),
                        deps=[tcv] if j == 0 else (), inc=(j == 3))
                trK = E.op("dv", lambda v, gsl=gsl: v.tensor_copy(Ktok[:, gsl], pT1[:, 0:512]),
                           deps=[ttk])
                trV = E.op("dv", lambda v, gsl=gsl: v.tensor_copy(Vtok[:, gsl],
                    pT1[:, 512:1024]), deps=[ttk])
                fcq = tq2
                fck = tk2
                fcv = ttk
                fq2 = trq
                fk2 = trk2
                fktmp = tk2b
                fpRED = tl5

            spc = sb1("spc", [128, 256], F32)
            spzlc = sb1("spzlc", [128, 256], F32)
            sp_t = sb1("sp_t", [128, 256], F32)
            spzl_t = sb1("spzl_t", [128, 256], F32)
            lcum_t = sb1("lcum_t", [128, 256], F32)
            lcum_c = sb1("lcum_c", [128, 256], F32)
            npos_c = sb1("npos_c", [128, 256], F32)
            b_t = sb1("b_t", [128, 256], F32)
            c2_t = sb1("c2_t", [128, 256], F32)
            d2_t = sb1("d2_t", [128, 256], F32)
            gC2 = sb1("gC2", [128, 128], F32)
            tmp_t = sb1("tmp_t", [128, 256], F32)
            rbufL = sb1("rbufL", [1, 256], F32)

            tr_sp = []
            for r, dst, hcol in ((0, spc, 0), (1, spc, 1), (2, spzlc, 0), (3, spzlc, 1)):
                tr_sp.append(E.op("gp", lambda g, r=r, dst=dst, hcol=hcol: g.dma_start(
                    dst[:, hcol * 128:(hcol + 1) * 128],
                    bass.AP(z_hbm, r * S, [[128, 128], [1, 128]])),
                    deps=[zstage]))
            t_spa = E.op("ac", lambda a: a.activation(spc[:], spc[:], AF.Exp, scale=-1.0),
                         deps=tr_sp)
            t_spb = E.op("ac", lambda a: a.activation(spc[:], spc[:], AF.Ln, bias=1.0),
                         deps=[t_spa])
            t_spc2 = E.op("ac", lambda a: a.activation(spzlc[:], spzlc[:], AF.Exp, scale=-1.0),
                          deps=[t_spb])
            t_sp2 = E.op("ac", lambda a: a.activation(spzlc[:], spzlc[:], AF.Ln, bias=1.0),
                         deps=[t_spc2])
            tt1 = E.op("pe", lambda t: t.transpose(pBC1[:, 0:128], spc[:, 0:128], identf[:]),
                       deps=[t_spb, bc1_free, tkn], inc=False)
            tt2 = E.op("pe", lambda t: t.transpose(pBC1[:, 128:256], spc[:, 128:256], identf[:]),
                       inc=True)
            t_spt = E.op("dv", lambda v: v.tensor_copy(sp_t[:], pBC1[:, 0:256]), deps=[tt2])
            tt3 = E.op("pe", lambda t: t.transpose(pBC1[:, 256:384], spzlc[:, 0:128], identf[:]),
                       deps=[t_spt, t_sp2], inc=False)
            tt4 = E.op("pe", lambda t: t.transpose(pBC1[:, 384:512], spzlc[:, 128:256], identf[:]),
                       inc=True)
            t_szt = E.op("dv", lambda v: v.tensor_copy(spzl_t[:], pBC1[:, 256:512]), deps=[tt4])
            t_lc0 = E.op("pe", lambda t: t.matmul(pBC2[:, 0:256], triu_f[:], sp_t[:],
                start=True, stop=True), deps=[t_spt, bc2_free, cst1])
            t_lct = E.op("dv", lambda v: v.tensor_copy(lcum_t[:], pBC2[:, 0:256]), deps=[t_lc0])
            tt5 = E.op("pe", lambda t: t.transpose(pBC1[:, 0:128], lcum_t[:, 0:128], identf[:]),
                       deps=[t_lct, t_spt, t_szt], inc=False)
            tt6 = E.op("pe", lambda t: t.transpose(pBC1[:, 128:256], lcum_t[:, 128:256], identf[:]),
                       inc=True)
            t_lcc = E.op("dv", lambda v: v.tensor_copy(lcum_c[:], pBC1[:, 0:256]), deps=[tt6])
            t_np1 = E.op("dv", lambda v: v.tensor_sub(npos_c[:], lcum_c[:], spc[:]), deps=[t_lcc])
            t_np2 = E.op("dv", lambda v: v.tensor_add(npos_c[:], npos_c[:], spzlc[:]),
                         deps=[t_np1])
            t_bt = E.op("ac", lambda a: a.activation(b_t[:], spzl_t[:], AF.Exp, scale=-1.0),
                        deps=[t_szt])
            t_c21 = E.op("dv", lambda v: v.tensor_sub(tmp_t[:], sp_t[:], spzl_t[:]),
                         deps=[t_szt, t_spt])
            t_c22 = E.op("dv", lambda v: v.tensor_sub(tmp_t[:], tmp_t[:], lcum_t[:]),
                         deps=[t_c21, t_lct])
            t_c2t = E.op("ac", lambda a: a.activation(c2_t[:], tmp_t[:], AF.Exp), deps=[t_c22])
            t_rb = E.op("sp", lambda s: s.dma_start(rbufL[0:1, :], lcum_t[127:128, 0:256]),
                        deps=[t_lct])
            t_bcL = E.op("pe", lambda t: t.matmul(pBC2[:, 0:256], ones1f[:], rbufL[0:1, :],
                start=True, stop=True), deps=[t_rb, t_lcc])
            t_d21 = E.op("dv", lambda v: v.tensor_sub(tmp_t[:], lcum_t[:], pBC2[:, 0:256]),
                         deps=[t_bcL, t_c2t])
            t_d2t = E.op("ac", lambda a: a.activation(d2_t[:], tmp_t[:], AF.Exp), deps=[t_d21])
            t_gcb = E.op("ac", lambda a: a.activation(tmp_t[:], pBC2[:, 0:256], AF.Exp,
                scale=-1.0), deps=[t_d21, t_d2t])
            t_gc1 = E.op("dv", lambda v: v.tensor_copy(gC2[0:64, :], tmp_t[0:64, 0:128]),
                         deps=[t_gcb])
            t_gc2 = E.op("dv", lambda v: v.tensor_copy(gC2[64:128, :], tmp_t[64:128, 128:256]),
                         deps=[t_gcb])
            if DEBUG:
                E.op("gp", lambda g: g.dma_start(dbg["d_lcum"], lcum_t[:]), deps=[t_lct])
                E.op("gp", lambda g: g.dma_start(dbg["d_bt"], b_t[:]), deps=[t_bt])
                E.op("gp", lambda g: g.dma_start(dbg["d_zpk"], spc[:]), deps=[t_spb])
                E.op("gp", lambda g: g.dma_start(dbg["d_Qch"], Qch[:]), deps=[tqb2])
                E.op("gp", lambda g: g.dma_start(dbg["d_Kch"], Kch[:]), deps=[tkn])
                E.op("gp", lambda g: g.dma_start(dbg["d_Ktok"], Ktok[:]), deps=[trK])
                E.op("gp", lambda g: g.dma_start(dbg["d_Vtok"], Vtok[:]), deps=[trV])
            E.flush()

            if PHASE == "p1":
                _dummy_out(nc, E, es, y_out)
                return nc

            NTs = sb1("NTs", [128, 256], BF16)
            Ms = sb1("Ms", [128, 256], BF16)
            Wsol = sb1("Wsol", [128, 256], BF16)
            EMT = sb1("EMT", [128, 256], BF16)
            expsb = sb1("expsb", [128, 512], BF16)
            gbsb = sb1("gbsb", [128, 256], BF16)
            argb = sb1("argb", [128, 256], F32)
            argb2 = sb1("argb2", [128, 256], F32)
            d2buf = sb1("d2buf", [128, 256], BF16)
            ptmp = sb1("ptmp", [128, 128], BF16)
            PTs = sb1("PTs", [128, 128], BF16)
            STf = sb1("STf", [128, 64], F32)
            STb = sb1("STb", [128, 64], BF16)
            Mch = sb1("Mch", [128, 64], F32)
            rbAs = [sb1(f"rbA{i}", [1, 256], F32) for i in range(2)]
            rbBs = [sb1(f"rbB{i}", [1, 256], F32) for i in range(2)]

            bkA, bkB, bkC, bkD, bkE = pP1, pP2, pRED, pBC1, pBC2
            bkFO = psf1("bkFO", [128, 512])
            bkF = bkFO
            pO = pFL
            bkT = pT1

            E.sep_fn = lambda t: t.ldweights(identb[:])
            t_s0 = E.gpc(lambda g: g.memset(STf[:], 0.0))
            t_s1 = E.gpc(lambda g: g.memset(STb[:], 0.0))
            tok_ST, tok_STb = t_s0, t_s1

            fA = fB = fC = fD = fE = fF = fO = fT = None
            fNT = fMs = fW = fEMT = fgb = farg = [None]
            farg_e = farg_n = fexp_e = fexp_n = None
            fd2 = fpt = fmc = None
            rb_free = [None] * 2
            ow_tok = [None, None]
            t_oev = None

            for c in range(NCH_RUN):
                par = c % 2
                rbA = rbAs[par]
                rbB = rbBs[par]
                csl = slice(c * 128, (c + 1) * 128)
                if SCAN_PART != "g2":
                    t_rba = E.op("sp", lambda s, c=c, rbA=rbA: s.dma_start(rbA[0:1, :],
                        lcum_c[c:c + 1, 0:256]), deps=[t_lcc, rb_free[par]],
                        sem=f"rbq{par}")
                    t_rbb = E.op("sp", lambda s, c=c, rbB=rbB: s.dma_start(rbB[0:1, :],
                        npos_c[c:c + 1, 0:256]), deps=[t_np2], sem=f"rbq{par}")
                else:
                    t_rba = t_rbb = None
                tG = None
                glist = [] if SCAN_PART == "g3" else [
                    (Kch, Kch, 0), (Kch, Kch, 128), (Kch, Qch, 256), (Kch, Qch, 384)]
                for i, (srcL, srcR, col) in enumerate(glist):
                    h = i % 2
                    hsl = slice(h * 64, (h + 1) * 64)
                    tG = E.op("pe", lambda t, srcL=srcL, srcR=srcR, col=col, hsl=hsl, csl=csl:
                        t.matmul(bkA[:, col:col + 128], srcL[hsl, csl], srcR[hsl, csl],
                                 start=True, stop=True),
                        deps=[fA, tkn, tqb2] if i == 0 else (), inc=(i == 3),
                        rows=(h * 64, 64))
                tBC = None
                for i in range(4 if SCAN_PART != "g2" else 0):
                    h = i % 2
                    rb = rbA if i < 2 else rbB
                    col = i * 128
                    tBC = E.op("pe", lambda t, h=h, rb=rb, col=col: t.matmul(
                        bkB[:, col:col + 128], ones1f[:], rb[0:1, h * 128:(h + 1) * 128],
                        start=True, stop=True),
                        deps=[t_rba, t_rbb, fB] if i == 0 else (), inc=(i == 3),
                        rows=(0, 1))
                if SCAN_PART in ("g", "g2", "g3"):
                    fO = tG if tG else tBC
                    ow_tok[par] = tBC if tBC else tG
                    fA = tG
                    fB = tBC
                    continue
                t_ae = E.op("dv", lambda v: v.tensor_sub(argb[:], inclm[:], bkB[:, 0:256]),
                            deps=[tBC, farg_e])
                t_an = E.op("dv", lambda v: v.tensor_sub(argb2[:], strictm[:], bkB[:, 256:512]),
                            deps=[tBC, farg_n])
                t_e = None
                for h in range(2):
                    t_e = E.op("ac", lambda a, h=h, c=c: a.activation(
                        expsb[:, h * 128:(h + 1) * 128], argb[:, h * 128:(h + 1) * 128],
                        AF.Exp, bias=lcum_t[:, h * 128 + c: h * 128 + c + 1]),
                        deps=[t_ae, fexp_e] if h == 0 else (), inc=(h == 1))
                farg_e = t_e
                t_en = None
                for h in range(2):
                    t_en = E.op("ac", lambda a, h=h, c=c: a.activation(
                        expsb[:, 256 + h * 128:256 + (h + 1) * 128],
                        argb2[:, h * 128:(h + 1) * 128], AF.Exp,
                        bias=lcum_t[:, h * 128 + c: h * 128 + c + 1]),
                        deps=[t_an, fexp_n] if h == 0 else (), inc=(h == 1))
                farg_n = t_en
                t_gb = E.op("ac", lambda a: a.activation(gbsb[:], bkB[:, 0:256], AF.Exp,
                    scale=-1.0), deps=[tBC, fgb])
                fB = [t_ae, t_an, t_gb]
                if SCAN_PART == "arg":
                    fO = t_gb
                    ow_tok[par] = t_gb
                    fA = tG
                    fexp_e = t_e
                    fexp_n = t_en
                    continue
                t_emt = E.op("dv", lambda v: v.tensor_mul(EMT[:], bkA[:, 256:512],
                    expsb[:, 0:256]), deps=[tG, t_e, fEMT])
                fexp_e = t_emt
                t_nt = E.op("dv", lambda v: v.scalar_tensor_tensor(NTs[:], bkA[:, 0:256],
                    -1.0, expsb[:, 256:512], op0=ALU.mult, op1=ALU.mult),
                    deps=[tG, t_en, fNT])
                fexp_n = t_nt
                fA = t_nt
                t_w = E.op("dv", lambda v, c=c, csl=csl: v.tensor_scalar(Wsol[:, 0:64],
                    Vtok[:, csl][:, 0:64], b_t[:, c:c + 1], None, op0=ALU.mult),
                    deps=[t_bt, fW, trV])
                t_w = E.op("dv", lambda v, c=c, csl=csl: v.tensor_scalar(Wsol[:, 128:192],
                    Vtok[:, csl][:, 64:128], b_t[:, 128 + c:129 + c], None, op0=ALU.mult),
                    deps=[t_w])
                t_w = E.op("dv", lambda v, c=c, csl=csl: v.tensor_scalar(Wsol[:, 64:128],
                    Ktok[:, csl][:, 0:64], c2_t[:, c:c + 1], None, op0=ALU.mult),
                    deps=[t_w, t_c2t, trK])
                t_w = E.op("dv", lambda v, c=c, csl=csl: v.tensor_scalar(Wsol[:, 192:256],
                    Ktok[:, csl][:, 64:128], c2_t[:, 128 + c:129 + c], None, op0=ALU.mult),
                    deps=[t_w])
                tok_W = t_w
                if SCAN_PART == "rhs":
                    fO = t_w
                    ow_tok[par] = t_w
                    fW = t_w
                    continue
                t_m0 = None
                for h in range(2):
                    t_m0 = E.op("pe", lambda t, h=h: t.transpose(
                        bkT[:, h * 128:(h + 1) * 128], NTs[:, h * 128:(h + 1) * 128], identb[:]),
                        deps=[t_nt, fT] if h == 0 else (), inc=(h == 1))
                tok_Ms = E.op("dv", lambda v: v.tensor_copy(Ms[:], bkT[:, 0:256]),
                              deps=[t_m0, fMs])
                fT = tok_Ms
                tok_NT = t_nt
                if SCAN_PART == "pro":
                    fO = tok_Ms
                    ow_tok[par] = tok_Ms
                    continue

                tdw = None
                for j in range(7):
                    for h in range(2):
                        tdw = E.op("pe", lambda t, h=h: t.matmul(
                            bkC[:, h * 128:(h + 1) * 128], NTs[:, h * 128:(h + 1) * 128],
                            Wsol[:, h * 128:(h + 1) * 128], start=True, stop=True),
                            deps=[tok_NT, tok_W, fC] if h == 0 else (), inc=(h == 1))
                    if j < 6:
                        tsq = None
                        for h in range(2):
                            tsq = E.op("pe", lambda t, h=h: t.matmul(
                                bkD[:, h * 128:(h + 1) * 128], NTs[:, h * 128:(h + 1) * 128],
                                Ms[:, h * 128:(h + 1) * 128], start=True, stop=True),
                                deps=[tok_Ms, fD] if h == 0 else (), inc=False)
                        for h in range(2):
                            tsq = E.op("pe", lambda t, h=h: t.matmul(
                                bkD[:, 256 + h * 128:256 + (h + 1) * 128],
                                Ms[:, h * 128:(h + 1) * 128], NTs[:, h * 128:(h + 1) * 128],
                                start=True, stop=True), deps=(), inc=(h == 1))
                        tok_Ms = E.op("ac", lambda a: a.activation(Ms[:], bkD[:, 0:256],
                            AF.Copy), deps=[tsq])
                        tok_NT = E.op("ac", lambda a: a.activation(NTs[:], bkD[:, 256:512],
                            AF.Copy), deps=[tsq])
                        fD = tok_NT
                    tok_W = E.op("dv", lambda v: v.tensor_add(Wsol[:], Wsol[:], bkC[:, 0:256]),
                                 deps=[tdw, tok_W])
                    fC = tok_W
                fNT = tdw
                fMs = tok_Ms
                if DEBUG and c == 0:
                    E.op("gp", lambda g: g.dma_start(dbg["d_W"], Wsol[:]), deps=[tok_W])
                    E.op("gp", lambda g: g.dma_start(dbg["d_EMT"], EMT[:]), deps=[t_emt])
                    E.op("gp", lambda g: g.dma_start(dbg["d_NT"], NTs[:]), deps=[t_nt])
                    E.op("gp", lambda g: g.dma_start(dbg["d_exp"], expsb[:]), deps=[t_en, t_e])
                if SCAN_PART == "solve":
                    fO = tok_W
                    ow_tok[par] = tok_W
                    continue

                t_d = E.op("dv", lambda v, c=c: v.tensor_scalar(d2buf[:, 0:64], Wsol[:, 0:64],
                    d2_t[:, c:c + 1], None, op0=ALU.mult), deps=[tok_W, t_d2t, fd2])
                t_d = E.op("dv", lambda v, c=c: v.tensor_scalar(d2buf[:, 64:128],
                    Wsol[:, 128:192], d2_t[:, 128 + c:129 + c], None, op0=ALU.mult),
                    deps=[t_d])
                t_d = E.op("dv", lambda v, c=c, csl=csl: v.tensor_scalar(d2buf[:, 128:192],
                    Ktok[:, csl][:, 0:64], d2_t[:, c:c + 1], None, op0=ALU.mult), deps=[t_d])
                t_d = E.op("dv", lambda v, c=c, csl=csl: v.tensor_scalar(d2buf[:, 192:256],
                    Ktok[:, csl][:, 64:128], d2_t[:, 128 + c:129 + c], None, op0=ALU.mult),
                    deps=[t_d])
                tG2 = None
                for h in range(2):
                    tG2 = E.op("pe", lambda t, h=h: t.matmul(
                        bkE[h * 64:(h + 1) * 64, 0:64], Wsol[:, h * 128 + 64:h * 128 + 128],
                        d2buf[:, 128 + h * 64:128 + (h + 1) * 64], start=True, stop=True),
                        deps=[t_d, fE] if h == 0 else (), inc=(h == 1))
                tZq = None
                for h in range(2):
                    tZq = E.op("pe", lambda t, h=h: t.matmul(
                        bkE[h * 64:(h + 1) * 64, 64:192], Wsol[:, h * 128 + 64:h * 128 + 128],
                        EMT[:, h * 128:(h + 1) * 128], start=True, stop=True),
                        deps=[t_emt] if h == 0 else (), inc=(h == 1))

                t_mc = E.op("dv", lambda v, c=c: v.scalar_tensor_tensor(Mch[:], i64two[:],
                    gC2[:, c:c + 1], bkE[:, 0:64], op0=ALU.mult, op1=ALU.subtract),
                    deps=[tG2, tZq, t_gc1, t_gc2, t_i64b, fmc])
                t_p = E.op("dv", lambda v, csl=csl: v.tensor_mul(ptmp[0:64, :],
                    Qch[0:64, csl], gbsb[0:64, 0:128]), deps=[t_gb, fpt])
                t_p = E.op("dv", lambda v, csl=csl: v.tensor_mul(ptmp[64:128, :],
                    Qch[64:128, csl], gbsb[64:128, 128:256]), deps=[t_p])
                t_pt = E.op("dv", lambda v: v.tensor_sub(PTs[:], ptmp[:], bkE[:, 64:192]),
                            deps=[t_p, tZq])
                fgb = t_pt
                fE = t_pt
                if SCAN_PART == "epi1":
                    fO = t_pt
                    ow_tok[par] = t_pt
                    fF = [t_mc]
                    fd2 = tG1
                    fmc = t_mc
                    continue
                tch = None
                tG1 = None
                for h in range(2):
                    hsl = slice(h * 64, (h + 1) * 64)
                    tG1 = E.op("pe", lambda t, h=h, hsl=hsl, csl=csl: t.matmul(
                        bkF[hsl, 0:64], Ktok[:, csl][:, h * 64:(h + 1) * 64],
                        d2buf[:, h * 64:(h + 1) * 64], start=True, stop=False),
                        deps=[fF, t_mc, tok_ST] if h == 0 else (), inc=False)
                    tch = E.op("pe", lambda t, hsl=hsl: t.matmul(
                        bkF[hsl, 0:64], Mch[hsl, :], STf[hsl, :], start=False, stop=True),
                        deps=(), inc=(h == 1), rows=(h * 64, 64))
                fmc = tch
                fd2 = tch
                if DEBUG and c == 0:
                    E.op("gp", lambda g: g.dma_start(dbg["d_STb"], STb[:]), deps=[tok_STb, t_pt])
                    E.op("gp", lambda g: g.dma_start(dbg["d_PT"], PTs[:]), deps=[t_pt])
                tout = None
                for h in range(2):
                    hsl = slice(h * 64, (h + 1) * 64)
                    tout = E.op("pe", lambda t, h=h, hsl=hsl: t.matmul(
                        pO[hsl, 0:128], STb[hsl, :], PTs[hsl, :],
                        start=True, stop=False),
                        deps=[t_pt, tok_STb, fO] if h == 0 else (), inc=False,
                        rows=(h * 64, 64))
                    tout = E.op("pe", lambda t, h=h, hsl=hsl: t.matmul(
                        pO[hsl, 0:128], Wsol[:, h * 128:h * 128 + 64],
                        EMT[:, h * 128:(h + 1) * 128], start=False, stop=True),
                        deps=(), inc=(h == 1))
                fEMT = tout
                fW = tout
                fpt = tout
                obuf = obuf0 if par == 0 else obuf1
                t_oev = E.op("dv", lambda v, obuf=obuf: v.tensor_copy(obuf[:],
                    pO[:, 0:128]), deps=[tout, ow_tok[par]])
                rdest = c // 16
                ow_tok[par] = E.op("gp", lambda g, c=c, rdest=rdest, obuf=obuf: g.dma_start(
                    bass.AP(agO_in, rdest * 128 * TPC + (c % 16) * 128,
                            [[TPC, 128], [1, 128]]), obuf[:]), deps=[t_oev], sem="owq")
                if DEBUG:
                    ow_tok[par] = E.op("gp", lambda g, csl=csl, obuf=obuf: g.dma_start(
                        dbg["d_out"][:, csl], obuf[:]), deps=[t_oev], sem="owq")
                tok_ST = E.op("dv", lambda v: v.tensor_copy(STf[:], bkF[:, 0:64]),
                              deps=[tch, tout])
                tok_STb = E.op("ac", lambda a: a.activation(STb[:], bkF[:, 0:64], AF.Copy),
                               deps=[tch, tout])
                fF = [tok_ST, tok_STb]
                fO = t_oev
                rb_free[par] = tBC

            def a2a(g):
                with allow_ici_all_to_all():
                    return g.collective_compute(
                        "AllToAll", ALU.bypass, replica_groups=[list(range(NCORES))],
                        ins=[agO_in.ap().opt()], outs=[agO_out.ap().opt()])
            if PHASE == "p2x":
                ag3 = None
            else:
                ag3 = E.cc(a2a, deps=[ow_tok[0], ow_tok[1]])
            E.flush()

        E.sep_fn = None
        if PHASE in ("p2", "p2x"):
            _dummy_out(nc, E, es, y_out)
            return nc

        with ExitStack() as es4:
            sb4 = lambda n, shape, dt=F32: es4.enter_context(nc.sbuf_tensor(n, shape, dt))

            x2Thi = sb4("x2Thi", [128, 8 * TPC], BF16)
            comb = sb4("comb", [128, 128], F32)
            tg_last = None

            with ExitStack() as es4b:
                sb4b = lambda n, shape, dt=F32: es4b.enter_context(nc.sbuf_tensor(n, shape, dt))
                psf4 = lambda n, shape: es4b.enter_context(nc.psum_tensor(n, shape, F32))
                psb4 = lambda n, shape: es4b.enter_context(nc.psum_tensor(n, shape, BF16))
                OUTSl = sb4b("OUTSl", [128, 8 * TPC], BF16)
                x2Tlo = sb4b("x2Tlo", [128, 8 * TPC], BF16)
                wob = sb4b("wob", [128, 8 * 1024], BF16)
                normed = sb4b("normed", [128, 4096], BF16)
                sqb = sb4b("sqb", [128, 512], BF16)
                rstd4 = sb4b("rstd4", [1, 512], F32)
                xrow = sb4b("xrow", [128, 1024], F32)
                x2t4 = sb4b("x2t4", [128, 1024], F32)
                x2bf = sb4b("x2bf", [128, 1024], BF16)
                x2lo = sb4b("x2lo", [128, 1024], BF16)
                lnc1 = sb4b("lnc1", [128, 1], F32)
                lnc2 = sb4b("lnc2", [128, 1], F32)
                lnsq = sb4b("lnsq", [128, 1024], F32)
                gwf = sb4b("gwf", [128, 64], F32)
                gwhi = sb4b("gwhi", [128, 64], BF16)
                gwlo = sb4b("gwlo", [128, 64], BF16)
                gbb = sb4b("gbb", [128, 8], F32)
                glog = sb4b("glog", [128, 8], F32)
                gsc = sb4b("gsc", [128, 8], F32)
                gtmp = sb4b("gtmp", [128, 8], F32)
                gm = sb4b("gm", [128, 1], F32)
                gs2 = sb4b("gs2", [128, 1], F32)
                gwsb = sb4b("gwsb", [8, 1024], F32)

                pM1 = psf4("pM1", [128, 1024])
                pR4 = psf4("pR4", [128, 512])
                pB4 = psf4("pB4", [128, 512])
                pG4 = psf4("pG4", [128, 512])
                pT4 = psb4("pT4", [128, 1024])
                pT5 = psb4("pT5", [128, 1024])

                t_og = []
                for r in range(NCORES):
                    t_og.append(E.op("sp", lambda s, r=r: s.dma_start(
                        OUTSl[:, r * TPC:(r + 1) * TPC],
                        bass.AP(agO_out, r * 128 * TPC, [[TPC, 128], [1, TPC]])),
                        deps=[ag3], sem="oq"))
                t_wo = E.op("gp", lambda g: g.dma_start(wob[:],
                    wo.rearrange("(b p) c -> p b c", p=128)), sem="wobq")
                t_gw = E.op("gp", lambda g: g.dma_start(gwsb[:], gate_w), sem="g4q")
                t_gt = None
                for mb in range(8):
                    t_gt = E.op("pe", lambda t, mb=mb: t.transpose(
                        pG4[:, mb * 8:(mb + 1) * 8], gwsb[0:8, mb * 128:(mb + 1) * 128],
                        identf[0:8, 0:8]), deps=[t_gw] if mb == 0 else (), inc=(mb == 7))
                t_gwf = E.op("dv", lambda v: v.tensor_copy(gwf[:], pG4[:, 0:64]), deps=[t_gt])
                t_ghi = E.op("dv", lambda v: v.tensor_copy(gwhi[:], gwf[:]), deps=[t_gwf])
                t_glo = E.op("dv", lambda v: v.scalar_tensor_tensor(gwlo[:], gwhi[:], -1.0,
                    gwf[:], op0=ALU.mult, op1=ALU.add), deps=[t_ghi])
                t_gbb0 = E.op("pe", lambda t: t.matmul(pG4[:, 64:72], ones1f[:], gbrow[0:1, :],
                    start=True, stop=True), deps=[t_gt, sm_last], rows=(0, 1))
                t_gbbc = E.op("dv", lambda v: v.tensor_copy(gbb[:], pG4[:, 64:72]),
                              deps=[t_gbb0])

                fR4 = fB4 = None
                fM = [None, None]
                fth1 = fxbf = fxlo = fg3 = None
                fT4 = fT5 = None
                fnorm = None
                fx2 = None
                fglog = None
                frstd = None
                fsqb = None
                x2w_toks = []

                for tt in range(4):
                    tred = None
                    for mb in range(8):
                        tqq = E.op("ac", lambda a, mb=mb, tt=tt: a.activation(sqb[:],
                            OUTSl[:, mb * TPC + tt * 512: mb * TPC + (tt + 1) * 512],
                            AF.Square), deps=[t_og[-1], fsqb])
                        tred = E.op("pe", lambda t, mb=mb: t.matmul(pR4[0:1, 0:512],
                            onesvb[:], sqb[:], start=(mb == 0), stop=(mb == 7)),
                            deps=[tqq, fR4] if mb == 0 else [tqq], inc=True)
                        fsqb = tred
                    if DEBUG and tt == 0:
                        dsq = sb4b("dsq", [1, 512], F32)
                        tdsq = E.op("dv", lambda v: v.tensor_copy(dsq[0:1, :], pR4[0:1, 0:512]),
                                    deps=[tred])
                        E.op("gp", lambda g: g.dma_start(dbg["d_sqs"], dsq[0:1, :]), deps=[tdsq])
                    tln = E.op("ac", lambda a: a.activation(rstd4[0:1, :], pR4[0:1, 0:512],
                        AF.Ln, scale=1.0 / D, bias=ceps6[0:1, 0:1]), deps=[tred, frstd, (tdsq if (DEBUG and tt == 0) else None)])
                    tex = E.op("ac", lambda a: a.activation(rstd4[0:1, :], rstd4[0:1, :],
                        AF.Exp, scale=-0.5), deps=[tln])
                    if DEBUG and tt == 0:
                        E.op("gp", lambda g: g.dma_start(dbg["d_rstd"], rstd4[0:1, :]), deps=[tex])
                    fR4 = [tex, (tdsq if (DEBUG and tt == 0) else None)]
                    tbc = E.op("pe", lambda t: t.matmul(pB4[:, 0:512], ones1f[:],
                        rstd4[0:1, :], start=True, stop=True), deps=[tex, fB4])
                    frstd = tbc
                    tnm = None
                    for mb in range(8):
                        tnm = E.op("dv", lambda v, mb=mb, tt=tt: v.scalar_tensor_tensor(
                            normed[:, mb * 512:(mb + 1) * 512],
                            OUTSl[:, mb * TPC + tt * 512: mb * TPC + (tt + 1) * 512],
                            onw_c[:, mb:mb + 1], pB4[:, 0:512], op0=ALU.mult, op1=ALU.mult),
                            deps=[tbc, fnorm] if mb == 0 else (), inc=(mb == 7))
                    fB4 = tnm

                    for ts_i in range(4):
                        gt = tt * 4 + ts_i
                        tmix = None
                        for ch in range(2):
                            for mb in range(8):
                                tmix = E.op("pe", lambda t, ch=ch, mb=mb, ts_i=ts_i:
                                    t.matmul(pM1[:, ch * 512:(ch + 1) * 512],
                                             normed[:, mb * 512 + ts_i * 128: mb * 512 + (ts_i + 1) * 128],
                                             wob[:, mb * 1024 + ch * 512: mb * 1024 + (ch + 1) * 512],
                                             start=(mb == 0), stop=(mb == 7)),
                                    deps=[tnm, t_wo, fM[ch]] if mb == 0 else (),
                                    inc=(mb == 7))
                                fmix = tmix
                        fnorm = tmix
                        txl = E.op("sp", lambda s, gt=gt: s.dma_start(xrow[:],
                            x_rows[gt * 128:(gt + 1) * 128, :]), deps=[fx2], sem="xrq")
                        ta = E.op("dv", lambda v: v.tensor_add(x2t4[:, 0:512], xrow[:, 0:512],
                            pM1[:, 0:512]), deps=[tmix, txl, fth1])
                        fM[0] = ta
                        ta = E.op("dv", lambda v: v.tensor_add(x2t4[:, 512:1024],
                            xrow[:, 512:1024], pM1[:, 512:1024]), deps=[ta])
                        fM[1] = ta
                        u1 = E.op("dv", lambda v: v.tensor_reduce(lnc1[:, 0:1], x2t4[:],
                            axis=mybir.AxisListType.X, op=ALU.add, negate=True), deps=[ta])
                        u2 = E.op("dv", lambda v: v.tensor_scalar_mul(lnc1[:, 0:1],
                            lnc1[:, 0:1], 1.0 / D), deps=[u1])
                        u3 = E.op("dv", lambda v: v.tensor_scalar(x2t4[:], x2t4[:],
                            lnc1[:, 0:1], None, op0=ALU.add), deps=[u2])
                        u4 = E.op("ac", lambda a: a.activation(lnsq[:], x2t4[:], AF.Square),
                                  deps=[u3])
                        u5 = E.op("dv", lambda v: v.tensor_reduce(lnc2[:, 0:1], lnsq[:],
                            axis=mybir.AxisListType.X, op=ALU.add), deps=[u4])
                        u6 = E.op("ac", lambda a: a.activation(lnc2[:, 0:1], lnc2[:, 0:1],
                            AF.Ln, scale=1.0 / D, bias=ceps5[:, 0:1]), deps=[u5])
                        u7 = E.op("ac", lambda a: a.activation(lnc2[:, 0:1], lnc2[:, 0:1],
                            AF.Exp, scale=-0.5), deps=[u6])
                        u8 = E.op("dv", lambda v: v.tensor_scalar(x2t4[:], x2t4[:],
                            lnc2[:, 0:1], None, op0=ALU.mult), deps=[u7])
                        fx2 = E.op("sp", lambda s, gt=gt: s.dma_start(
                            x2_scr.ap()[gt * 128:(gt + 1) * 128, :], x2t4[:]), deps=[u8],
                            sem="x2wq")
                        x2w_toks.append(fx2)
                        th1 = E.op("ac", lambda a: a.activation(x2bf[:], x2t4[:], AF.Copy),
                                   deps=[u8, fxbf])
                        th2 = E.op("dv", lambda v: v.scalar_tensor_tensor(x2lo[:], x2bf[:],
                            -1.0, x2t4[:], op0=ALU.mult, op1=ALU.add), deps=[th1, fxlo])
                        tq1 = None
                        for mb in range(8):
                            tq1 = E.op("pe", lambda t, mb=mb: t.transpose(
                                pT4[:, mb * 128:(mb + 1) * 128],
                                x2bf[:, mb * 128:(mb + 1) * 128], identb[:]),
                                deps=[th1, fT4] if mb == 0 else (), inc=(mb == 7))
                        fT4 = E.op("dv", lambda v, gt=gt: v.tensor_copy(
                            bass.AP(x2Thi, gt * 128, [[8 * TPC, 128], [TPC, 8], [1, 128]]),
                            pT4[:, 0:1024]), deps=[tq1])
                        hi_ev = fT4
                        fth1 = th1
                        fxbf = tq1
                        tq2 = None
                        for mb in range(8):
                            tq2 = E.op("pe", lambda t, mb=mb: t.transpose(
                                pT5[:, mb * 128:(mb + 1) * 128],
                                x2lo[:, mb * 128:(mb + 1) * 128], identb[:]),
                                deps=[th2, fT5] if mb == 0 else (), inc=(mb == 7))
                        fT5 = E.op("dv", lambda v, gt=gt: v.tensor_copy(
                            bass.AP(x2Tlo, gt * 128, [[8 * TPC, 128], [TPC, 8], [1, 128]]),
                            pT5[:, 0:1024]), deps=[tq2])
                        lo_ev = fT5
                        fxlo = tq2
                        tgl = None
                        for mb in range(8):
                            tgl = E.op("pe", lambda t, mb=mb, gt=gt: t.matmul(
                                pG4[:, 128:136],
                                bass.AP(x2Thi, mb * TPC + gt * 128, [[8 * TPC, 128], [1, 128]]),
                                gwhi[:, mb * 8:(mb + 1) * 8], start=(mb == 0), stop=False),
                                deps=[hi_ev, t_ghi, t_gbbc, fglog] if mb == 0 else (), inc=False)
                        for mb in range(8):
                            tgl = E.op("pe", lambda t, mb=mb, gt=gt: t.matmul(
                                pG4[:, 128:136],
                                bass.AP(x2Thi, mb * TPC + gt * 128, [[8 * TPC, 128], [1, 128]]),
                                gwlo[:, mb * 8:(mb + 1) * 8], start=False, stop=False),
                                deps=[t_glo] if mb == 0 else (), inc=False)
                        for mb in range(8):
                            tgl = E.op("pe", lambda t, mb=mb, gt=gt: t.matmul(
                                pG4[:, 128:136],
                                bass.AP(x2Tlo, mb * TPC + gt * 128, [[8 * TPC, 128], [1, 128]]),
                                gwhi[:, mb * 8:(mb + 1) * 8], start=False, stop=(mb == 7)),
                                deps=[lo_ev] if mb == 0 else (), inc=(mb == 7))
                        tg1 = E.op("dv", lambda v: v.tensor_copy(glog[:], pG4[:, 128:136]),
                                   deps=[tgl, fg3])
                        fglog = tg1
                        tg2 = E.op("dv", lambda v: v.tensor_reduce(gm[:, 0:1], glog[:],
                            axis=mybir.AxisListType.X, op=ALU.max, negate=True), deps=[tg1])
                        tg3 = E.op("ac", lambda a: a.activation(gsc[:], glog[:], AF.Exp,
                            bias=gm[:, 0:1]), deps=[tg2])
                        fg3 = tg3
                        tg4 = E.op("dv", lambda v: v.tensor_reduce(gm[:, 0:1], gsc[:],
                            axis=mybir.AxisListType.X, op=ALU.add), deps=[tg3])
                        tg5 = E.op("dv", lambda v: v.reciprocal(gs2[:, 0:1], gm[:, 0:1]),
                                   deps=[tg4])
                        tg6 = E.op("dv", lambda v: v.tensor_scalar(gsc[:], gsc[:],
                            gs2[:, 0:1], None, op0=ALU.mult), deps=[tg5])
                        tg7 = E.op("dv", lambda v: v.tensor_add(gtmp[:], gsc[:], gbb[:]),
                                   deps=[tg6])
                        tg8 = E.op("dv", lambda v: v.tensor_reduce(gm[:, 0:1], gtmp[:],
                            axis=mybir.AxisListType.X, op=ALU.max), deps=[tg7])
                        tg9 = E.op("dv", lambda v: v.tensor_scalar(glog[:], gtmp[:],
                            gm[:, 0:1], None, op0=ALU.is_ge), deps=[tg8])
                        tg10 = E.op("dv", lambda v: v.scalar_tensor_tensor(glog[:], glog[:],
                            NEG, gtmp[:], op0=ALU.mult, op1=ALU.add), deps=[tg9])
                        tg11 = E.op("dv", lambda v: v.tensor_reduce(gm[:, 0:1], glog[:],
                            axis=mybir.AxisListType.X, op=ALU.max), deps=[tg10])
                        tg12 = E.op("dv", lambda v: v.tensor_scalar(gtmp[:], gtmp[:],
                            gm[:, 0:1], None, op0=ALU.is_ge), deps=[tg11])
                        tg13 = E.op("dv", lambda v, gt=gt: v.tensor_mul(
                            comb[:, gt * 8:(gt + 1) * 8], gtmp[:], gsc[:]), deps=[tg12])
                        tg_last = tg13
                if DEBUG:
                    E.op("gp", lambda g: g.dma_start(dbg["d_x2"], x2_scr.ap()),
                         deps=[x2w_toks[-1]])
                    E.op("gp", lambda g: g.dma_start(dbg["d_comb"], comb[:]), deps=[tg_last])
                E.flush()

            with ExitStack() as es5:
                sb5 = lambda n, shape, dt=F32: es5.enter_context(nc.sbuf_tensor(n, shape, dt))
                psf5 = lambda n, shape: es5.enter_context(nc.psum_tensor(n, shape, F32))
                yacc = sb5("yacc", [128, 16 * D], F32)
                es5b = es5.enter_context(ExitStack())
                sb5b = lambda n, shape, dt=F32: es5b.enter_context(nc.sbuf_tensor(n, shape, dt))
                h_e = sb5b("h_e", [128, 16 * TPC], BF16)
                w1b0 = sb5b("w1b0", [128, 1024], BF16)
                w3b0 = sb5b("w3b0", [128, 1024], BF16)
                w2full = sb5b("w2full", [128, 16 * 1024], BF16)
                silu_t = sb5b("silu_t", [128, 512], BF16)
                ytmp0 = sb5b("ytmp0", [128, 512], F32)
                ytmp1 = sb5b("ytmp1", [128, 512], F32)
                pH1a = psf5("pH1a", [128, 512])
                pH1b = psf5("pH1b", [128, 512])
                pH3a = psf5("pH3a", [128, 512])
                pH3b = psf5("pH3b", [128, 512])
                pY1 = psf5("pY1", [128, 512])
                pY2 = psf5("pY2", [128, 512])

                def w1_off(e, which):
                    return (e * 3 * PIECE if e < E_EXP else base_sh) + which * PIECE

                def w2_off(e):
                    return (e * 3 * PIECE if e < E_EXP else base_sh) + 2 * PIECE

                experts = [E_EXP] + list(range(E_EXP))
                w1_rd = {}
                w3_rd = {}
                w1_ld = {}
                w3_ld = {}
                prev_y_last = None
                y_free = [None, None]
                fsilu = None
                fytmp = [None, None]
                yt = {}
                ph_free = {"1a": None, "1b": None, "3a": None, "3b": None}

                for ei, e in enumerate(experts):
                    w1b = w1b0
                    w3b = w3b0
                    w2tk = []
                    for ib in range(16):
                        rk = ib // 2
                        w2tk.append(E.op("sp", lambda s, e=e, ib=ib, rk=rk: s.dma_start(
                            w2full[:, ib * 1024:(ib + 1) * 1024],
                            bass.AP(agW_out, rk * WCHUNK + w2_off(e) + (ib % 2) * 128 * 1024,
                                    [[1024, 128], [1, 1024]])),
                            deps=[ag2, prev_y_last] if ib == 0 else (), sem="w2q"))
                    hA_last = None
                    for ib in range(16):
                        prev1 = w1_rd.get((ei, ib - 1), w1_rd.get((ei - 1, 15)))
                        prev3 = w3_rd.get((ei, ib - 1), w3_rd.get((ei - 1, 15)))
                        tl1 = E.op("sp", lambda s, e=e, ib=ib, w1b=w1b: s.dma_start(
                            bass.AP(w1b, 0, [[1024, 128], [128, 8], [1, 128]]),
                            bass.AP(agW_out, w1_off(e, 0) + ib * 128,
                                    [[2048, 128], [WCHUNK, 8], [1, 128]])),
                            deps=[ag2, prev1], sem="w1q")
                        w1_ld[(ei, ib)] = tl1
                        tl3 = E.op("sp", lambda s, e=e, ib=ib, w3b=w3b: s.dma_start(
                            bass.AP(w3b, 0, [[1024, 128], [128, 8], [1, 128]]),
                            bass.AP(agW_out, w1_off(e, 1) + ib * 128,
                                    [[2048, 128], [WCHUNK, 8], [1, 128]])),
                            deps=[prev3], sem="w3q")
                        w3_ld[(ei, ib)] = tl3
                        for tg in range(4):
                            pH1 = pH1a if tg % 2 == 0 else pH1b
                            pH3 = pH3a if tg % 2 == 0 else pH3b
                            k1 = "1a" if tg % 2 == 0 else "1b"
                            k3 = "3a" if tg % 2 == 0 else "3b"
                            tm1 = None
                            for mb in range(8):
                                tm1 = E.op("pe", lambda t, mb=mb, tg=tg, pH1=pH1, w1b=w1b:
                                    t.matmul(pH1[:, 0:512], w1b[:, mb * 128:(mb + 1) * 128],
                                             x2Thi[:, mb * TPC + tg * 512: mb * TPC + (tg + 1) * 512],
                                             start=(mb == 0), stop=(mb == 7)),
                                    deps=[tl1, ph_free[k1], prev_y_last, hi_ev] if mb == 0 else (),
                                    inc=(mb == 7))
                            w1_rd[(ei, ib)] = tm1
                            tm3 = None
                            for mb in range(8):
                                tm3 = E.op("pe", lambda t, mb=mb, tg=tg, pH3=pH3, w3b=w3b:
                                    t.matmul(pH3[:, 0:512], w3b[:, mb * 128:(mb + 1) * 128],
                                             x2Thi[:, mb * TPC + tg * 512: mb * TPC + (tg + 1) * 512],
                                             start=(mb == 0), stop=(mb == 7)),
                                    deps=[tl3, ph_free[k3]] if mb == 0 else (), inc=(mb == 7))
                            w3_rd[(ei, ib)] = tm3
                            tsl = E.op("ac", lambda a, pH1=pH1: a.activation(silu_t[:],
                                pH1[:, 0:512], AF.Silu), deps=[tm1, fsilu])
                            ph_free[k1] = tsl
                            tmu = E.op("dv", lambda v, ib=ib, tg=tg, pH3=pH3: v.tensor_mul(
                                h_e[:, ib * TPC + tg * 512: ib * TPC + (tg + 1) * 512],
                                silu_t[:], pH3[:, 0:512]), deps=[tsl, tm3])
                            fsilu = tmu
                            ph_free[k3] = tmu
                            yt[(ib, tg)] = tmu
                            hA_last = tmu
                    for ts_i in range(16):
                        for ch in range(2):
                            par = (ts_i * 2 + ch) % 2
                            pY = pY1 if par == 0 else pY2
                            ty = None
                            for ib in range(16):
                                ty = E.op("pe", lambda t, ib=ib, ts_i=ts_i, ch=ch, pY=pY:
                                    t.matmul(pY[:, 0:512],
                                             h_e[:, ib * TPC + ts_i * 128:ib * TPC + (ts_i + 1) * 128],
                                             w2full[:, ib * 1024 + ch * 512: ib * 1024 + (ch + 1) * 512],
                                             start=(ib == 0), stop=(ib == 15)),
                                    deps=[yt[(ib, ts_i // 4)], w2tk[-1], y_free[par]]
                                         if ib == 0 else [yt[(ib, ts_i // 4)]],
                                    inc=(ib == 15))
                            if e == E_EXP:
                                tya = E.op("ac", lambda a, ts_i=ts_i, ch=ch, pY=pY:
                                    a.activation(
                                        yacc[:, ts_i * D + ch * 512: ts_i * D + (ch + 1) * 512],
                                        pY[:, 0:512], AF.Copy), deps=[ty])
                                y_free[par] = tya
                                yt[("acc", ts_i, ch)] = tya
                            else:
                                ytmp = ytmp0 if par == 0 else ytmp1
                                tya = E.op("ac", lambda a, ts_i=ts_i, ch=ch, pY=pY, e=e, ytmp=ytmp:
                                    a.activation(ytmp[:], pY[:, 0:512], AF.Copy,
                                                 scale=comb[:, ts_i * 8 + e: ts_i * 8 + e + 1]),
                                    deps=[ty, tg_last, fytmp[par]])
                                y_free[par] = tya
                                tyb = E.op("dv", lambda v, ts_i=ts_i, ch=ch, ytmp=ytmp: v.tensor_add(
                                    yacc[:, ts_i * D + ch * 512: ts_i * D + (ch + 1) * 512],
                                    yacc[:, ts_i * D + ch * 512: ts_i * D + (ch + 1) * 512],
                                    ytmp[:]), deps=[tya, yt[("acc", ts_i, ch)]])
                                fytmp[par] = tyb
                                yt[("acc", ts_i, ch)] = tyb
                            prev_y_last = ty
                E.flush()
                es5b.close()

                x2r = sb5("x2r", [128, 1024], F32)
                o6 = sb5("o6", [128, 1024], F32)
                l6a = sb5("l6a", [128, 1], F32)
                l6b = sb5("l6b", [128, 1], F32)
                sq6 = sb5("sq6", [128, 1024], F32)
                fx2r = None
                out_toks = []
                for ts_i in range(16):
                    tx = E.op("sp", lambda s, ts_i=ts_i: s.dma_start(x2r[:],
                        x2_scr.ap()[ts_i * 128:(ts_i + 1) * 128, :]),
                        deps=[x2w_toks[-1], fx2r], sem="x2q")
                    tadd = E.op("dv", lambda v, ts_i=ts_i: v.tensor_add(o6[:], x2r[:],
                        yacc[:, ts_i * D:(ts_i + 1) * D]),
                        deps=[tx, yt[("acc", ts_i, 0)], yt[("acc", ts_i, 1)]])
                    u1 = E.op("dv", lambda v: v.tensor_reduce(l6a[:, 0:1], o6[:],
                        axis=mybir.AxisListType.X, op=ALU.add, negate=True), deps=[tadd])
                    u2 = E.op("dv", lambda v: v.tensor_scalar_mul(l6a[:, 0:1], l6a[:, 0:1],
                        1.0 / D), deps=[u1])
                    u3 = E.op("dv", lambda v: v.tensor_scalar(o6[:], o6[:], l6a[:, 0:1],
                        None, op0=ALU.add), deps=[u2])
                    u4 = E.op("ac", lambda a: a.activation(sq6[:], o6[:], AF.Square), deps=[u3])
                    u5 = E.op("dv", lambda v: v.tensor_reduce(l6b[:, 0:1], sq6[:],
                        axis=mybir.AxisListType.X, op=ALU.add), deps=[u4])
                    u6 = E.op("ac", lambda a: a.activation(l6b[:, 0:1], l6b[:, 0:1], AF.Ln,
                        scale=1.0 / D, bias=ceps5[:, 0:1]), deps=[u5])
                    u7 = E.op("ac", lambda a: a.activation(l6b[:, 0:1], l6b[:, 0:1], AF.Exp,
                        scale=-0.5), deps=[u6])
                    u8 = E.op("dv", lambda v: v.tensor_scalar(o6[:], o6[:], l6b[:, 0:1],
                        None, op0=ALU.mult), deps=[u7])
                    tout6 = E.op("sp", lambda s, ts_i=ts_i: s.dma_start(
                        y_out[ts_i * 128:(ts_i + 1) * 128, :], o6[:]), deps=[u8], sem="yq")
                    out_toks.append(tout6)
                    fx2r = tout6

                fin = out_toks[-1]
                E.raw("gp", lambda g, fin=fin: g.wait_ge(E.sem["yq"], fin[1]))
                E.raw("sp", lambda s, fin=fin: s.wait_ge(E.sem["yq"], fin[1]))
                E.flush()

    return nc


def kernel(**inputs):
    x = np.asarray(inputs["L0_attn_x"])[0]
    get = lambda n: np.asarray(inputs[n])
    in_maps = []
    for c in range(NCORES):
        hs = slice(2 * c, 2 * c + 2)
        m = {
            "x_rows": np.ascontiguousarray(x[c * TPC:(c + 1) * TPC]),
            "wq_s": np.ascontiguousarray(get("L0_attn_Wq")[:, c * 128:(c + 1) * 128]),
            "wk_s": np.ascontiguousarray(get("L0_attn_Wk")[:, c * 128:(c + 1) * 128]),
            "wv_s": np.ascontiguousarray(get("L0_attn_Wv")[:, c * 128:(c + 1) * 128]),
            "wrq_s": np.ascontiguousarray(get("L0_attn_Wrq")[:, c * 96:(c + 1) * 96]),
            "wrk_s": np.ascontiguousarray(get("L0_attn_Wrk")[:, c * 96:(c + 1) * 96]),
            "convq_s": np.ascontiguousarray(get("L0_attn_conv_q")[c * 128:(c + 1) * 128]),
            "convk_s": np.ascontiguousarray(get("L0_attn_conv_k")[c * 128:(c + 1) * 128]),
            "convv_s": np.ascontiguousarray(get("L0_attn_conv_v")[c * 128:(c + 1) * 128]),
            "wf_s": np.ascontiguousarray(get("L0_attn_Wf")[:, hs]),
            "wl_s": np.ascontiguousarray(get("L0_attn_Wl")[:, hs]),
            "bfbl": np.ascontiguousarray(np.concatenate(
                [get("L0_attn_bf")[hs], get("L0_attn_bl")[hs]])),
            "gs_s": np.ascontiguousarray(get("L0_attn_graph_scale")[hs]),
            "qn_w": np.ascontiguousarray(get("L0_attn_qn_w")),
            "kn_w": np.ascontiguousarray(get("L0_attn_kn_w")),
            "on_w": np.ascontiguousarray(get("L0_attn_on_w")),
            "wo": np.ascontiguousarray(get("L0_attn_Wo")),
            "gate_w": np.ascontiguousarray(get("L0_ffn_gate_w")),
            "gate_b": np.ascontiguousarray(get("L0_ffn_gate_bias")),
            "w1p": np.ascontiguousarray(get("L0_ffn_w1")[:, :, c * 128:(c + 1) * 128]),
            "w3p": np.ascontiguousarray(get("L0_ffn_w3")[:, :, c * 128:(c + 1) * 128]),
            "w2p": np.ascontiguousarray(get("L0_ffn_w2")[:, :, c * 256:(c + 1) * 256]),
            "sw1p": np.ascontiguousarray(get("L0_ffn_sw1")[:, c * 128:(c + 1) * 128]),
            "sw3p": np.ascontiguousarray(get("L0_ffn_sw3")[:, c * 128:(c + 1) * 128]),
            "sw2p": np.ascontiguousarray(get("L0_ffn_sw2")[:, c * 256:(c + 1) * 256]),
        }
        in_maps.append(m)
    nc = build_program()
    res = run_bass_kernel_spmd(nc, in_maps, core_ids=list(range(NCORES)))
    out = np.concatenate([np.asarray(res.results[c]["y"]) for c in range(NCORES)], axis=0)
    return out.reshape(1, S, D).astype(np.float32)




# version 08
# speedup vs baseline: 1.0052x; 1.0050x over previous
import numpy as np
import ml_dtypes
import concourse.bass as bass
import concourse.mybir as mybir
from concourse.bass_utils import run_bass_kernel_spmd
from concourse.aot_env import allow_ici_all_to_all
from contextlib import ExitStack

F32 = mybir.dt.float32
BF16 = mybir.dt.bfloat16
BF = ml_dtypes.bfloat16
AF = mybir.ActivationFunctionType
ALU = mybir.AluOpType

NCORES = 8
S, D, H, dH = 16384, 1024, 16, 64
C = 128
NCH = S // C
TPC = S // NCORES
INTER, E_EXP = 2048, 8
NEG = -1.0e30
PIECE = 128 * 2048
WCHUNK = 27 * PIECE

DEBUG = False
PHASE = "all"
RACE_DETECT = False
NCH_RUN = NCH
SCAN_PART = "all"


def _dummy_out(nc, E, es, y_out):
    with nc.sbuf_tensor("dummy0", [128, 1024], F32) as dz:
        tz = E.gpc(lambda g: g.memset(dz[:], 0.0))
        tl = None
        for i in range(16):
            tl = E.op("gp", lambda g, i=i: g.dma_start(
                y_out[i * 128:(i + 1) * 128, :], dz[:]), deps=[tz, tl])
        E.raw("gp", lambda g, tl=tl: g.wait_ge(E.sem["gp"], tl[1]))
        E.flush()


class Emit:
    ENGS = ("pe", "dv", "ac", "sp", "gp")

    def __init__(self, nc, es):
        self.nc = nc
        self.q = {k: [] for k in self.ENGS}
        self.sem = {k: es.enter_context(nc.semaphore(f"s_{k}")) for k in self.ENGS}
        self.sem["cc"] = es.enter_context(nc.semaphore("s_cc"))
        self.sem["gpc"] = es.enter_context(nc.semaphore("s_gpc"))
        self.cnt = {k: 0 for k in list(self.ENGS) + ["cc", "gpc"]}
        self.waited = {k: {} for k in self.ENGS}
        self.sep_fn = None
        self._last_rows = None

    def _deps(self, eng, deps):
        wl = []
        for d in deps:
            if d is None:
                continue
            if isinstance(d, list):
                for dd in d:
                    if dd is None:
                        continue
                    dsem, dval = dd
                    if dval > self.waited[eng].get(dsem, -1):
                        self.waited[eng][dsem] = dval
                        wl.append((dsem, dval))
            else:
                dsem, dval = d
                if dval > self.waited[eng].get(dsem, -1):
                    self.waited[eng][dsem] = dval
                    wl.append((dsem, dval))
        return tuple(wl)

    def addsem(self, es, nc, name):
        self.sem[name] = es.enter_context(nc.semaphore("s_" + name))
        self.cnt[name] = 0

    def op(self, eng, fn, deps=(), inc=True, rows=None, sem=None):
        step = 16 if eng in ("sp", "gp") else 1
        if eng == "pe":
            r = rows if rows is not None else (0, 128)
            last = getattr(self, "_last_rows", None)
            if (r != (0, 128) and last is not None and last != (0, 128)
                    and last != r and self.sep_fn is not None):
                sfn = self.sep_fn
                self.q["pe"].append(lambda e, sfn=sfn: sfn(e))
            self._last_rows = r
        wl = self._deps(eng, deps)
        tok = None
        key = sem if sem is not None else eng
        if inc:
            self.cnt[key] += step
            tok = (key, self.cnt[key])
        semmap = self.sem

        def run(e, fn=fn, wl=wl, inc=inc, step=step, key=key):
            for dsem, dval in wl:
                e.wait_ge(semmap[dsem], dval)
            inst = fn(e)
            if inc:
                inst.then_inc(semmap[key], step)
        self.q[eng].append(run)
        return tok

    def gpc(self, fn, deps=()):
        deps = list(deps) + ([("gpc", self.cnt["gpc"])] if self.cnt["gpc"] > 0 else [])
        wl = self._deps("gp", deps)
        self.cnt["gpc"] += 1
        tok = ("gpc", self.cnt["gpc"])
        semmap = self.sem

        def run(g, fn=fn, wl=wl):
            for dsem, dval in wl:
                g.wait_ge(semmap[dsem], dval)
            fn(g).then_inc(semmap["gpc"], 1)
        self.q["gp"].append(run)
        return tok

    def cc(self, fn, deps=()):
        wl = self._deps("gp", deps)
        self.cnt["cc"] += 1
        tok = ("cc", self.cnt["cc"])
        semmap = self.sem

        def run(g, fn=fn, wl=wl):
            for dsem, dval in wl:
                g.wait_ge(semmap[dsem], dval)
            fn(g).then_inc(semmap["cc"], 1)
        self.q["gp"].append(run)
        return tok

    def raw(self, eng, fn):
        self.q[eng].append(fn)

    def flush(self):
        nc = self.nc
        q = self.q
        with nc.Block() as blk:
            if q["pe"]:
                @blk.tensor
                def _(t, ops=tuple(q["pe"])):
                    for f in ops:
                        f(t)
            if q["dv"]:
                @blk.vector
                def _(v, ops=tuple(q["dv"])):
                    for f in ops:
                        f(v)
            if q["ac"]:
                @blk.scalar
                def _(a, ops=tuple(q["ac"])):
                    for f in ops:
                        f(a)
            if q["sp"]:
                @blk.sync
                def _(s, ops=tuple(q["sp"])):
                    for f in ops:
                        f(s)
            if q["gp"]:
                @blk.gpsimd
                def _(g, ops=tuple(q["gp"])):
                    for f in ops:
                        f(g)
        self.q = {k: [] for k in self.ENGS}


def build_program():
    nc = bass.Bass("TRN2", target_bir_lowering=False, detect_race_conditions=RACE_DETECT)

    def din(name, shape):
        return nc.dram_tensor(name, shape, F32, kind="ExternalInput").ap()

    x_rows = din("x_rows", [TPC, D])
    wq_s = din("wq_s", [D, 128])
    wk_s = din("wk_s", [D, 128])
    wv_s = din("wv_s", [D, 128])
    wrq_s = din("wrq_s", [D, 96])
    wrk_s = din("wrk_s", [D, 96])
    convq_s = din("convq_s", [128, 4])
    convk_s = din("convk_s", [128, 4])
    convv_s = din("convv_s", [128, 4])
    wf_s = din("wf_s", [D, 2])
    wl_s = din("wl_s", [D, 2])
    bfbl = din("bfbl", [4])
    gs_s = din("gs_s", [2])
    qn_w = din("qn_w", [dH])
    kn_w = din("kn_w", [dH])
    on_w = din("on_w", [D])
    wo = din("wo", [D, D])
    gate_w = din("gate_w", [E_EXP, D])
    gate_b = din("gate_b", [E_EXP])
    w1p = din("w1p", [E_EXP, INTER, 128])
    w3p = din("w3p", [E_EXP, INTER, 128])
    w2p = din("w2p", [E_EXP, D, 256])
    sw1p = din("sw1p", [INTER, 128])
    sw3p = din("sw3p", [INTER, 128])
    sw2p = din("sw2p", [D, 256])

    y_out = nc.dram_tensor("y", [TPC, D], F32, kind="ExternalOutput").ap()
    dbg = {}
    if DEBUG:
        for nm, shp in [("d_Qch", [128, S]), ("d_Kch", [128, S]), ("d_Ktok", [128, S]),
                        ("d_Vtok", [128, S]), ("d_lcum", [128, 256]), ("d_out", [128, S]),
                        ("d_x2", [TPC, D]), ("d_comb", [128, 128]), ("d_bt", [128, 256]),
                        ("d_zpk", [128, 256]), ("d_W", [128, 256]), ("d_EMT", [128, 256]),
                        ("d_NT", [128, 256]), ("d_exp", [128, 512]),
                        ("d_STb", [128, 64]), ("d_PT", [128, 128]), ("d_rstd", [1, 512]),
                        ("d_sqs", [1, 512])]:
            dt = F32 if nm in ("d_lcum", "d_x2", "d_comb", "d_bt", "d_zpk") else BF16
            dbg[nm] = nc.dram_tensor(nm, shp, dt, kind="ExternalOutput").ap()

    agX_in = nc.dram_tensor("agX_in", [D, TPC], BF16)
    agX_out = nc.dram_tensor("agX_out", [NCORES * D, TPC], BF16)
    agW_in = nc.dram_tensor("agW_in", [WCHUNK], BF16)
    agW_out = nc.dram_tensor("agW_out", [NCORES * WCHUNK], BF16)
    agO_in = nc.dram_tensor("agO_in", [NCORES, 128, TPC], BF16)
    agO_out = nc.dram_tensor("agO_out", [NCORES, 128, TPC], BF16)
    x2_scr = nc.dram_tensor("x2_scr", [TPC, D], F32)
    z_hbm = nc.dram_tensor("z_hbm", [4, S], F32)

    es = ExitStack()
    with es:
        E = Emit(nc, es)
        for _s in ("smq", "agxq", "wldq", "wldq1", "wstq", "wstq1", "p1wq", "xq0", "xq1", "zq",
                   "owq", "oq", "wobq", "g4q", "xrq", "x2wq", "x2q", "w1q",
                   "w3q", "w2q", "yq", "xbq") + tuple(f"rbq{i}" for i in range(8)):
            E.addsem(es, nc, _s)
        sb = lambda n, shape, dt=F32: es.enter_context(nc.sbuf_tensor(n, shape, dt))

        identb = sb("identb", [128, 128], BF16)
        identf = sb("identf", [128, 128], F32)
        ones1f = sb("ones1f", [1, 128], F32)
        onesvb = sb("onesvb", [128, 1], BF16)
        qnw_c = sb("qnw_c", [128, 1], F32)
        knw_c = sb("knw_c", [128, 1], F32)
        lam_c = sb("lam_c", [128, 1], F32)
        onw_c = sb("onw_c", [128, 8], F32)
        cvq_c = sb("cvq_c", [128, 4], F32)
        cvk_c = sb("cvk_c", [128, 4], F32)
        cvv_c = sb("cvv_c", [128, 4], F32)
        bfbl4 = sb("bfbl4", [4, 1], F32)
        gsbuf = sb("gsbuf", [2, 1], F32)
        absgs = sb("absgs", [2, 1], F32)
        gbrow = sb("gbrow", [1, 8], F32)
        ceps6 = sb("ceps6", [128, 1], F32)
        ceps24 = sb("ceps24", [128, 1], F32)
        ceps5 = sb("ceps5", [128, 1], F32)

        cst = None
        for fn in [
            lambda g: g.memset(identb[:], 0.0),
            lambda g: g.affine_select(out=identb[:], in_=identb[:], compare_op=ALU.not_equal,
                                      fill=1.0, base=0, pattern=[[-1, 128]], channel_multiplier=1),
            lambda g: g.memset(identf[:], 0.0),
            lambda g: g.affine_select(out=identf[:], in_=identf[:], compare_op=ALU.not_equal,
                                      fill=1.0, base=0, pattern=[[-1, 128]], channel_multiplier=1),
            lambda g: g.memset(ones1f[:], 1.0),
            lambda g: g.memset(onesvb[:], 1.0),
            lambda g: g.memset(ceps6[:], 1e-6),
            lambda g: g.memset(ceps24[:], 1e-24),
            lambda g: g.memset(ceps5[:], 1e-5),
        ]:
            cst = E.gpc(fn)

        sm = []
        sm.append(E.op("gp", lambda g: g.dma_start(qnw_c[0:64, 0:1],
            qn_w.rearrange("(a b) -> a b", b=1), allow_slow_non_contiguous=True), sem="smq"))
        sm.append(E.op("gp", lambda g: g.dma_start(qnw_c[64:128, 0:1],
            qn_w.rearrange("(a b) -> a b", b=1), allow_slow_non_contiguous=True), sem="smq"))
        sm.append(E.op("gp", lambda g: g.dma_start(knw_c[0:64, 0:1],
            kn_w.rearrange("(a b) -> a b", b=1), allow_slow_non_contiguous=True), sem="smq"))
        sm.append(E.op("gp", lambda g: g.dma_start(knw_c[64:128, 0:1],
            kn_w.rearrange("(a b) -> a b", b=1), allow_slow_non_contiguous=True), sem="smq"))
        sm.append(E.op("gp", lambda g: g.dma_start(onw_c[:],
            on_w.rearrange("(b p) -> p b", p=128), allow_slow_non_contiguous=True), sem="smq"))
        sm.append(E.op("gp", lambda g: g.dma_start(cvq_c[:], convq_s), sem="smq"))
        sm.append(E.op("gp", lambda g: g.dma_start(cvk_c[:], convk_s), sem="smq"))
        sm.append(E.op("gp", lambda g: g.dma_start(cvv_c[:], convv_s), sem="smq"))
        sm.append(E.op("gp", lambda g: g.dma_start(bfbl4[0:4, 0:1],
            bfbl.rearrange("(a b) -> a b", b=1), allow_slow_non_contiguous=True), sem="smq"))
        sm.append(E.op("gp", lambda g: g.dma_start(gsbuf[:],
            gs_s.rearrange("(a b) -> a b", b=1), allow_slow_non_contiguous=True), sem="smq"))
        sm.append(E.op("gp", lambda g: g.dma_start(gbrow[0:1, :],
            gate_b.rearrange("(b a) -> b a", b=1)), sem="smq"))
        sm_last = sm[-1]
        t_ag = E.op("ac", lambda a: a.activation(absgs[:], gsbuf[:], AF.Abs), deps=[sm[9]])

        with ExitStack() as es0:
            sb0 = lambda n, shape, dt=BF16: es0.enter_context(nc.sbuf_tensor(n, shape, dt))
            xbf = sb0("xbf", [128, S])
            xts = sb0("xts", [128, S])
            wtbs = [sb0("wtb0", [128, 2048]), sb0("wtb1", [128, 2048])]
            wtss = [sb0("wts0", [128, 2048]), sb0("wts1", [128, 2048])]
            psx = es0.enter_context(nc.psum_tensor("psx", [128, 1024], BF16))
            psw = es0.enter_context(nc.psum_tensor("psw", [128, 1024], BF16))

            t_xld = E.op("gp", lambda g: g.dma_start(xbf[:],
                x_rows.rearrange("(g p) m -> p g m", p=128)), sem="xbq")
            ev_last = None
            ev_all = []
            for g_i in range(16):
                tt = None
                for mb in range(8):
                    tt = E.op("pe", lambda t, g_i=g_i, mb=mb: t.transpose(
                        psx[:, mb * 128:(mb + 1) * 128],
                        xbf[:, g_i * 1024 + mb * 128: g_i * 1024 + (mb + 1) * 128],
                        identb[:]),
                        deps=[t_xld, cst, ev_last] if mb == 0 else (), inc=(mb == 7))
                ev_last = E.op("dv", lambda v, g_i=g_i: v.tensor_copy(
                    bass.AP(xts, g_i * 128, [[S, 128], [2048, 8], [1, 128]]),
                    psx[:, 0:1024]), deps=[tt])
                ev_all.append(ev_last)
            t_agxw = E.op("gp", lambda g: g.dma_start(
                agX_in.ap().rearrange("(b p) t -> p b t", p=128), xts[:]), deps=[ev_last],
                sem="agxq")
            ag1 = E.cc(lambda g: g.collective_compute(
                "AllGather", ALU.bypass, replica_groups=[list(range(NCORES))],
                ins=[agX_in.ap().opt()], outs=[agX_out.ap().opt()]), deps=[t_agxw])

            pieces = []
            for e in range(E_EXP):
                pieces.append((w1p[e, :, :], 16, e * 3 * PIECE))
                pieces.append((w3p[e, :, :], 16, e * 3 * PIECE + PIECE))
                pieces.append((w2p[e, :, 0:128], 8, e * 3 * PIECE + 2 * PIECE))
                pieces.append((w2p[e, :, 128:256], 8, e * 3 * PIECE + 2 * PIECE + 128 * 1024))
            base_sh = E_EXP * 3 * PIECE
            pieces.append((sw1p[:, :], 16, base_sh))
            pieces.append((sw3p[:, :], 16, base_sh + PIECE))
            pieces.append((sw2p[:, 0:128], 8, base_sh + 2 * PIECE))
            pieces.append((sw2p[:, 128:256], 8, base_sh + 2 * PIECE + 128 * 1024))

            prev_out = [None, None]
            prev_tt = [None, None]
            prev_wev = [None, None]
            last_wev = None
            for pidx, (src, nblk, coff) in enumerate(pieces):
                wpar = pidx % 2
                wtb = wtbs[wpar]
                wts = wtss[wpar]
                t_ld = E.op("gp", lambda g, src=src, nblk=nblk, wtb=wtb: g.dma_start(
                    wtb[:, 0:nblk * 128], src.rearrange("(k p) m -> p k m", p=128)),
                    deps=[prev_out[wpar], prev_tt[wpar]],
                    sem=("wldq" if wpar == 0 else "wldq1"))
                evs = []
                for half in range((nblk + 7) // 8):
                    kk0 = half * 8
                    kn = min(8, nblk - kk0)
                    tt = None
                    for k in range(kn):
                        tt = E.op("pe", lambda t, k=k, kk0=kk0, wtb=wtb: t.transpose(
                            psw[:, k * 128:(k + 1) * 128],
                            wtb[:, (kk0 + k) * 128:(kk0 + k + 1) * 128], identb[:]),
                            deps=[t_ld, cst, last_wev] if k == 0 else (), inc=(k == kn - 1))
                    last_wev = E.op("dv", lambda v, kk0=kk0, kn=kn, wts=wts: v.tensor_copy(
                        wts[:, kk0 * 128:(kk0 + kn) * 128], psw[:, 0:kn * 128]),
                        deps=[tt, prev_out[wpar]])
                    evs.append(last_wev)
                prev_tt[wpar] = tt
                prev_wev[wpar] = last_wev
                prev_out[wpar] = E.op("gp", lambda g, nblk=nblk, coff=coff, wts=wts: g.dma_start(
                    bass.AP(agW_in, coff, [[nblk * 128, 128], [1, nblk * 128]]),
                    wts[:, 0:nblk * 128]), deps=evs,
                    sem=("wstq" if wpar == 0 else "wstq1"))
            ag2 = None
            ag2 = E.cc(lambda g: g.collective_compute(
                "AllGather", ALU.bypass, replica_groups=[list(range(NCORES))],
                ins=[agW_in.ap().opt()], outs=[agW_out.ap().opt()]),
                deps=[prev_out[0], prev_out[1]])

            E.flush()

        if PHASE == "p0":
            _dummy_out(nc, E, es, y_out)
            return nc

        with ExitStack() as es1:
            sb1 = lambda n, shape, dt=F32: es1.enter_context(nc.sbuf_tensor(n, shape, dt))
            psf1 = lambda n, shape: es1.enter_context(nc.psum_tensor(n, shape, F32))
            psb1 = lambda n, shape: es1.enter_context(nc.psum_tensor(n, shape, BF16))

            Kch = sb1("Kch", [128, S], BF16)
            Qch = sb1("Qch", [128, S], BF16)
            Ktok = sb1("Ktok", [128, S], BF16)
            Vtok = sb1("Vtok", [128, S], BF16)
            obuf0 = sb1("obuf0", [128, 128], BF16)
            obuf1 = sb1("obuf1", [128, 128], BF16)
            xt0 = sb1("xt0", [128, 4096], BF16)
            xt1 = sb1("xt1", [128, 4096], BF16)
            wqb8 = sb1("wqb8", [128, 1024], BF16)
            wkb8 = sb1("wkb8", [128, 1024], BF16)
            wvb8 = sb1("wvb8", [128, 1024], BF16)
            wrq8 = sb1("wrq8", [128, 8 * 48], BF16)
            wfl8 = sb1("wfl8", [128, 8 * 4], BF16)
            wrq8b = sb1("wrq8b", [128, 8 * 48], BF16)
            wrk8 = sb1("wrk8", [128, 8 * 48], BF16)
            wrk8b = sb1("wrk8b", [128, 8 * 48], BF16)
            qw = sb1("qw", [128, 516], BF16)
            kw = sb1("kw", [128, 516], BF16)
            vw = sb1("vw", [128, 516], BF16)
            cq = sb1("cq", [128, 512], BF16)
            ck = sb1("ck", [128, 512], BF16)
            cv = sb1("cv", [128, 512], BF16)
            ktmp = sb1("ktmp", [128, 512], BF16)
            rqsb = sb1("rqsb", [128, 512], BF16)
            rksb = sb1("rksb", [128, 512], BF16)
            q2t = sb1("q2t", [128, 512], BF16)
            k2t = sb1("k2t", [128, 512], BF16)
            flst = sb1("flst", [4, 512], F32)
            rr1 = sb1("rr1", [2, 512], F32)
            rr2 = sb1("rr2", [2, 512], F32)
            rr3 = sb1("rr3", [2, 512], F32)

            triu_f = sb1("triu_f", [128, 128], F32)
            inclm = sb1("inclm", [128, 256], F32)
            strictm = sb1("strictm", [128, 256], F32)
            bdones = sb1("bdones", [128, 2], BF16)
            self2f = sb1("self2f", [2, 128], F32)
            sel48 = sb1("sel48", [2, 128], F32)
            i64two = sb1("i64two", [128, 64], F32)

            pP1 = psf1("pP1", [128, 512])
            pP2 = psf1("pP2", [128, 512])
            pRED = psf1("pRED", [128, 512])
            pBC1 = psf1("pBC1", [128, 512])
            pBC2 = psf1("pBC2", [128, 512])
            pT1 = psb1("pT1", [128, 1024])
            pFL = psf1("pFL", [128, 512])

            twq = E.op("gp", lambda g: g.dma_start(wqb8[:],
                wq_s.rearrange("(b p) c -> p b c", p=128)), sem="p1wq")
            twk = E.op("gp", lambda g: g.dma_start(wkb8[:],
                wk_s.rearrange("(b p) c -> p b c", p=128)), sem="p1wq")
            twv = E.op("gp", lambda g: g.dma_start(wvb8[:],
                wv_s.rearrange("(b p) c -> p b c", p=128)), sem="p1wq")
            tw1 = E.op("gp", lambda g: g.dma_start(
                bass.AP(wrq8, 0, [[8 * 48, 128], [48, 8], [1, 48]]),
                wrq_s.rearrange("(b p) c -> p b c", p=128)[:, :, 0:48]), sem="p1wq")
            tw2 = E.op("gp", lambda g: g.dma_start(
                bass.AP(wfl8, 0, [[8 * 4, 128], [4, 8], [1, 2]]),
                wf_s.rearrange("(b p) c -> p b c", p=128)), sem="p1wq")
            tw3 = E.op("gp", lambda g: g.dma_start(
                bass.AP(wfl8, 2, [[8 * 4, 128], [4, 8], [1, 2]]),
                wl_s.rearrange("(b p) c -> p b c", p=128)), sem="p1wq")
            tw4 = E.op("gp", lambda g: g.dma_start(
                bass.AP(wrq8b, 0, [[8 * 48, 128], [48, 8], [1, 48]]),
                wrq_s.rearrange("(b p) c -> p b c", p=128)[:, :, 48:96]), sem="p1wq")
            tw5 = E.op("gp", lambda g: g.dma_start(
                bass.AP(wrk8, 0, [[8 * 48, 128], [48, 8], [1, 48]]),
                wrk_s.rearrange("(b p) c -> p b c", p=128)[:, :, 0:48]), sem="p1wq")
            tw6 = E.op("gp", lambda g: g.dma_start(
                bass.AP(wrk8b, 0, [[8 * 48, 128], [48, 8], [1, 48]]),
                wrk_s.rearrange("(b p) c -> p b c", p=128)[:, :, 48:96]), sem="p1wq")

            cst1 = None
            for fn in [
                lambda g: g.memset(triu_f[:], 0.0),
                lambda g: g.affine_select(out=triu_f[:], in_=triu_f[:], compare_op=ALU.is_gt,
                                          fill=1.0, base=0, pattern=[[-1, 128]], channel_multiplier=1),
                lambda g: g.memset(inclm[:], 0.0),
                lambda g: g.affine_select(out=inclm[:, 0:128], in_=inclm[:, 0:128], compare_op=ALU.is_ge,
                                          fill=NEG, base=0, pattern=[[1, 128]], channel_multiplier=-1),
                lambda g: g.affine_select(out=inclm[:, 128:256], in_=inclm[:, 128:256], compare_op=ALU.is_ge,
                                          fill=NEG, base=0, pattern=[[1, 128]], channel_multiplier=-1),
                lambda g: g.memset(strictm[:], 0.0),
                lambda g: g.affine_select(out=strictm[:, 0:128], in_=strictm[:, 0:128], compare_op=ALU.is_gt,
                                          fill=NEG, base=0, pattern=[[1, 128]], channel_multiplier=-1),
                lambda g: g.affine_select(out=strictm[:, 128:256], in_=strictm[:, 128:256], compare_op=ALU.is_gt,
                                          fill=NEG, base=0, pattern=[[1, 128]], channel_multiplier=-1),
                lambda g: g.memset(bdones[:], 0.0),
                lambda g: g.memset(bdones[0:64, 0:1], 1.0),
                lambda g: g.memset(bdones[64:128, 1:2], 1.0),
                lambda g: g.memset(self2f[:], 1.0),
                lambda g: g.affine_select(out=self2f[:], in_=self2f[:], compare_op=ALU.is_ge,
                                          fill=0.0, base=0, pattern=[[1, 128]], channel_multiplier=-64),
                lambda g: g.affine_select(out=self2f[:], in_=self2f[:], compare_op=ALU.is_ge,
                                          fill=0.0, base=63, pattern=[[-1, 128]], channel_multiplier=64),
                lambda g: g.memset(sel48[:], 1.0),
                lambda g: g.affine_select(out=sel48[:], in_=sel48[:], compare_op=ALU.is_ge,
                                          fill=0.0, base=0, pattern=[[1, 128]], channel_multiplier=-64),
                lambda g: g.affine_select(out=sel48[:], in_=sel48[:], compare_op=ALU.is_ge,
                                          fill=0.0, base=47, pattern=[[-1, 128]], channel_multiplier=64),
            ]:
                cst1 = E.gpc(fn)
            t_i64a = E.op("dv", lambda v: v.tensor_add(i64two[0:64, :], identf[0:64, 0:64],
                identf[0:64, 64:128]), deps=[cst1])
            t_i64b = E.op("dv", lambda v: v.tensor_add(i64two[64:128, :], identf[64:128, 0:64],
                identf[64:128, 64:128]), deps=[t_i64a])
            t_lam0 = E.op("pe", lambda t: t.matmul(pRED[:, 0:1], sel48[:], absgs[:],
                start=True, stop=True), deps=[t_ag, cst1])
            t_lam = E.op("dv", lambda v: v.tensor_copy(lam_c[:], pRED[:, 0:1]), deps=[t_lam0])

            h0 = E.gpc(lambda g: g.memset(qw[:, 0:3], 0.0))
            h1 = E.gpc(lambda g: g.memset(kw[:, 0:3], 0.0))
            h2 = E.gpc(lambda g: g.memset(vw[:, 0:3], 0.0))

            xtok = [None, None]
            xt_rd = [None, None]
            halo_q, halo_k, halo_v = [h0], [h1], [h2]
            evQp = evKp = evVp = None
            fcq = fck = fcv = None
            fFL = None
            fq2 = fk2 = None
            fktmp = None
            fpRED = t_lam
            bc1_free = bc2_free = None
            trK = trV = None
            zstage = None
            rr_free = [None, None, None]

            for g_i in range(32):
                xt = xt0 if g_i % 2 == 0 else xt1
                t0g = g_i * 512
                rank = t0g // TPC
                toff = t0g % TPC
                gsl = slice(g_i * 512, (g_i + 1) * 512)
                xtok[g_i % 2] = E.op("sp", lambda s, rank=rank, toff=toff, xt=xt:
                    s.dma_start(bass.AP(xt, 0, [[4096, 128], [512, 8], [1, 512]]),
                                bass.AP(agX_out, rank * D * TPC + toff,
                                        [[TPC, 128], [128 * TPC, 8], [1, 512]])),
                    deps=[xtok[g_i % 2], xt_rd[g_i % 2], ag1, tw6],
                    sem=("xq0" if g_i % 2 == 0 else "xq1"))
                t_x = xtok[g_i % 2]

                def proj(bank, wtile, ncols, deps, pstart=True):
                    tk = None
                    for mb in range(8):
                        tk = E.op("pe", lambda t, mb=mb, bank=bank, wtile=wtile, ncols=ncols, xt=xt:
                            t.matmul(bank[0:ncols, 0:512],
                                     wtile[:, mb * ncols:(mb + 1) * ncols],
                                     xt[:, mb * 512:(mb + 1) * 512],
                                     start=(mb == 0 and pstart), stop=(mb == 7)),
                            deps=deps if mb == 0 else (), inc=(mb == 7))
                    return tk

                tQ = proj(pP1, wqb8, 128, [t_x, twq, evQp])
                tK = proj(pP2, wkb8, 128, [t_x, twk, evKp])
                evQ = E.op("ac", lambda a: a.activation(qw[:, 3:515], pP1[:, 0:512], AF.Copy),
                           deps=[tQ, halo_q[-1]])
                tV = proj(pP1, wvb8, 128, [evQ, twv])
                evK = E.op("ac", lambda a: a.activation(kw[:, 3:515], pP2[:, 0:512], AF.Copy),
                           deps=[tK, halo_k[-1]])
                tRa = None
                for mb in range(8):
                    tRa = E.op("pe", lambda t, mb=mb, xt=xt: t.matmul(
                        pP2[0:48, 0:512], wrq8[:, mb * 48:(mb + 1) * 48],
                        xt[:, mb * 512:(mb + 1) * 512], start=(mb == 0), stop=(mb == 7)),
                        deps=[evK, tw6] if mb == 0 else (), inc=False)
                for mb in range(8):
                    tRa = E.op("pe", lambda t, mb=mb, xt=xt: t.matmul(
                        pP2[64:112, 0:512], wrq8b[:, mb * 48:(mb + 1) * 48],
                        xt[:, mb * 512:(mb + 1) * 512], start=(mb == 0), stop=(mb == 7)),
                        deps=[tw6] if mb == 0 else (), inc=(mb == 7))
                tFL = None
                for mb in range(8):
                    tFL = E.op("pe", lambda t, mb=mb, xt=xt: t.matmul(
                        pFL[0:4, 0:512], wfl8[:, mb * 4:(mb + 1) * 4],
                        xt[:, mb * 512:(mb + 1) * 512], start=(mb == 0), stop=(mb == 7)),
                        deps=[tw6, fFL] if mb == 0 else (), inc=(mb == 7))
                evV = E.op("ac", lambda a: a.activation(vw[:, 3:515], pP1[:, 0:512], AF.Copy),
                           deps=[tV, halo_v[-1]])
                evR1 = E.op("dv", lambda v: v.tensor_copy(rqsb[:, 0:512], pP2[:, 0:512]),
                            deps=[tRa])
                evFL = E.op("dv", lambda v: v.tensor_scalar(flst[0:4, 0:512],
                    pFL[0:4, 0:512], bfbl4[0:4, 0:1], None, op0=ALU.add),
                    deps=[tFL, zstage, sm_last])
                fFL = evFL
                zstage = E.op("gp", lambda g, g_i=g_i: g.dma_start(
                    bass.AP(z_hbm, g_i * 512, [[S, 4], [1, 512]]), flst[0:4, :]),
                    deps=[evFL], sem="zq")
                tRk = None
                for mb in range(8):
                    tRk = E.op("pe", lambda t, mb=mb, xt=xt: t.matmul(
                        pP1[0:48, 0:512], wrk8[:, mb * 48:(mb + 1) * 48],
                        xt[:, mb * 512:(mb + 1) * 512], start=(mb == 0), stop=(mb == 7)),
                        deps=[evV, tw6] if mb == 0 else (), inc=False)
                for mb in range(8):
                    tRk = E.op("pe", lambda t, mb=mb, xt=xt: t.matmul(
                        pP1[64:112, 0:512], wrk8b[:, mb * 48:(mb + 1) * 48],
                        xt[:, mb * 512:(mb + 1) * 512], start=(mb == 0), stop=(mb == 7)),
                        deps=[tw6] if mb == 0 else (), inc=(mb == 7))
                evR2 = E.op("dv", lambda v: v.tensor_copy(rksb[:, 0:512], pP1[0:128, 0:512]),
                            deps=[tRk])
                xt_rd[g_i % 2] = tRk
                evQp = evR2
                evKp = [evR1, evFL]

                def conv(dst, src, wcol, dep0, dfree=None):
                    t1 = E.op("dv", lambda v: v.tensor_scalar(dst[:, 0:512], src[:, 0:512],
                        wcol[:, 0:1], None, op0=ALU.mult), deps=[dep0, sm_last, dfree])
                    for j in range(1, 4):
                        t1 = E.op("dv", lambda v, j=j: v.scalar_tensor_tensor(
                            dst[:, 0:512], src[:, j:512 + j], wcol[:, j:j + 1], dst[:, 0:512],
                            op0=ALU.mult, op1=ALU.add), deps=[t1])
                    return t1
                tcq = conv(cq, qw, cvq_c, evQ, fcq)
                tck = conv(ck, kw, cvk_c, evK, fck)
                tcv = conv(cv, vw, cvv_c, evV, fcv)
                halo_q.append(E.op("dv", lambda v: v.tensor_copy(qw[:, 0:3], qw[:, 512:515]),
                                   deps=[tcq]))
                halo_k.append(E.op("dv", lambda v: v.tensor_copy(kw[:, 0:3], kw[:, 512:515]),
                                   deps=[tck]))
                halo_v.append(E.op("dv", lambda v: v.tensor_copy(vw[:, 0:3], vw[:, 512:515]),
                                   deps=[tcv]))

                tq2 = E.op("ac", lambda a: a.activation(q2t[:], cq[:], AF.Square),
                           deps=[tcq, fq2])
                trq = E.op("pe", lambda t: t.matmul(pRED[0:2, 0:512], bdones[:], q2t[:],
                    start=True, stop=True), deps=[tq2, cst1, fpRED])
                tl1 = E.op("ac", lambda a: a.activation(rr1[:], pRED[0:2, 0:512], AF.Ln,
                    scale=1.0 / dH, bias=ceps6[0:2, 0:1]), deps=[trq, rr_free[0]])
                tl2 = E.op("ac", lambda a: a.activation(rr1[:], rr1[:], AF.Exp, scale=-0.5),
                           deps=[tl1])
                tbq = E.op("pe", lambda t: t.matmul(pBC1[:, 0:512], self2f[:], rr1[:],
                    start=True, stop=True), deps=[tl2, bc1_free])
                rr_free[0] = tbq
                tqa = E.op("dv", lambda v, gsl=gsl: v.scalar_tensor_tensor(
                    Qch[:, gsl], cq[:], qnw_c[:, 0:1], pBC1[:, 0:512],
                    op0=ALU.mult, op1=ALU.mult), deps=[tbq])
                tk2 = E.op("ac", lambda a: a.activation(k2t[:], ck[:], AF.Square),
                           deps=[tck, fk2])
                trk = E.op("pe", lambda t: t.matmul(pRED[32:34, 0:512], bdones[:], k2t[:],
                    start=True, stop=True), deps=[tk2, tl1], inc=True)
                tl3 = E.op("ac", lambda a: a.activation(rr2[:], pRED[32:34, 0:512], AF.Ln,
                    scale=1.0 / dH, bias=ceps6[0:2, 0:1]), deps=[trk, rr_free[1]])
                tl4 = E.op("ac", lambda a: a.activation(rr2[:], rr2[:], AF.Exp, scale=-0.5),
                           deps=[tl3])
                tbk = E.op("pe", lambda t: t.matmul(pBC2[:, 0:512], self2f[:], rr2[:],
                    start=True, stop=True), deps=[tl4, bc2_free])
                rr_free[1] = tbk
                tka = E.op("dv", lambda v: v.scalar_tensor_tensor(
                    ktmp[:], ck[:], knw_c[:, 0:1], pBC2[:, 0:512],
                    op0=ALU.mult, op1=ALU.mult), deps=[tbk, fktmp])
                bc2_free = tka
                tqb1 = E.op("dv", lambda v, gsl=gsl: v.scalar_tensor_tensor(
                    Qch[0:48, gsl], rqsb[0:48, :], lam_c[0:48, 0:1], Qch[0:48, gsl],
                    op0=ALU.mult, op1=ALU.add), deps=[tqa, evR1, t_lam])
                tqb2 = E.op("dv", lambda v, gsl=gsl: v.scalar_tensor_tensor(
                    Qch[64:112, gsl], rqsb[64:112, :], lam_c[64:112, 0:1], Qch[64:112, gsl],
                    op0=ALU.mult, op1=ALU.add), deps=[tqb1])
                tkb1 = E.op("dv", lambda v: v.scalar_tensor_tensor(
                    ktmp[0:48, :], rksb[0:48, :], lam_c[0:48, 0:1], ktmp[0:48, :],
                    op0=ALU.mult, op1=ALU.add), deps=[tka, evR2])
                tkb2 = E.op("dv", lambda v: v.scalar_tensor_tensor(
                    ktmp[64:112, :], rksb[64:112, :], lam_c[64:112, 0:1], ktmp[64:112, :],
                    op0=ALU.mult, op1=ALU.add), deps=[tkb1])
                tk2b = E.op("ac", lambda a: a.activation(k2t[:], ktmp[:], AF.Square),
                            deps=[tkb2, trk])
                trk2 = E.op("pe", lambda t: t.matmul(pRED[64:66, 0:512], bdones[:], k2t[:],
                    start=True, stop=True), deps=[tk2b, tl3], inc=True)
                tl5 = E.op("ac", lambda a: a.activation(rr3[:], pRED[64:66, 0:512], AF.Ln,
                    scale=1.0, bias=ceps24[0:2, 0:1]), deps=[trk2, rr_free[2]])
                tl6 = E.op("ac", lambda a: a.activation(rr3[:], rr3[:], AF.Exp, scale=-0.5),
                           deps=[tl5])
                tbk2 = E.op("pe", lambda t: t.matmul(pBC1[:, 0:512], self2f[:], rr3[:],
                    start=True, stop=True), deps=[tl6, tqa])
                rr_free[2] = tbk2
                tkn = E.op("dv", lambda v, gsl=gsl: v.tensor_mul(Kch[:, gsl], ktmp[:],
                    pBC1[:, 0:512]), deps=[tbk2])
                bc1_free = tkn
                ttk = None
                for j in range(4):
                    ttk = E.op("pe", lambda t, j=j, g_i=g_i: t.transpose(
                        pT1[:, j * 128:(j + 1) * 128],
                        Kch[:, g_i * 512 + j * 128: g_i * 512 + (j + 1) * 128], identb[:]),
                        deps=[tkn, trK, trV] if j == 0 else (), inc=False)
                for j in range(4):
                    ttk = E.op("pe", lambda t, j=j: t.transpose(
                        pT1[:, 512 + j * 128: 512 + (j + 1) * 128],
                        cv[:, j * 128:(j + 1) * 128], identb[:]),
                        deps=[tcv] if j == 0 else (), inc=(j == 3))
                trK = E.op("dv", lambda v, gsl=gsl: v.tensor_copy(Ktok[:, gsl], pT1[:, 0:512]),
                           deps=[ttk])
                trV = E.op("dv", lambda v, gsl=gsl: v.tensor_copy(Vtok[:, gsl],
                    pT1[:, 512:1024]), deps=[ttk])
                fcq = tq2
                fck = tk2
                fcv = ttk
                fq2 = trq
                fk2 = trk2
                fktmp = tk2b
                fpRED = tl5

            spc = sb1("spc", [128, 256], F32)
            spzlc = sb1("spzlc", [128, 256], F32)
            sp_t = sb1("sp_t", [128, 256], F32)
            spzl_t = sb1("spzl_t", [128, 256], F32)
            lcum_t = sb1("lcum_t", [128, 256], F32)
            lcum_c = sb1("lcum_c", [128, 256], F32)
            npos_c = sb1("npos_c", [128, 256], F32)
            b_t = sb1("b_t", [128, 256], F32)
            c2_t = sb1("c2_t", [128, 256], F32)
            d2_t = sb1("d2_t", [128, 256], F32)
            gC2 = sb1("gC2", [128, 128], F32)
            tmp_t = sb1("tmp_t", [128, 256], F32)
            rbufL = sb1("rbufL", [1, 256], F32)

            tr_sp = []
            for r, dst, hcol in ((0, spc, 0), (1, spc, 1), (2, spzlc, 0), (3, spzlc, 1)):
                tr_sp.append(E.op("gp", lambda g, r=r, dst=dst, hcol=hcol: g.dma_start(
                    dst[:, hcol * 128:(hcol + 1) * 128],
                    bass.AP(z_hbm, r * S, [[128, 128], [1, 128]])),
                    deps=[zstage]))
            t_spa = E.op("ac", lambda a: a.activation(spc[:], spc[:], AF.Exp, scale=-1.0),
                         deps=tr_sp)
            t_spb = E.op("ac", lambda a: a.activation(spc[:], spc[:], AF.Ln, bias=1.0),
                         deps=[t_spa])
            t_spc2 = E.op("ac", lambda a: a.activation(spzlc[:], spzlc[:], AF.Exp, scale=-1.0),
                          deps=[t_spb])
            t_sp2 = E.op("ac", lambda a: a.activation(spzlc[:], spzlc[:], AF.Ln, bias=1.0),
                         deps=[t_spc2])
            tt1 = E.op("pe", lambda t: t.transpose(pBC1[:, 0:128], spc[:, 0:128], identf[:]),
                       deps=[t_spb, bc1_free, tkn], inc=False)
            tt2 = E.op("pe", lambda t: t.transpose(pBC1[:, 128:256], spc[:, 128:256], identf[:]),
                       inc=True)
            t_spt = E.op("dv", lambda v: v.tensor_copy(sp_t[:], pBC1[:, 0:256]), deps=[tt2])
            tt3 = E.op("pe", lambda t: t.transpose(pBC1[:, 256:384], spzlc[:, 0:128], identf[:]),
                       deps=[t_spt, t_sp2], inc=False)
            tt4 = E.op("pe", lambda t: t.transpose(pBC1[:, 384:512], spzlc[:, 128:256], identf[:]),
                       inc=True)
            t_szt = E.op("dv", lambda v: v.tensor_copy(spzl_t[:], pBC1[:, 256:512]), deps=[tt4])
            t_lc0 = E.op("pe", lambda t: t.matmul(pBC2[:, 0:256], triu_f[:], sp_t[:],
                start=True, stop=True), deps=[t_spt, bc2_free, cst1])
            t_lct = E.op("dv", lambda v: v.tensor_copy(lcum_t[:], pBC2[:, 0:256]), deps=[t_lc0])
            tt5 = E.op("pe", lambda t: t.transpose(pBC1[:, 0:128], lcum_t[:, 0:128], identf[:]),
                       deps=[t_lct, t_spt, t_szt], inc=False)
            tt6 = E.op("pe", lambda t: t.transpose(pBC1[:, 128:256], lcum_t[:, 128:256], identf[:]),
                       inc=True)
            t_lcc = E.op("dv", lambda v: v.tensor_copy(lcum_c[:], pBC1[:, 0:256]), deps=[tt6])
            t_np1 = E.op("dv", lambda v: v.tensor_sub(npos_c[:], lcum_c[:], spc[:]), deps=[t_lcc])
            t_np2 = E.op("dv", lambda v: v.tensor_add(npos_c[:], npos_c[:], spzlc[:]),
                         deps=[t_np1])
            t_bt = E.op("ac", lambda a: a.activation(b_t[:], spzl_t[:], AF.Exp, scale=-1.0),
                        deps=[t_szt])
            t_c21 = E.op("dv", lambda v: v.tensor_sub(tmp_t[:], sp_t[:], spzl_t[:]),
                         deps=[t_szt, t_spt])
            t_c22 = E.op("dv", lambda v: v.tensor_sub(tmp_t[:], tmp_t[:], lcum_t[:]),
                         deps=[t_c21, t_lct])
            t_c2t = E.op("ac", lambda a: a.activation(c2_t[:], tmp_t[:], AF.Exp), deps=[t_c22])
            t_rb = E.op("sp", lambda s: s.dma_start(rbufL[0:1, :], lcum_t[127:128, 0:256]),
                        deps=[t_lct])
            t_bcL = E.op("pe", lambda t: t.matmul(pBC2[:, 0:256], ones1f[:], rbufL[0:1, :],
                start=True, stop=True), deps=[t_rb, t_lcc])
            t_d21 = E.op("dv", lambda v: v.tensor_sub(tmp_t[:], lcum_t[:], pBC2[:, 0:256]),
                         deps=[t_bcL, t_c2t])
            t_d2t = E.op("ac", lambda a: a.activation(d2_t[:], tmp_t[:], AF.Exp), deps=[t_d21])
            t_gcb = E.op("ac", lambda a: a.activation(tmp_t[:], pBC2[:, 0:256], AF.Exp,
                scale=-1.0), deps=[t_d21, t_d2t])
            t_gc1 = E.op("dv", lambda v: v.tensor_copy(gC2[0:64, :], tmp_t[0:64, 0:128]),
                         deps=[t_gcb])
            t_gc2 = E.op("dv", lambda v: v.tensor_copy(gC2[64:128, :], tmp_t[64:128, 128:256]),
                         deps=[t_gcb])
            if DEBUG:
                E.op("gp", lambda g: g.dma_start(dbg["d_lcum"], lcum_t[:]), deps=[t_lct])
                E.op("gp", lambda g: g.dma_start(dbg["d_bt"], b_t[:]), deps=[t_bt])
                E.op("gp", lambda g: g.dma_start(dbg["d_zpk"], spc[:]), deps=[t_spb])
                E.op("gp", lambda g: g.dma_start(dbg["d_Qch"], Qch[:]), deps=[tqb2])
                E.op("gp", lambda g: g.dma_start(dbg["d_Kch"], Kch[:]), deps=[tkn])
                E.op("gp", lambda g: g.dma_start(dbg["d_Ktok"], Ktok[:]), deps=[trK])
                E.op("gp", lambda g: g.dma_start(dbg["d_Vtok"], Vtok[:]), deps=[trV])
            E.flush()

            if PHASE == "p1":
                _dummy_out(nc, E, es, y_out)
                return nc

            NTs = sb1("NTs", [128, 256], BF16)
            Ms = sb1("Ms", [128, 256], BF16)
            Wsol = sb1("Wsol", [128, 256], BF16)
            EMT = sb1("EMT", [128, 256], BF16)
            expsb = sb1("expsb", [128, 512], BF16)
            gbsb = sb1("gbsb", [128, 256], BF16)
            argb = sb1("argb", [128, 256], F32)
            argb2 = sb1("argb2", [128, 256], F32)
            d2buf = sb1("d2buf", [128, 256], BF16)
            ptmp = sb1("ptmp", [128, 128], BF16)
            PTs = sb1("PTs", [128, 128], BF16)
            STf = sb1("STf", [128, 64], F32)
            STb = sb1("STb", [128, 64], BF16)
            Mch = sb1("Mch", [128, 64], F32)
            rbAs = [sb1(f"rbA{i}", [1, 256], F32) for i in range(2)]
            rbBs = [sb1(f"rbB{i}", [1, 256], F32) for i in range(2)]

            bkA, bkB, bkC, bkD, bkE = pP1, pP2, pRED, pBC1, pBC2
            bkFO = psf1("bkFO", [128, 512])
            bkF = bkFO
            pO = pFL
            bkT = pT1

            E.sep_fn = lambda t: t.ldweights(identb[:])
            t_s0 = E.gpc(lambda g: g.memset(STf[:], 0.0))
            t_s1 = E.gpc(lambda g: g.memset(STb[:], 0.0))
            tok_ST, tok_STb = t_s0, t_s1

            fA = fB = fC = fD = fE = fF = fO = fT = None
            fNT = fMs = fW = fEMT = fgb = farg = [None]
            farg_e = farg_n = fexp_e = fexp_n = None
            fd2 = fpt = fmc = None
            rb_free = [None] * 2
            ow_tok = [None, None]
            t_oev = None

            for c in range(NCH_RUN):
                par = c % 2
                rbA = rbAs[par]
                rbB = rbBs[par]
                csl = slice(c * 128, (c + 1) * 128)
                if SCAN_PART != "g2":
                    t_rba = E.op("sp", lambda s, c=c, rbA=rbA: s.dma_start(rbA[0:1, :],
                        lcum_c[c:c + 1, 0:256]), deps=[t_lcc, rb_free[par]],
                        sem=f"rbq{par}")
                    t_rbb = E.op("sp", lambda s, c=c, rbB=rbB: s.dma_start(rbB[0:1, :],
                        npos_c[c:c + 1, 0:256]), deps=[t_np2], sem=f"rbq{par}")
                else:
                    t_rba = t_rbb = None
                tG = None
                glist = [] if SCAN_PART == "g3" else [
                    (Kch, Kch, 0), (Kch, Kch, 128), (Kch, Qch, 256), (Kch, Qch, 384)]
                for i, (srcL, srcR, col) in enumerate(glist):
                    h = i % 2
                    hsl = slice(h * 64, (h + 1) * 64)
                    tG = E.op("pe", lambda t, srcL=srcL, srcR=srcR, col=col, hsl=hsl, csl=csl:
                        t.matmul(bkA[:, col:col + 128], srcL[hsl, csl], srcR[hsl, csl],
                                 start=True, stop=True),
                        deps=[fA, tkn, tqb2] if i == 0 else (), inc=(i == 3),
                        rows=(h * 64, 64))
                tBC = None
                for i in range(4 if SCAN_PART != "g2" else 0):
                    h = i % 2
                    rb = rbA if i < 2 else rbB
                    col = i * 128
                    tBC = E.op("pe", lambda t, h=h, rb=rb, col=col: t.matmul(
                        bkB[:, col:col + 128], ones1f[:], rb[0:1, h * 128:(h + 1) * 128],
                        start=True, stop=True),
                        deps=[t_rba, t_rbb, fB] if i == 0 else (), inc=(i == 3),
                        rows=(0, 1))
                if SCAN_PART in ("g", "g2", "g3"):
                    fO = tG if tG else tBC
                    ow_tok[par] = tBC if tBC else tG
                    fA = tG
                    fB = tBC
                    continue
                t_ae = E.op("dv", lambda v: v.tensor_sub(argb[:], inclm[:], bkB[:, 0:256]),
                            deps=[tBC, farg_e])
                t_an = E.op("dv", lambda v: v.tensor_sub(argb2[:], strictm[:], bkB[:, 256:512]),
                            deps=[tBC, farg_n])
                t_e = None
                for h in range(2):
                    t_e = E.op("ac", lambda a, h=h, c=c: a.activation(
                        expsb[:, h * 128:(h + 1) * 128], argb[:, h * 128:(h + 1) * 128],
                        AF.Exp, bias=lcum_t[:, h * 128 + c: h * 128 + c + 1]),
                        deps=[t_ae, fexp_e] if h == 0 else (), inc=(h == 1))
                farg_e = t_e
                t_en = None
                for h in range(2):
                    t_en = E.op("ac", lambda a, h=h, c=c: a.activation(
                        expsb[:, 256 + h * 128:256 + (h + 1) * 128],
                        argb2[:, h * 128:(h + 1) * 128], AF.Exp,
                        bias=lcum_t[:, h * 128 + c: h * 128 + c + 1]),
                        deps=[t_an, fexp_n] if h == 0 else (), inc=(h == 1))
                farg_n = t_en
                t_gb = E.op("ac", lambda a: a.activation(gbsb[:], bkB[:, 0:256], AF.Exp,
                    scale=-1.0), deps=[tBC, fgb])
                fB = [t_ae, t_an, t_gb]
                if SCAN_PART == "arg":
                    fO = t_gb
                    ow_tok[par] = t_gb
                    fA = tG
                    fexp_e = t_e
                    fexp_n = t_en
                    continue
                t_emt = E.op("dv", lambda v: v.tensor_mul(EMT[:], bkA[:, 256:512],
                    expsb[:, 0:256]), deps=[tG, t_e, fEMT])
                fexp_e = t_emt
                t_nt = E.op("dv", lambda v: v.scalar_tensor_tensor(NTs[:], bkA[:, 0:256],
                    -1.0, expsb[:, 256:512], op0=ALU.mult, op1=ALU.mult),
                    deps=[tG, t_en, fNT])
                fexp_n = t_nt
                fA = t_nt
                t_w = E.op("dv", lambda v, c=c, csl=csl: v.tensor_scalar(Wsol[:, 0:64],
                    Vtok[:, csl][:, 0:64], b_t[:, c:c + 1], None, op0=ALU.mult),
                    deps=[t_bt, fW, trV])
                t_w = E.op("dv", lambda v, c=c, csl=csl: v.tensor_scalar(Wsol[:, 128:192],
                    Vtok[:, csl][:, 64:128], b_t[:, 128 + c:129 + c], None, op0=ALU.mult),
                    deps=[t_w])
                t_w = E.op("dv", lambda v, c=c, csl=csl: v.tensor_scalar(Wsol[:, 64:128],
                    Ktok[:, csl][:, 0:64], c2_t[:, c:c + 1], None, op0=ALU.mult),
                    deps=[t_w, t_c2t, trK])
                t_w = E.op("dv", lambda v, c=c, csl=csl: v.tensor_scalar(Wsol[:, 192:256],
                    Ktok[:, csl][:, 64:128], c2_t[:, 128 + c:129 + c], None, op0=ALU.mult),
                    deps=[t_w])
                tok_W = t_w
                if SCAN_PART == "rhs":
                    fO = t_w
                    ow_tok[par] = t_w
                    fW = t_w
                    continue
                t_m0 = None
                for h in range(2):
                    t_m0 = E.op("pe", lambda t, h=h: t.transpose(
                        bkT[:, h * 128:(h + 1) * 128], NTs[:, h * 128:(h + 1) * 128], identb[:]),
                        deps=[t_nt, fT] if h == 0 else (), inc=(h == 1))
                tok_Ms = E.op("dv", lambda v: v.tensor_copy(Ms[:], bkT[:, 0:256]),
                              deps=[t_m0, fMs])
                fT = tok_Ms
                tok_NT = t_nt
                if SCAN_PART == "pro":
                    fO = tok_Ms
                    ow_tok[par] = tok_Ms
                    continue

                tdw = None
                for j in range(7):
                    for h in range(2):
                        tdw = E.op("pe", lambda t, h=h: t.matmul(
                            bkC[:, h * 128:(h + 1) * 128], NTs[:, h * 128:(h + 1) * 128],
                            Wsol[:, h * 128:(h + 1) * 128], start=True, stop=True),
                            deps=[tok_NT, tok_W, fC] if h == 0 else (), inc=(h == 1))
                    if j < 6:
                        tsq = None
                        for h in range(2):
                            tsq = E.op("pe", lambda t, h=h: t.matmul(
                                bkD[:, h * 128:(h + 1) * 128], NTs[:, h * 128:(h + 1) * 128],
                                Ms[:, h * 128:(h + 1) * 128], start=True, stop=True),
                                deps=[tok_Ms, fD] if h == 0 else (), inc=False)
                        for h in range(2):
                            tsq = E.op("pe", lambda t, h=h: t.matmul(
                                bkD[:, 256 + h * 128:256 + (h + 1) * 128],
                                Ms[:, h * 128:(h + 1) * 128], NTs[:, h * 128:(h + 1) * 128],
                                start=True, stop=True), deps=(), inc=(h == 1))
                        tok_Ms = E.op("ac", lambda a: a.activation(Ms[:], bkD[:, 0:256],
                            AF.Copy), deps=[tsq])
                        tok_NT = E.op("ac", lambda a: a.activation(NTs[:], bkD[:, 256:512],
                            AF.Copy), deps=[tsq])
                        fD = tok_NT
                    tok_W = E.op("dv", lambda v: v.tensor_add(Wsol[:], Wsol[:], bkC[:, 0:256]),
                                 deps=[tdw, tok_W])
                    fC = tok_W
                fNT = tdw
                fMs = tok_Ms
                if DEBUG and c == 0:
                    E.op("gp", lambda g: g.dma_start(dbg["d_W"], Wsol[:]), deps=[tok_W])
                    E.op("gp", lambda g: g.dma_start(dbg["d_EMT"], EMT[:]), deps=[t_emt])
                    E.op("gp", lambda g: g.dma_start(dbg["d_NT"], NTs[:]), deps=[t_nt])
                    E.op("gp", lambda g: g.dma_start(dbg["d_exp"], expsb[:]), deps=[t_en, t_e])
                if SCAN_PART == "solve":
                    fO = tok_W
                    ow_tok[par] = tok_W
                    continue

                t_d = E.op("dv", lambda v, c=c: v.tensor_scalar(d2buf[:, 0:64], Wsol[:, 0:64],
                    d2_t[:, c:c + 1], None, op0=ALU.mult), deps=[tok_W, t_d2t, fd2])
                t_d = E.op("dv", lambda v, c=c: v.tensor_scalar(d2buf[:, 64:128],
                    Wsol[:, 128:192], d2_t[:, 128 + c:129 + c], None, op0=ALU.mult),
                    deps=[t_d])
                t_d = E.op("dv", lambda v, c=c, csl=csl: v.tensor_scalar(d2buf[:, 128:192],
                    Ktok[:, csl][:, 0:64], d2_t[:, c:c + 1], None, op0=ALU.mult), deps=[t_d])
                t_d = E.op("dv", lambda v, c=c, csl=csl: v.tensor_scalar(d2buf[:, 192:256],
                    Ktok[:, csl][:, 64:128], d2_t[:, 128 + c:129 + c], None, op0=ALU.mult),
                    deps=[t_d])
                tG2 = None
                for h in range(2):
                    tG2 = E.op("pe", lambda t, h=h: t.matmul(
                        bkE[h * 64:(h + 1) * 64, 0:64], Wsol[:, h * 128 + 64:h * 128 + 128],
                        d2buf[:, 128 + h * 64:128 + (h + 1) * 64], start=True, stop=True),
                        deps=[t_d, fE] if h == 0 else (), inc=(h == 1))
                tZq = None
                for h in range(2):
                    tZq = E.op("pe", lambda t, h=h: t.matmul(
                        bkE[h * 64:(h + 1) * 64, 64:192], Wsol[:, h * 128 + 64:h * 128 + 128],
                        EMT[:, h * 128:(h + 1) * 128], start=True, stop=True),
                        deps=[t_emt] if h == 0 else (), inc=(h == 1))

                t_mc = E.op("dv", lambda v, c=c: v.scalar_tensor_tensor(Mch[:], i64two[:],
                    gC2[:, c:c + 1], bkE[:, 0:64], op0=ALU.mult, op1=ALU.subtract),
                    deps=[tG2, tZq, t_gc1, t_gc2, t_i64b, fmc])
                t_p = E.op("dv", lambda v, csl=csl: v.tensor_mul(ptmp[0:64, :],
                    Qch[0:64, csl], gbsb[0:64, 0:128]), deps=[t_gb, fpt])
                t_p = E.op("dv", lambda v, csl=csl: v.tensor_mul(ptmp[64:128, :],
                    Qch[64:128, csl], gbsb[64:128, 128:256]), deps=[t_p])
                t_pt = E.op("dv", lambda v: v.tensor_sub(PTs[:], ptmp[:], bkE[:, 64:192]),
                            deps=[t_p, tZq])
                fgb = t_pt
                fE = t_pt
                if SCAN_PART == "epi1":
                    fO = t_pt
                    ow_tok[par] = t_pt
                    fF = [t_mc]
                    fd2 = tG1
                    fmc = t_mc
                    continue
                tch = None
                tG1 = None
                for h in range(2):
                    hsl = slice(h * 64, (h + 1) * 64)
                    tG1 = E.op("pe", lambda t, h=h, hsl=hsl, csl=csl: t.matmul(
                        bkF[hsl, 0:64], Ktok[:, csl][:, h * 64:(h + 1) * 64],
                        d2buf[:, h * 64:(h + 1) * 64], start=True, stop=False),
                        deps=[fF, t_mc, tok_ST] if h == 0 else (), inc=False)
                    tch = E.op("pe", lambda t, hsl=hsl: t.matmul(
                        bkF[hsl, 0:64], Mch[hsl, :], STf[hsl, :], start=False, stop=True),
                        deps=(), inc=(h == 1), rows=(h * 64, 64))
                fmc = tch
                fd2 = tch
                if DEBUG and c == 0:
                    E.op("gp", lambda g: g.dma_start(dbg["d_STb"], STb[:]), deps=[tok_STb, t_pt])
                    E.op("gp", lambda g: g.dma_start(dbg["d_PT"], PTs[:]), deps=[t_pt])
                tout = None
                for h in range(2):
                    hsl = slice(h * 64, (h + 1) * 64)
                    tout = E.op("pe", lambda t, h=h, hsl=hsl: t.matmul(
                        pO[hsl, 0:128], STb[hsl, :], PTs[hsl, :],
                        start=True, stop=False),
                        deps=[t_pt, tok_STb, fO] if h == 0 else (), inc=False,
                        rows=(h * 64, 64))
                    tout = E.op("pe", lambda t, h=h, hsl=hsl: t.matmul(
                        pO[hsl, 0:128], Wsol[:, h * 128:h * 128 + 64],
                        EMT[:, h * 128:(h + 1) * 128], start=False, stop=True),
                        deps=(), inc=(h == 1))
                fEMT = tout
                fW = tout
                fpt = tout
                obuf = obuf0 if par == 0 else obuf1
                t_oev = E.op("dv", lambda v, obuf=obuf: v.tensor_copy(obuf[:],
                    pO[:, 0:128]), deps=[tout, ow_tok[par]])
                rdest = c // 16
                ow_tok[par] = E.op("gp", lambda g, c=c, rdest=rdest, obuf=obuf: g.dma_start(
                    bass.AP(agO_in, rdest * 128 * TPC + (c % 16) * 128,
                            [[TPC, 128], [1, 128]]), obuf[:]), deps=[t_oev], sem="owq")
                if DEBUG:
                    ow_tok[par] = E.op("gp", lambda g, csl=csl, obuf=obuf: g.dma_start(
                        dbg["d_out"][:, csl], obuf[:]), deps=[t_oev], sem="owq")
                tok_ST = E.op("dv", lambda v: v.tensor_copy(STf[:], bkF[:, 0:64]),
                              deps=[tch, tout])
                tok_STb = E.op("ac", lambda a: a.activation(STb[:], bkF[:, 0:64], AF.Copy),
                               deps=[tch, tout])
                fF = [tok_ST, tok_STb]
                fO = t_oev
                rb_free[par] = tBC

            def a2a(g):
                with allow_ici_all_to_all():
                    return g.collective_compute(
                        "AllToAll", ALU.bypass, replica_groups=[list(range(NCORES))],
                        ins=[agO_in.ap().opt()], outs=[agO_out.ap().opt()])
            if PHASE == "p2x":
                ag3 = None
            else:
                ag3 = E.cc(a2a, deps=[ow_tok[0], ow_tok[1]])
            E.flush()

        E.sep_fn = None
        if PHASE in ("p2", "p2x"):
            _dummy_out(nc, E, es, y_out)
            return nc

        with ExitStack() as es4:
            sb4 = lambda n, shape, dt=F32: es4.enter_context(nc.sbuf_tensor(n, shape, dt))

            x2Thi = sb4("x2Thi", [128, 8 * TPC], BF16)
            comb = sb4("comb", [128, 128], F32)
            tg_last = None

            with ExitStack() as es4b:
                sb4b = lambda n, shape, dt=F32: es4b.enter_context(nc.sbuf_tensor(n, shape, dt))
                psf4 = lambda n, shape: es4b.enter_context(nc.psum_tensor(n, shape, F32))
                psb4 = lambda n, shape: es4b.enter_context(nc.psum_tensor(n, shape, BF16))
                OUTSl = sb4b("OUTSl", [128, 8 * TPC], BF16)
                x2Tlo = sb4b("x2Tlo", [128, 8 * TPC], BF16)
                wob = sb4b("wob", [128, 8 * 1024], BF16)
                normed = sb4b("normed", [128, 4096], BF16)
                sqb = sb4b("sqb", [128, 512], BF16)
                rstd4 = sb4b("rstd4", [1, 512], F32)
                xrow = sb4b("xrow", [128, 1024], F32)
                x2t4 = sb4b("x2t4", [128, 1024], F32)
                x2bf = sb4b("x2bf", [128, 1024], BF16)
                x2lo = sb4b("x2lo", [128, 1024], BF16)
                lnc1 = sb4b("lnc1", [128, 1], F32)
                lnc2 = sb4b("lnc2", [128, 1], F32)
                lnsq = sb4b("lnsq", [128, 1024], F32)
                gwf = sb4b("gwf", [128, 64], F32)
                gwhi = sb4b("gwhi", [128, 64], BF16)
                gwlo = sb4b("gwlo", [128, 64], BF16)
                gbb = sb4b("gbb", [128, 8], F32)
                glog = sb4b("glog", [128, 8], F32)
                gsc = sb4b("gsc", [128, 8], F32)
                gtmp = sb4b("gtmp", [128, 8], F32)
                gm = sb4b("gm", [128, 1], F32)
                gs2 = sb4b("gs2", [128, 1], F32)
                gwsb = sb4b("gwsb", [8, 1024], F32)

                pM1 = psf4("pM1", [128, 1024])
                pR4 = psf4("pR4", [128, 512])
                pB4 = psf4("pB4", [128, 512])
                pG4 = psf4("pG4", [128, 512])
                pT4 = psb4("pT4", [128, 1024])
                pT5 = psb4("pT5", [128, 1024])

                t_og = []
                for r in range(NCORES):
                    t_og.append(E.op("sp", lambda s, r=r: s.dma_start(
                        OUTSl[:, r * TPC:(r + 1) * TPC],
                        bass.AP(agO_out, r * 128 * TPC, [[TPC, 128], [1, TPC]])),
                        deps=[ag3], sem="oq"))
                t_wo = E.op("gp", lambda g: g.dma_start(wob[:],
                    wo.rearrange("(b p) c -> p b c", p=128)), sem="wobq")
                t_gw = E.op("gp", lambda g: g.dma_start(gwsb[:], gate_w), sem="g4q")
                t_gt = None
                for mb in range(8):
                    t_gt = E.op("pe", lambda t, mb=mb: t.transpose(
                        pG4[:, mb * 8:(mb + 1) * 8], gwsb[0:8, mb * 128:(mb + 1) * 128],
                        identf[0:8, 0:8]), deps=[t_gw] if mb == 0 else (), inc=(mb == 7))
                t_gwf = E.op("dv", lambda v: v.tensor_copy(gwf[:], pG4[:, 0:64]), deps=[t_gt])
                t_ghi = E.op("dv", lambda v: v.tensor_copy(gwhi[:], gwf[:]), deps=[t_gwf])
                t_glo = E.op("dv", lambda v: v.scalar_tensor_tensor(gwlo[:], gwhi[:], -1.0,
                    gwf[:], op0=ALU.mult, op1=ALU.add), deps=[t_ghi])
                t_gbb0 = E.op("pe", lambda t: t.matmul(pG4[:, 64:72], ones1f[:], gbrow[0:1, :],
                    start=True, stop=True), deps=[t_gt, sm_last], rows=(0, 1))
                t_gbbc = E.op("dv", lambda v: v.tensor_copy(gbb[:], pG4[:, 64:72]),
                              deps=[t_gbb0])

                fR4 = fB4 = None
                fM = [None, None]
                fth1 = fxbf = fxlo = fg3 = None
                fT4 = fT5 = None
                fnorm = None
                fx2 = None
                fglog = None
                frstd = None
                fsqb = None
                x2w_toks = []

                for tt in range(4):
                    tred = None
                    for mb in range(8):
                        tqq = E.op("ac", lambda a, mb=mb, tt=tt: a.activation(sqb[:],
                            OUTSl[:, mb * TPC + tt * 512: mb * TPC + (tt + 1) * 512],
                            AF.Square), deps=[t_og[-1], fsqb])
                        tred = E.op("pe", lambda t, mb=mb: t.matmul(pR4[0:1, 0:512],
                            onesvb[:], sqb[:], start=(mb == 0), stop=(mb == 7)),
                            deps=[tqq, fR4] if mb == 0 else [tqq], inc=True)
                        fsqb = tred
                    if DEBUG and tt == 0:
                        dsq = sb4b("dsq", [1, 512], F32)
                        tdsq = E.op("dv", lambda v: v.tensor_copy(dsq[0:1, :], pR4[0:1, 0:512]),
                                    deps=[tred])
                        E.op("gp", lambda g: g.dma_start(dbg["d_sqs"], dsq[0:1, :]), deps=[tdsq])
                    tln = E.op("ac", lambda a: a.activation(rstd4[0:1, :], pR4[0:1, 0:512],
                        AF.Ln, scale=1.0 / D, bias=ceps6[0:1, 0:1]), deps=[tred, frstd, (tdsq if (DEBUG and tt == 0) else None)])
                    tex = E.op("ac", lambda a: a.activation(rstd4[0:1, :], rstd4[0:1, :],
                        AF.Exp, scale=-0.5), deps=[tln])
                    if DEBUG and tt == 0:
                        E.op("gp", lambda g: g.dma_start(dbg["d_rstd"], rstd4[0:1, :]), deps=[tex])
                    fR4 = [tex, (tdsq if (DEBUG and tt == 0) else None)]
                    tbc = E.op("pe", lambda t: t.matmul(pB4[:, 0:512], ones1f[:],
                        rstd4[0:1, :], start=True, stop=True), deps=[tex, fB4])
                    frstd = tbc
                    tnm = None
                    for mb in range(8):
                        tnm = E.op("dv", lambda v, mb=mb, tt=tt: v.scalar_tensor_tensor(
                            normed[:, mb * 512:(mb + 1) * 512],
                            OUTSl[:, mb * TPC + tt * 512: mb * TPC + (tt + 1) * 512],
                            onw_c[:, mb:mb + 1], pB4[:, 0:512], op0=ALU.mult, op1=ALU.mult),
                            deps=[tbc, fnorm] if mb == 0 else (), inc=(mb == 7))
                    fB4 = tnm

                    for ts_i in range(4):
                        gt = tt * 4 + ts_i
                        tmix = None
                        for ch in range(2):
                            for mb in range(8):
                                tmix = E.op("pe", lambda t, ch=ch, mb=mb, ts_i=ts_i:
                                    t.matmul(pM1[:, ch * 512:(ch + 1) * 512],
                                             normed[:, mb * 512 + ts_i * 128: mb * 512 + (ts_i + 1) * 128],
                                             wob[:, mb * 1024 + ch * 512: mb * 1024 + (ch + 1) * 512],
                                             start=(mb == 0), stop=(mb == 7)),
                                    deps=[tnm, t_wo, fM[ch]] if mb == 0 else (),
                                    inc=(mb == 7))
                                fmix = tmix
                        fnorm = tmix
                        txl = E.op("sp", lambda s, gt=gt: s.dma_start(xrow[:],
                            x_rows[gt * 128:(gt + 1) * 128, :]), deps=[fx2], sem="xrq")
                        ta = E.op("dv", lambda v: v.tensor_add(x2t4[:, 0:512], xrow[:, 0:512],
                            pM1[:, 0:512]), deps=[tmix, txl, fth1])
                        fM[0] = ta
                        ta = E.op("dv", lambda v: v.tensor_add(x2t4[:, 512:1024],
                            xrow[:, 512:1024], pM1[:, 512:1024]), deps=[ta])
                        fM[1] = ta
                        u1 = E.op("dv", lambda v: v.tensor_reduce(lnc1[:, 0:1], x2t4[:],
                            axis=mybir.AxisListType.X, op=ALU.add, negate=True), deps=[ta])
                        u2 = E.op("dv", lambda v: v.tensor_scalar_mul(lnc1[:, 0:1],
                            lnc1[:, 0:1], 1.0 / D), deps=[u1])
                        u3 = E.op("dv", lambda v: v.tensor_scalar(x2t4[:], x2t4[:],
                            lnc1[:, 0:1], None, op0=ALU.add), deps=[u2])
                        u4 = E.op("ac", lambda a: a.activation(lnsq[:], x2t4[:], AF.Square),
                                  deps=[u3])
                        u5 = E.op("dv", lambda v: v.tensor_reduce(lnc2[:, 0:1], lnsq[:],
                            axis=mybir.AxisListType.X, op=ALU.add), deps=[u4])
                        u6 = E.op("ac", lambda a: a.activation(lnc2[:, 0:1], lnc2[:, 0:1],
                            AF.Ln, scale=1.0 / D, bias=ceps5[:, 0:1]), deps=[u5])
                        u7 = E.op("ac", lambda a: a.activation(lnc2[:, 0:1], lnc2[:, 0:1],
                            AF.Exp, scale=-0.5), deps=[u6])
                        u8 = E.op("dv", lambda v: v.tensor_scalar(x2t4[:], x2t4[:],
                            lnc2[:, 0:1], None, op0=ALU.mult), deps=[u7])
                        fx2 = E.op("sp", lambda s, gt=gt: s.dma_start(
                            x2_scr.ap()[gt * 128:(gt + 1) * 128, :], x2t4[:]), deps=[u8],
                            sem="x2wq")
                        x2w_toks.append(fx2)
                        th1 = E.op("ac", lambda a: a.activation(x2bf[:], x2t4[:], AF.Copy),
                                   deps=[u8, fxbf])
                        th2 = E.op("dv", lambda v: v.scalar_tensor_tensor(x2lo[:], x2bf[:],
                            -1.0, x2t4[:], op0=ALU.mult, op1=ALU.add), deps=[th1, fxlo])
                        tq1 = None
                        for mb in range(8):
                            tq1 = E.op("pe", lambda t, mb=mb: t.transpose(
                                pT4[:, mb * 128:(mb + 1) * 128],
                                x2bf[:, mb * 128:(mb + 1) * 128], identb[:]),
                                deps=[th1, fT4] if mb == 0 else (), inc=(mb == 7))
                        fT4 = E.op("dv", lambda v, gt=gt: v.tensor_copy(
                            bass.AP(x2Thi, gt * 128, [[8 * TPC, 128], [TPC, 8], [1, 128]]),
                            pT4[:, 0:1024]), deps=[tq1])
                        hi_ev = fT4
                        fth1 = th1
                        fxbf = tq1
                        tq2 = None
                        for mb in range(8):
                            tq2 = E.op("pe", lambda t, mb=mb: t.transpose(
                                pT5[:, mb * 128:(mb + 1) * 128],
                                x2lo[:, mb * 128:(mb + 1) * 128], identb[:]),
                                deps=[th2, fT5] if mb == 0 else (), inc=(mb == 7))
                        fT5 = E.op("dv", lambda v, gt=gt: v.tensor_copy(
                            bass.AP(x2Tlo, gt * 128, [[8 * TPC, 128], [TPC, 8], [1, 128]]),
                            pT5[:, 0:1024]), deps=[tq2])
                        lo_ev = fT5
                        fxlo = tq2
                        tgl = None
                        for mb in range(8):
                            tgl = E.op("pe", lambda t, mb=mb, gt=gt: t.matmul(
                                pG4[:, 128:136],
                                bass.AP(x2Thi, mb * TPC + gt * 128, [[8 * TPC, 128], [1, 128]]),
                                gwhi[:, mb * 8:(mb + 1) * 8], start=(mb == 0), stop=False),
                                deps=[hi_ev, t_ghi, t_gbbc, fglog] if mb == 0 else (), inc=False)
                        for mb in range(8):
                            tgl = E.op("pe", lambda t, mb=mb, gt=gt: t.matmul(
                                pG4[:, 128:136],
                                bass.AP(x2Thi, mb * TPC + gt * 128, [[8 * TPC, 128], [1, 128]]),
                                gwlo[:, mb * 8:(mb + 1) * 8], start=False, stop=False),
                                deps=[t_glo] if mb == 0 else (), inc=False)
                        for mb in range(8):
                            tgl = E.op("pe", lambda t, mb=mb, gt=gt: t.matmul(
                                pG4[:, 128:136],
                                bass.AP(x2Tlo, mb * TPC + gt * 128, [[8 * TPC, 128], [1, 128]]),
                                gwhi[:, mb * 8:(mb + 1) * 8], start=False, stop=(mb == 7)),
                                deps=[lo_ev] if mb == 0 else (), inc=(mb == 7))
                        tg1 = E.op("dv", lambda v: v.tensor_copy(glog[:], pG4[:, 128:136]),
                                   deps=[tgl, fg3])
                        fglog = tg1
                        tg2 = E.op("dv", lambda v: v.tensor_reduce(gm[:, 0:1], glog[:],
                            axis=mybir.AxisListType.X, op=ALU.max, negate=True), deps=[tg1])
                        tg3 = E.op("ac", lambda a: a.activation(gsc[:], glog[:], AF.Exp,
                            bias=gm[:, 0:1]), deps=[tg2])
                        fg3 = tg3
                        tg4 = E.op("dv", lambda v: v.tensor_reduce(gm[:, 0:1], gsc[:],
                            axis=mybir.AxisListType.X, op=ALU.add), deps=[tg3])
                        tg5 = E.op("dv", lambda v: v.reciprocal(gs2[:, 0:1], gm[:, 0:1]),
                                   deps=[tg4])
                        tg6 = E.op("dv", lambda v: v.tensor_scalar(gsc[:], gsc[:],
                            gs2[:, 0:1], None, op0=ALU.mult), deps=[tg5])
                        tg7 = E.op("dv", lambda v: v.tensor_add(gtmp[:], gsc[:], gbb[:]),
                                   deps=[tg6])
                        tg8 = E.op("dv", lambda v: v.tensor_reduce(gm[:, 0:1], gtmp[:],
                            axis=mybir.AxisListType.X, op=ALU.max), deps=[tg7])
                        tg9 = E.op("dv", lambda v: v.tensor_scalar(glog[:], gtmp[:],
                            gm[:, 0:1], None, op0=ALU.is_ge), deps=[tg8])
                        tg10 = E.op("dv", lambda v: v.scalar_tensor_tensor(glog[:], glog[:],
                            NEG, gtmp[:], op0=ALU.mult, op1=ALU.add), deps=[tg9])
                        tg11 = E.op("dv", lambda v: v.tensor_reduce(gm[:, 0:1], glog[:],
                            axis=mybir.AxisListType.X, op=ALU.max), deps=[tg10])
                        tg12 = E.op("dv", lambda v: v.tensor_scalar(gtmp[:], gtmp[:],
                            gm[:, 0:1], None, op0=ALU.is_ge), deps=[tg11])
                        tg13 = E.op("dv", lambda v, gt=gt: v.tensor_mul(
                            comb[:, gt * 8:(gt + 1) * 8], gtmp[:], gsc[:]), deps=[tg12])
                        tg_last = tg13
                if DEBUG:
                    E.op("gp", lambda g: g.dma_start(dbg["d_x2"], x2_scr.ap()),
                         deps=[x2w_toks[-1]])
                    E.op("gp", lambda g: g.dma_start(dbg["d_comb"], comb[:]), deps=[tg_last])
                E.flush()

            with ExitStack() as es5:
                sb5 = lambda n, shape, dt=F32: es5.enter_context(nc.sbuf_tensor(n, shape, dt))
                psf5 = lambda n, shape: es5.enter_context(nc.psum_tensor(n, shape, F32))
                yacc = sb5("yacc", [128, 16 * D], F32)
                es5b = es5.enter_context(ExitStack())
                sb5b = lambda n, shape, dt=F32: es5b.enter_context(nc.sbuf_tensor(n, shape, dt))
                h_e = sb5b("h_e", [128, 16 * TPC], BF16)
                w1b0 = sb5b("w1b0", [128, 1024], BF16)
                w3b0 = sb5b("w3b0", [128, 1024], BF16)
                w2full = sb5b("w2full", [128, 16 * 1024], BF16)
                silu_t = sb5b("silu_t", [128, 512], BF16)
                ytmp0 = sb5b("ytmp0", [128, 512], F32)
                ytmp1 = sb5b("ytmp1", [128, 512], F32)
                pH1a = psf5("pH1a", [128, 512])
                pH1b = psf5("pH1b", [128, 512])
                pH3a = psf5("pH3a", [128, 512])
                pH3b = psf5("pH3b", [128, 512])
                pY1 = psf5("pY1", [128, 512])
                pY2 = psf5("pY2", [128, 512])

                def w1_off(e, which):
                    return (e * 3 * PIECE if e < E_EXP else base_sh) + which * PIECE

                def w2_off(e):
                    return (e * 3 * PIECE if e < E_EXP else base_sh) + 2 * PIECE

                experts = [E_EXP] + list(range(E_EXP))
                w1_rd = {}
                w3_rd = {}
                w1_ld = {}
                w3_ld = {}
                prev_y_last = None
                y_free = [None, None]
                fsilu = None
                fytmp = [None, None]
                yt = {}
                ph_free = {"1a": None, "1b": None, "3a": None, "3b": None}

                for ei, e in enumerate(experts):
                    w1b = w1b0
                    w3b = w3b0
                    w2tk = []
                    for ib in range(16):
                        rk = ib // 2
                        w2tk.append(E.op("sp", lambda s, e=e, ib=ib, rk=rk: s.dma_start(
                            w2full[:, ib * 1024:(ib + 1) * 1024],
                            bass.AP(agW_out, rk * WCHUNK + w2_off(e) + (ib % 2) * 128 * 1024,
                                    [[1024, 128], [1, 1024]])),
                            deps=[ag2, prev_y_last] if ib == 0 else (), sem="w2q"))
                    hA_last = None
                    for ib in range(16):
                        prev1 = w1_rd.get((ei, ib - 1), w1_rd.get((ei - 1, 15)))
                        prev3 = w3_rd.get((ei, ib - 1), w3_rd.get((ei - 1, 15)))
                        tl1 = E.op("sp", lambda s, e=e, ib=ib, w1b=w1b: s.dma_start(
                            bass.AP(w1b, 0, [[1024, 128], [128, 8], [1, 128]]),
                            bass.AP(agW_out, w1_off(e, 0) + ib * 128,
                                    [[2048, 128], [WCHUNK, 8], [1, 128]])),
                            deps=[ag2, prev1], sem="w1q")
                        w1_ld[(ei, ib)] = tl1
                        tl3 = E.op("sp", lambda s, e=e, ib=ib, w3b=w3b: s.dma_start(
                            bass.AP(w3b, 0, [[1024, 128], [128, 8], [1, 128]]),
                            bass.AP(agW_out, w1_off(e, 1) + ib * 128,
                                    [[2048, 128], [WCHUNK, 8], [1, 128]])),
                            deps=[prev3], sem="w3q")
                        w3_ld[(ei, ib)] = tl3
                        for tg in range(4):
                            pH1 = pH1a if tg % 2 == 0 else pH1b
                            pH3 = pH3a if tg % 2 == 0 else pH3b
                            k1 = "1a" if tg % 2 == 0 else "1b"
                            k3 = "3a" if tg % 2 == 0 else "3b"
                            tm1 = None
                            for mb in range(8):
                                tm1 = E.op("pe", lambda t, mb=mb, tg=tg, pH1=pH1, w1b=w1b:
                                    t.matmul(pH1[:, 0:512], w1b[:, mb * 128:(mb + 1) * 128],
                                             x2Thi[:, mb * TPC + tg * 512: mb * TPC + (tg + 1) * 512],
                                             start=(mb == 0), stop=(mb == 7)),
                                    deps=[tl1, ph_free[k1], prev_y_last, hi_ev] if mb == 0 else (),
                                    inc=(mb == 7))
                            w1_rd[(ei, ib)] = tm1
                            tm3 = None
                            for mb in range(8):
                                tm3 = E.op("pe", lambda t, mb=mb, tg=tg, pH3=pH3, w3b=w3b:
                                    t.matmul(pH3[:, 0:512], w3b[:, mb * 128:(mb + 1) * 128],
                                             x2Thi[:, mb * TPC + tg * 512: mb * TPC + (tg + 1) * 512],
                                             start=(mb == 0), stop=(mb == 7)),
                                    deps=[tl3, ph_free[k3]] if mb == 0 else (), inc=(mb == 7))
                            w3_rd[(ei, ib)] = tm3
                            tsl = E.op("ac", lambda a, pH1=pH1: a.activation(silu_t[:],
                                pH1[:, 0:512], AF.Silu), deps=[tm1, fsilu])
                            ph_free[k1] = tsl
                            tmu = E.op("dv", lambda v, ib=ib, tg=tg, pH3=pH3: v.tensor_mul(
                                h_e[:, ib * TPC + tg * 512: ib * TPC + (tg + 1) * 512],
                                silu_t[:], pH3[:, 0:512]), deps=[tsl, tm3])
                            fsilu = tmu
                            ph_free[k3] = tmu
                            yt[(ib, tg)] = tmu
                            hA_last = tmu
                    for ts_i in range(16):
                        for ch in range(2):
                            par = (ts_i * 2 + ch) % 2
                            pY = pY1 if par == 0 else pY2
                            ty = None
                            for ib in range(16):
                                ty = E.op("pe", lambda t, ib=ib, ts_i=ts_i, ch=ch, pY=pY:
                                    t.matmul(pY[:, 0:512],
                                             h_e[:, ib * TPC + ts_i * 128:ib * TPC + (ts_i + 1) * 128],
                                             w2full[:, ib * 1024 + ch * 512: ib * 1024 + (ch + 1) * 512],
                                             start=(ib == 0), stop=(ib == 15)),
                                    deps=[yt[(ib, ts_i // 4)], w2tk[-1], y_free[par]]
                                         if ib == 0 else [yt[(ib, ts_i // 4)]],
                                    inc=(ib == 15))
                            if e == E_EXP:
                                tya = E.op("ac", lambda a, ts_i=ts_i, ch=ch, pY=pY:
                                    a.activation(
                                        yacc[:, ts_i * D + ch * 512: ts_i * D + (ch + 1) * 512],
                                        pY[:, 0:512], AF.Copy), deps=[ty])
                                y_free[par] = tya
                                yt[("acc", ts_i, ch)] = tya
                            else:
                                ytmp = ytmp0 if par == 0 else ytmp1
                                tya = E.op("ac", lambda a, ts_i=ts_i, ch=ch, pY=pY, e=e, ytmp=ytmp:
                                    a.activation(ytmp[:], pY[:, 0:512], AF.Copy,
                                                 scale=comb[:, ts_i * 8 + e: ts_i * 8 + e + 1]),
                                    deps=[ty, tg_last, fytmp[par]])
                                y_free[par] = tya
                                tyb = E.op("dv", lambda v, ts_i=ts_i, ch=ch, ytmp=ytmp: v.tensor_add(
                                    yacc[:, ts_i * D + ch * 512: ts_i * D + (ch + 1) * 512],
                                    yacc[:, ts_i * D + ch * 512: ts_i * D + (ch + 1) * 512],
                                    ytmp[:]), deps=[tya, yt[("acc", ts_i, ch)]])
                                fytmp[par] = tyb
                                yt[("acc", ts_i, ch)] = tyb
                            prev_y_last = ty
                E.flush()
                es5b.close()

                x2r = sb5("x2r", [128, 1024], F32)
                o6 = sb5("o6", [128, 1024], F32)
                l6a = sb5("l6a", [128, 1], F32)
                l6b = sb5("l6b", [128, 1], F32)
                sq6 = sb5("sq6", [128, 1024], F32)
                fx2r = None
                out_toks = []
                for ts_i in range(16):
                    tx = E.op("sp", lambda s, ts_i=ts_i: s.dma_start(x2r[:],
                        x2_scr.ap()[ts_i * 128:(ts_i + 1) * 128, :]),
                        deps=[x2w_toks[-1], fx2r], sem="x2q")
                    tadd = E.op("dv", lambda v, ts_i=ts_i: v.tensor_add(o6[:], x2r[:],
                        yacc[:, ts_i * D:(ts_i + 1) * D]),
                        deps=[tx, yt[("acc", ts_i, 0)], yt[("acc", ts_i, 1)]])
                    u1 = E.op("dv", lambda v: v.tensor_reduce(l6a[:, 0:1], o6[:],
                        axis=mybir.AxisListType.X, op=ALU.add, negate=True), deps=[tadd])
                    u2 = E.op("dv", lambda v: v.tensor_scalar_mul(l6a[:, 0:1], l6a[:, 0:1],
                        1.0 / D), deps=[u1])
                    u3 = E.op("dv", lambda v: v.tensor_scalar(o6[:], o6[:], l6a[:, 0:1],
                        None, op0=ALU.add), deps=[u2])
                    u4 = E.op("ac", lambda a: a.activation(sq6[:], o6[:], AF.Square), deps=[u3])
                    u5 = E.op("dv", lambda v: v.tensor_reduce(l6b[:, 0:1], sq6[:],
                        axis=mybir.AxisListType.X, op=ALU.add), deps=[u4])
                    u6 = E.op("ac", lambda a: a.activation(l6b[:, 0:1], l6b[:, 0:1], AF.Ln,
                        scale=1.0 / D, bias=ceps5[:, 0:1]), deps=[u5])
                    u7 = E.op("ac", lambda a: a.activation(l6b[:, 0:1], l6b[:, 0:1], AF.Exp,
                        scale=-0.5), deps=[u6])
                    u8 = E.op("dv", lambda v: v.tensor_scalar(o6[:], o6[:], l6b[:, 0:1],
                        None, op0=ALU.mult), deps=[u7])
                    tout6 = E.op("sp", lambda s, ts_i=ts_i: s.dma_start(
                        y_out[ts_i * 128:(ts_i + 1) * 128, :], o6[:]), deps=[u8], sem="yq")
                    out_toks.append(tout6)
                    fx2r = tout6

                fin = out_toks[-1]
                E.raw("gp", lambda g, fin=fin: g.wait_ge(E.sem["yq"], fin[1]))
                E.raw("sp", lambda s, fin=fin: s.wait_ge(E.sem["yq"], fin[1]))
                E.flush()

    return nc


def kernel(**inputs):
    x = np.asarray(inputs["L0_attn_x"])[0]
    get = lambda n: np.asarray(inputs[n])
    in_maps = []
    for c in range(NCORES):
        hs = slice(2 * c, 2 * c + 2)
        m = {
            "x_rows": np.ascontiguousarray(x[c * TPC:(c + 1) * TPC]),
            "wq_s": np.ascontiguousarray(get("L0_attn_Wq")[:, c * 128:(c + 1) * 128]),
            "wk_s": np.ascontiguousarray(get("L0_attn_Wk")[:, c * 128:(c + 1) * 128]),
            "wv_s": np.ascontiguousarray(get("L0_attn_Wv")[:, c * 128:(c + 1) * 128]),
            "wrq_s": np.ascontiguousarray(get("L0_attn_Wrq")[:, c * 96:(c + 1) * 96]),
            "wrk_s": np.ascontiguousarray(get("L0_attn_Wrk")[:, c * 96:(c + 1) * 96]),
            "convq_s": np.ascontiguousarray(get("L0_attn_conv_q")[c * 128:(c + 1) * 128]),
            "convk_s": np.ascontiguousarray(get("L0_attn_conv_k")[c * 128:(c + 1) * 128]),
            "convv_s": np.ascontiguousarray(get("L0_attn_conv_v")[c * 128:(c + 1) * 128]),
            "wf_s": np.ascontiguousarray(get("L0_attn_Wf")[:, hs]),
            "wl_s": np.ascontiguousarray(get("L0_attn_Wl")[:, hs]),
            "bfbl": np.ascontiguousarray(np.concatenate(
                [get("L0_attn_bf")[hs], get("L0_attn_bl")[hs]])),
            "gs_s": np.ascontiguousarray(get("L0_attn_graph_scale")[hs]),
            "qn_w": np.ascontiguousarray(get("L0_attn_qn_w")),
            "kn_w": np.ascontiguousarray(get("L0_attn_kn_w")),
            "on_w": np.ascontiguousarray(get("L0_attn_on_w")),
            "wo": np.ascontiguousarray(get("L0_attn_Wo")),
            "gate_w": np.ascontiguousarray(get("L0_ffn_gate_w")),
            "gate_b": np.ascontiguousarray(get("L0_ffn_gate_bias")),
            "w1p": np.ascontiguousarray(get("L0_ffn_w1")[:, :, c * 128:(c + 1) * 128]),
            "w3p": np.ascontiguousarray(get("L0_ffn_w3")[:, :, c * 128:(c + 1) * 128]),
            "w2p": np.ascontiguousarray(get("L0_ffn_w2")[:, :, c * 256:(c + 1) * 256]),
            "sw1p": np.ascontiguousarray(get("L0_ffn_sw1")[:, c * 128:(c + 1) * 128]),
            "sw3p": np.ascontiguousarray(get("L0_ffn_sw3")[:, c * 128:(c + 1) * 128]),
            "sw2p": np.ascontiguousarray(get("L0_ffn_sw2")[:, c * 256:(c + 1) * 256]),
        }
        in_maps.append(m)
    nc = build_program()
    res = run_bass_kernel_spmd(nc, in_maps, core_ids=list(range(NCORES)))
    out = np.concatenate([np.asarray(res.results[c]["y"]) for c in range(NCORES)], axis=0)
    return out.reshape(1, S, D).astype(np.float32)




# version 09 (final)
# speedup vs baseline: 1.0291x; 1.0239x over previous
import numpy as np
import ml_dtypes
import concourse.bass as bass
import concourse.mybir as mybir
from concourse.bass_utils import run_bass_kernel_spmd
from concourse.aot_env import allow_ici_all_to_all
from contextlib import ExitStack

F32 = mybir.dt.float32
BF16 = mybir.dt.bfloat16
BF = ml_dtypes.bfloat16
AF = mybir.ActivationFunctionType
ALU = mybir.AluOpType

NCORES = 8
S, D, H, dH = 16384, 1024, 16, 64
C = 128
NCH = S // C
TPC = S // NCORES
INTER, E_EXP = 2048, 8
NEG = -1.0e30
PIECE = 128 * 2048
WCHUNK = 27 * PIECE

DEBUG = False
PHASE = "all"
RACE_DETECT = False
NCH_RUN = NCH
SCAN_PART = "all"


def _dummy_out(nc, E, es, y_out):
    with nc.sbuf_tensor("dummy0", [128, 1024], F32) as dz:
        tz = E.gpc(lambda g: g.memset(dz[:], 0.0))
        tl = None
        for i in range(16):
            tl = E.op("gp", lambda g, i=i: g.dma_start(
                y_out[i * 128:(i + 1) * 128, :], dz[:]), deps=[tz, tl])
        E.raw("gp", lambda g, tl=tl: g.wait_ge(E.sem["gp"], tl[1]))
        E.flush()


class Emit:
    ENGS = ("pe", "dv", "ac", "sp", "gp")

    def __init__(self, nc, es):
        self.nc = nc
        self.q = {k: [] for k in self.ENGS}
        self.sem = {k: es.enter_context(nc.semaphore(f"s_{k}")) for k in self.ENGS}
        self.sem["cc"] = es.enter_context(nc.semaphore("s_cc"))
        self.sem["gpc"] = es.enter_context(nc.semaphore("s_gpc"))
        self.cnt = {k: 0 for k in list(self.ENGS) + ["cc", "gpc"]}
        self.waited = {k: {} for k in self.ENGS}
        self.sep_fn = None
        self._last_rows = None

    def _deps(self, eng, deps):
        wl = []
        for d in deps:
            if d is None:
                continue
            if isinstance(d, list):
                for dd in d:
                    if dd is None:
                        continue
                    dsem, dval = dd
                    if dval > self.waited[eng].get(dsem, -1):
                        self.waited[eng][dsem] = dval
                        wl.append((dsem, dval))
            else:
                dsem, dval = d
                if dval > self.waited[eng].get(dsem, -1):
                    self.waited[eng][dsem] = dval
                    wl.append((dsem, dval))
        return tuple(wl)

    def addsem(self, es, nc, name):
        self.sem[name] = es.enter_context(nc.semaphore("s_" + name))
        self.cnt[name] = 0

    def op(self, eng, fn, deps=(), inc=True, rows=None, sem=None):
        step = 16 if eng in ("sp", "gp") else 1
        if eng == "pe":
            r = rows if rows is not None else (0, 128)
            last = getattr(self, "_last_rows", None)
            if (r != (0, 128) and last is not None and last != (0, 128)
                    and last != r and self.sep_fn is not None):
                sfn = self.sep_fn
                self.q["pe"].append(lambda e, sfn=sfn: sfn(e))
            self._last_rows = r
        wl = self._deps(eng, deps)
        tok = None
        key = sem if sem is not None else eng
        if inc:
            self.cnt[key] += step
            tok = (key, self.cnt[key])
        semmap = self.sem

        def run(e, fn=fn, wl=wl, inc=inc, step=step, key=key):
            for dsem, dval in wl:
                e.wait_ge(semmap[dsem], dval)
            inst = fn(e)
            if inc:
                inst.then_inc(semmap[key], step)
        self.q[eng].append(run)
        return tok

    def gpc(self, fn, deps=()):
        deps = list(deps) + ([("gpc", self.cnt["gpc"])] if self.cnt["gpc"] > 0 else [])
        wl = self._deps("gp", deps)
        self.cnt["gpc"] += 1
        tok = ("gpc", self.cnt["gpc"])
        semmap = self.sem

        def run(g, fn=fn, wl=wl):
            for dsem, dval in wl:
                g.wait_ge(semmap[dsem], dval)
            fn(g).then_inc(semmap["gpc"], 1)
        self.q["gp"].append(run)
        return tok

    def cc(self, fn, deps=()):
        wl = self._deps("gp", deps)
        self.cnt["cc"] += 1
        tok = ("cc", self.cnt["cc"])
        semmap = self.sem

        def run(g, fn=fn, wl=wl):
            for dsem, dval in wl:
                g.wait_ge(semmap[dsem], dval)
            fn(g).then_inc(semmap["cc"], 1)
        self.q["gp"].append(run)
        return tok

    def raw(self, eng, fn):
        self.q[eng].append(fn)

    def flush(self):
        nc = self.nc
        q = self.q
        with nc.Block() as blk:
            if q["pe"]:
                @blk.tensor
                def _(t, ops=tuple(q["pe"])):
                    for f in ops:
                        f(t)
            if q["dv"]:
                @blk.vector
                def _(v, ops=tuple(q["dv"])):
                    for f in ops:
                        f(v)
            if q["ac"]:
                @blk.scalar
                def _(a, ops=tuple(q["ac"])):
                    for f in ops:
                        f(a)
            if q["sp"]:
                @blk.sync
                def _(s, ops=tuple(q["sp"])):
                    for f in ops:
                        f(s)
            if q["gp"]:
                @blk.gpsimd
                def _(g, ops=tuple(q["gp"])):
                    for f in ops:
                        f(g)
        self.q = {k: [] for k in self.ENGS}


def build_program():
    nc = bass.Bass("TRN2", target_bir_lowering=False, detect_race_conditions=RACE_DETECT)

    def din(name, shape):
        return nc.dram_tensor(name, shape, F32, kind="ExternalInput").ap()

    x_rows = din("x_rows", [TPC, D])
    wq_s = din("wq_s", [D, 128])
    wk_s = din("wk_s", [D, 128])
    wv_s = din("wv_s", [D, 128])
    wrq_s = din("wrq_s", [D, 96])
    wrk_s = din("wrk_s", [D, 96])
    convq_s = din("convq_s", [128, 4])
    convk_s = din("convk_s", [128, 4])
    convv_s = din("convv_s", [128, 4])
    wf_s = din("wf_s", [D, 2])
    wl_s = din("wl_s", [D, 2])
    bfbl = din("bfbl", [4])
    gs_s = din("gs_s", [2])
    qn_w = din("qn_w", [dH])
    kn_w = din("kn_w", [dH])
    on_w = din("on_w", [D])
    wo = din("wo", [D, D])
    gate_w = din("gate_w", [E_EXP, D])
    gate_b = din("gate_b", [E_EXP])
    w1p = din("w1p", [E_EXP, INTER, 128])
    w3p = din("w3p", [E_EXP, INTER, 128])
    w2p = din("w2p", [E_EXP, D, 256])
    sw1p = din("sw1p", [INTER, 128])
    sw3p = din("sw3p", [INTER, 128])
    sw2p = din("sw2p", [D, 256])

    y_out = nc.dram_tensor("y", [TPC, D], F32, kind="ExternalOutput").ap()
    dbg = {}
    if DEBUG:
        for nm, shp in [("d_Qch", [128, S]), ("d_Kch", [128, S]), ("d_Ktok", [128, S]),
                        ("d_Vtok", [128, S]), ("d_lcum", [128, 256]), ("d_out", [128, S]),
                        ("d_x2", [TPC, D]), ("d_comb", [128, 128]), ("d_bt", [128, 256]),
                        ("d_zpk", [128, 256]), ("d_W", [128, 256]), ("d_EMT", [128, 256]),
                        ("d_NT", [128, 256]), ("d_exp", [128, 512]),
                        ("d_STb", [128, 64]), ("d_PT", [128, 128]), ("d_rstd", [1, 512]),
                        ("d_sqs", [1, 512])]:
            dt = F32 if nm in ("d_lcum", "d_x2", "d_comb", "d_bt", "d_zpk") else BF16
            dbg[nm] = nc.dram_tensor(nm, shp, dt, kind="ExternalOutput").ap()

    agX_in = nc.dram_tensor("agX_in", [D, TPC], BF16)
    agX_out = nc.dram_tensor("agX_out", [NCORES * D, TPC], BF16)
    agW_in = nc.dram_tensor("agW_in", [WCHUNK], BF16)
    agW_out = nc.dram_tensor("agW_out", [NCORES * WCHUNK], BF16)
    agO_in = nc.dram_tensor("agO_in", [NCORES, 128, TPC], BF16)
    agO_out = nc.dram_tensor("agO_out", [NCORES, 128, TPC], BF16)
    x2_scr = nc.dram_tensor("x2_scr", [TPC, D], F32)
    z_hbm = nc.dram_tensor("z_hbm", [4, S], F32)

    es = ExitStack()
    with es:
        E = Emit(nc, es)
        for _s in ("smq", "agxq", "wldq", "wldq1", "wstq", "wstq1", "p1wq", "xq0", "xq1", "zq",
                   "owq", "oq", "wobq", "g4q", "xrq", "x2wq", "x2q", "w1q",
                   "w3q", "w2q", "yq", "xbq") + tuple(f"rbq{i}" for i in range(8)):
            E.addsem(es, nc, _s)
        sb = lambda n, shape, dt=F32: es.enter_context(nc.sbuf_tensor(n, shape, dt))

        identb = sb("identb", [128, 128], BF16)
        identf = sb("identf", [128, 128], F32)
        ones1f = sb("ones1f", [1, 128], F32)
        onesvb = sb("onesvb", [128, 1], BF16)
        qnw_c = sb("qnw_c", [128, 1], F32)
        knw_c = sb("knw_c", [128, 1], F32)
        lam_c = sb("lam_c", [128, 1], F32)
        onw_c = sb("onw_c", [128, 8], F32)
        cvq_c = sb("cvq_c", [128, 4], F32)
        cvk_c = sb("cvk_c", [128, 4], F32)
        cvv_c = sb("cvv_c", [128, 4], F32)
        bfbl4 = sb("bfbl4", [4, 1], F32)
        gsbuf = sb("gsbuf", [2, 1], F32)
        absgs = sb("absgs", [2, 1], F32)
        gbrow = sb("gbrow", [1, 8], F32)
        ceps6 = sb("ceps6", [128, 1], F32)
        ceps24 = sb("ceps24", [128, 1], F32)
        ceps5 = sb("ceps5", [128, 1], F32)

        cst = None
        for fn in [
            lambda g: g.memset(identb[:], 0.0),
            lambda g: g.affine_select(out=identb[:], in_=identb[:], compare_op=ALU.not_equal,
                                      fill=1.0, base=0, pattern=[[-1, 128]], channel_multiplier=1),
            lambda g: g.memset(identf[:], 0.0),
            lambda g: g.affine_select(out=identf[:], in_=identf[:], compare_op=ALU.not_equal,
                                      fill=1.0, base=0, pattern=[[-1, 128]], channel_multiplier=1),
            lambda g: g.memset(ones1f[:], 1.0),
            lambda g: g.memset(onesvb[:], 1.0),
            lambda g: g.memset(ceps6[:], 1e-6),
            lambda g: g.memset(ceps24[:], 1e-24),
            lambda g: g.memset(ceps5[:], 1e-5),
        ]:
            cst = E.gpc(fn)

        sm = []
        sm.append(E.op("gp", lambda g: g.dma_start(qnw_c[0:64, 0:1],
            qn_w.rearrange("(a b) -> a b", b=1), allow_slow_non_contiguous=True), sem="smq"))
        sm.append(E.op("gp", lambda g: g.dma_start(qnw_c[64:128, 0:1],
            qn_w.rearrange("(a b) -> a b", b=1), allow_slow_non_contiguous=True), sem="smq"))
        sm.append(E.op("gp", lambda g: g.dma_start(knw_c[0:64, 0:1],
            kn_w.rearrange("(a b) -> a b", b=1), allow_slow_non_contiguous=True), sem="smq"))
        sm.append(E.op("gp", lambda g: g.dma_start(knw_c[64:128, 0:1],
            kn_w.rearrange("(a b) -> a b", b=1), allow_slow_non_contiguous=True), sem="smq"))
        sm.append(E.op("gp", lambda g: g.dma_start(onw_c[:],
            on_w.rearrange("(b p) -> p b", p=128), allow_slow_non_contiguous=True), sem="smq"))
        sm.append(E.op("gp", lambda g: g.dma_start(cvq_c[:], convq_s), sem="smq"))
        sm.append(E.op("gp", lambda g: g.dma_start(cvk_c[:], convk_s), sem="smq"))
        sm.append(E.op("gp", lambda g: g.dma_start(cvv_c[:], convv_s), sem="smq"))
        sm.append(E.op("gp", lambda g: g.dma_start(bfbl4[0:4, 0:1],
            bfbl.rearrange("(a b) -> a b", b=1), allow_slow_non_contiguous=True), sem="smq"))
        sm.append(E.op("gp", lambda g: g.dma_start(gsbuf[:],
            gs_s.rearrange("(a b) -> a b", b=1), allow_slow_non_contiguous=True), sem="smq"))
        sm.append(E.op("gp", lambda g: g.dma_start(gbrow[0:1, :],
            gate_b.rearrange("(b a) -> b a", b=1)), sem="smq"))
        sm_last = sm[-1]
        t_ag = E.op("ac", lambda a: a.activation(absgs[:], gsbuf[:], AF.Abs), deps=[sm[9]])

        with ExitStack() as es0:
            sb0 = lambda n, shape, dt=BF16: es0.enter_context(nc.sbuf_tensor(n, shape, dt))
            xbf = sb0("xbf", [128, S])
            xts = sb0("xts", [128, S])
            wtbs = [sb0("wtb0", [128, 2048]), sb0("wtb1", [128, 2048])]
            wtss = [sb0("wts0", [128, 2048]), sb0("wts1", [128, 2048])]
            psx = es0.enter_context(nc.psum_tensor("psx", [128, 1024], BF16))
            psw = es0.enter_context(nc.psum_tensor("psw", [128, 1024], BF16))

            t_xld = E.op("gp", lambda g: g.dma_start(xbf[:],
                x_rows.rearrange("(g p) m -> p g m", p=128)), sem="xbq")
            ev_last = None
            ev_all = []
            for g_i in range(16):
                tt = None
                for mb in range(8):
                    tt = E.op("pe", lambda t, g_i=g_i, mb=mb: t.transpose(
                        psx[:, mb * 128:(mb + 1) * 128],
                        xbf[:, g_i * 1024 + mb * 128: g_i * 1024 + (mb + 1) * 128],
                        identb[:]),
                        deps=[t_xld, cst, ev_last] if mb == 0 else (), inc=(mb == 7))
                ev_last = E.op("dv", lambda v, g_i=g_i: v.tensor_copy(
                    bass.AP(xts, g_i * 128, [[S, 128], [2048, 8], [1, 128]]),
                    psx[:, 0:1024]), deps=[tt])
                ev_all.append(ev_last)
            t_agxw = E.op("gp", lambda g: g.dma_start(
                agX_in.ap().rearrange("(b p) t -> p b t", p=128), xts[:]), deps=[ev_last],
                sem="agxq")
            ag1 = E.cc(lambda g: g.collective_compute(
                "AllGather", ALU.bypass, replica_groups=[list(range(NCORES))],
                ins=[agX_in.ap().opt()], outs=[agX_out.ap().opt()]), deps=[t_agxw])

            pieces = []
            for e in range(E_EXP):
                pieces.append((w1p[e, :, :], 16, e * 3 * PIECE))
                pieces.append((w3p[e, :, :], 16, e * 3 * PIECE + PIECE))
                pieces.append((w2p[e, :, 0:128], 8, e * 3 * PIECE + 2 * PIECE))
                pieces.append((w2p[e, :, 128:256], 8, e * 3 * PIECE + 2 * PIECE + 128 * 1024))
            base_sh = E_EXP * 3 * PIECE
            pieces.append((sw1p[:, :], 16, base_sh))
            pieces.append((sw3p[:, :], 16, base_sh + PIECE))
            pieces.append((sw2p[:, 0:128], 8, base_sh + 2 * PIECE))
            pieces.append((sw2p[:, 128:256], 8, base_sh + 2 * PIECE + 128 * 1024))

            prev_out = [None, None]
            prev_tt = [None, None]
            prev_wev = [None, None]
            last_wev = None
            for pidx, (src, nblk, coff) in enumerate(pieces):
                wpar = pidx % 2
                wtb = wtbs[wpar]
                wts = wtss[wpar]
                t_ld = E.op("gp", lambda g, src=src, nblk=nblk, wtb=wtb: g.dma_start(
                    wtb[:, 0:nblk * 128], src.rearrange("(k p) m -> p k m", p=128)),
                    deps=[prev_out[wpar], prev_tt[wpar]],
                    sem=("wldq" if wpar == 0 else "wldq1"))
                evs = []
                for half in range((nblk + 7) // 8):
                    kk0 = half * 8
                    kn = min(8, nblk - kk0)
                    tt = None
                    for k in range(kn):
                        tt = E.op("pe", lambda t, k=k, kk0=kk0, wtb=wtb: t.transpose(
                            psw[:, k * 128:(k + 1) * 128],
                            wtb[:, (kk0 + k) * 128:(kk0 + k + 1) * 128], identb[:]),
                            deps=[t_ld, cst, last_wev] if k == 0 else (), inc=(k == kn - 1))
                    last_wev = E.op("dv", lambda v, kk0=kk0, kn=kn, wts=wts: v.tensor_copy(
                        wts[:, kk0 * 128:(kk0 + kn) * 128], psw[:, 0:kn * 128]),
                        deps=[tt, prev_out[wpar]])
                    evs.append(last_wev)
                prev_tt[wpar] = tt
                prev_wev[wpar] = last_wev
                prev_out[wpar] = E.op("gp", lambda g, nblk=nblk, coff=coff, wts=wts: g.dma_start(
                    bass.AP(agW_in, coff, [[nblk * 128, 128], [1, nblk * 128]]),
                    wts[:, 0:nblk * 128]), deps=evs,
                    sem=("wstq" if wpar == 0 else "wstq1"))
            ag2 = None
            ag2 = E.cc(lambda g: g.collective_compute(
                "AllGather", ALU.bypass, replica_groups=[list(range(NCORES))],
                ins=[agW_in.ap().opt()], outs=[agW_out.ap().opt()]),
                deps=[prev_out[0], prev_out[1]])

            E.flush()

        if PHASE == "p0":
            _dummy_out(nc, E, es, y_out)
            return nc

        with ExitStack() as es1:
            sb1 = lambda n, shape, dt=F32: es1.enter_context(nc.sbuf_tensor(n, shape, dt))
            psf1 = lambda n, shape: es1.enter_context(nc.psum_tensor(n, shape, F32))
            psb1 = lambda n, shape: es1.enter_context(nc.psum_tensor(n, shape, BF16))

            Kch = sb1("Kch", [128, S], BF16)
            Qch = sb1("Qch", [128, S], BF16)
            Ktok = sb1("Ktok", [128, S], BF16)
            Vtok = sb1("Vtok", [128, S], BF16)
            obuf0 = sb1("obuf0", [128, 128], BF16)
            obuf1 = sb1("obuf1", [128, 128], BF16)
            xt0 = sb1("xt0", [128, 4096], BF16)
            xt1 = sb1("xt1", [128, 4096], BF16)
            wqb8 = sb1("wqb8", [128, 1024], BF16)
            wkb8 = sb1("wkb8", [128, 1024], BF16)
            wvb8 = sb1("wvb8", [128, 1024], BF16)
            wrq8 = sb1("wrq8", [128, 8 * 48], BF16)
            wfl8 = sb1("wfl8", [128, 8 * 4], BF16)
            wrq8b = sb1("wrq8b", [128, 8 * 48], BF16)
            wrk8 = sb1("wrk8", [128, 8 * 48], BF16)
            wrk8b = sb1("wrk8b", [128, 8 * 48], BF16)
            qw = sb1("qw", [128, 516], BF16)
            kw = sb1("kw", [128, 516], BF16)
            vw = sb1("vw", [128, 516], BF16)
            cq = sb1("cq", [128, 512], BF16)
            ck = sb1("ck", [128, 512], BF16)
            cv = sb1("cv", [128, 512], BF16)
            ktmp = sb1("ktmp", [128, 512], BF16)
            rqsb = sb1("rqsb", [128, 512], BF16)
            rksb = sb1("rksb", [128, 512], BF16)
            q2t = sb1("q2t", [128, 512], BF16)
            k2t = sb1("k2t", [128, 512], BF16)
            flst = sb1("flst", [4, 512], F32)
            rr1 = sb1("rr1", [2, 512], F32)
            rr2 = sb1("rr2", [2, 512], F32)
            rr3 = sb1("rr3", [2, 512], F32)

            triu_f = sb1("triu_f", [128, 128], F32)
            inclm = sb1("inclm", [128, 256], F32)
            strictm = sb1("strictm", [128, 256], F32)
            bdones = sb1("bdones", [128, 2], BF16)
            self2f = sb1("self2f", [2, 128], F32)
            sel48 = sb1("sel48", [2, 128], F32)
            i64two = sb1("i64two", [128, 64], F32)

            pP1 = psf1("pP1", [128, 512])
            pP2 = psf1("pP2", [128, 512])
            pRED = psf1("pRED", [128, 512])
            pBC1 = psf1("pBC1", [128, 512])
            pBC2 = psf1("pBC2", [128, 512])
            pT1 = psb1("pT1", [128, 1024])
            pFL = psf1("pFL", [128, 512])

            twq = E.op("gp", lambda g: g.dma_start(wqb8[:],
                wq_s.rearrange("(b p) c -> p b c", p=128)), sem="p1wq")
            twk = E.op("gp", lambda g: g.dma_start(wkb8[:],
                wk_s.rearrange("(b p) c -> p b c", p=128)), sem="p1wq")
            twv = E.op("gp", lambda g: g.dma_start(wvb8[:],
                wv_s.rearrange("(b p) c -> p b c", p=128)), sem="p1wq")
            tw1 = E.op("gp", lambda g: g.dma_start(
                bass.AP(wrq8, 0, [[8 * 48, 128], [48, 8], [1, 48]]),
                wrq_s.rearrange("(b p) c -> p b c", p=128)[:, :, 0:48]), sem="p1wq")
            tw2 = E.op("gp", lambda g: g.dma_start(
                bass.AP(wfl8, 0, [[8 * 4, 128], [4, 8], [1, 2]]),
                wf_s.rearrange("(b p) c -> p b c", p=128)), sem="p1wq")
            tw3 = E.op("gp", lambda g: g.dma_start(
                bass.AP(wfl8, 2, [[8 * 4, 128], [4, 8], [1, 2]]),
                wl_s.rearrange("(b p) c -> p b c", p=128)), sem="p1wq")
            tw4 = E.op("gp", lambda g: g.dma_start(
                bass.AP(wrq8b, 0, [[8 * 48, 128], [48, 8], [1, 48]]),
                wrq_s.rearrange("(b p) c -> p b c", p=128)[:, :, 48:96]), sem="p1wq")
            tw5 = E.op("gp", lambda g: g.dma_start(
                bass.AP(wrk8, 0, [[8 * 48, 128], [48, 8], [1, 48]]),
                wrk_s.rearrange("(b p) c -> p b c", p=128)[:, :, 0:48]), sem="p1wq")
            tw6 = E.op("gp", lambda g: g.dma_start(
                bass.AP(wrk8b, 0, [[8 * 48, 128], [48, 8], [1, 48]]),
                wrk_s.rearrange("(b p) c -> p b c", p=128)[:, :, 48:96]), sem="p1wq")

            cst1 = None
            for fn in [
                lambda g: g.memset(triu_f[:], 0.0),
                lambda g: g.affine_select(out=triu_f[:], in_=triu_f[:], compare_op=ALU.is_gt,
                                          fill=1.0, base=0, pattern=[[-1, 128]], channel_multiplier=1),
                lambda g: g.memset(inclm[:], 0.0),
                lambda g: g.affine_select(out=inclm[:, 0:128], in_=inclm[:, 0:128], compare_op=ALU.is_ge,
                                          fill=NEG, base=0, pattern=[[1, 128]], channel_multiplier=-1),
                lambda g: g.affine_select(out=inclm[:, 128:256], in_=inclm[:, 128:256], compare_op=ALU.is_ge,
                                          fill=NEG, base=0, pattern=[[1, 128]], channel_multiplier=-1),
                lambda g: g.memset(strictm[:], 0.0),
                lambda g: g.affine_select(out=strictm[:, 0:128], in_=strictm[:, 0:128], compare_op=ALU.is_gt,
                                          fill=NEG, base=0, pattern=[[1, 128]], channel_multiplier=-1),
                lambda g: g.affine_select(out=strictm[:, 128:256], in_=strictm[:, 128:256], compare_op=ALU.is_gt,
                                          fill=NEG, base=0, pattern=[[1, 128]], channel_multiplier=-1),
                lambda g: g.memset(bdones[:], 0.0),
                lambda g: g.memset(bdones[0:64, 0:1], 1.0),
                lambda g: g.memset(bdones[64:128, 1:2], 1.0),
                lambda g: g.memset(self2f[:], 1.0),
                lambda g: g.affine_select(out=self2f[:], in_=self2f[:], compare_op=ALU.is_ge,
                                          fill=0.0, base=0, pattern=[[1, 128]], channel_multiplier=-64),
                lambda g: g.affine_select(out=self2f[:], in_=self2f[:], compare_op=ALU.is_ge,
                                          fill=0.0, base=63, pattern=[[-1, 128]], channel_multiplier=64),
                lambda g: g.memset(sel48[:], 1.0),
                lambda g: g.affine_select(out=sel48[:], in_=sel48[:], compare_op=ALU.is_ge,
                                          fill=0.0, base=0, pattern=[[1, 128]], channel_multiplier=-64),
                lambda g: g.affine_select(out=sel48[:], in_=sel48[:], compare_op=ALU.is_ge,
                                          fill=0.0, base=47, pattern=[[-1, 128]], channel_multiplier=64),
            ]:
                cst1 = E.gpc(fn)
            t_i64a = E.op("dv", lambda v: v.tensor_add(i64two[0:64, :], identf[0:64, 0:64],
                identf[0:64, 64:128]), deps=[cst1])
            t_i64b = E.op("dv", lambda v: v.tensor_add(i64two[64:128, :], identf[64:128, 0:64],
                identf[64:128, 64:128]), deps=[t_i64a])
            t_lam0 = E.op("pe", lambda t: t.matmul(pRED[:, 0:1], sel48[:], absgs[:],
                start=True, stop=True), deps=[t_ag, cst1])
            t_lam = E.op("dv", lambda v: v.tensor_copy(lam_c[:], pRED[:, 0:1]), deps=[t_lam0])

            h0 = E.gpc(lambda g: g.memset(qw[:, 0:3], 0.0))
            h1 = E.gpc(lambda g: g.memset(kw[:, 0:3], 0.0))
            h2 = E.gpc(lambda g: g.memset(vw[:, 0:3], 0.0))

            xtok = [None, None]
            xt_rd = [None, None]
            halo_q, halo_k, halo_v = [h0], [h1], [h2]
            evQp = evKp = evVp = None
            fcq = fck = fcv = None
            fFL = None
            fq2 = fk2 = None
            fktmp = None
            fpRED = t_lam
            bc1_free = bc2_free = None
            trK = trV = None
            zstage = None
            rr_free = [None, None, None]

            for g_i in range(32):
                xt = xt0 if g_i % 2 == 0 else xt1
                t0g = g_i * 512
                rank = t0g // TPC
                toff = t0g % TPC
                gsl = slice(g_i * 512, (g_i + 1) * 512)
                xtok[g_i % 2] = E.op("sp", lambda s, rank=rank, toff=toff, xt=xt:
                    s.dma_start(bass.AP(xt, 0, [[4096, 128], [512, 8], [1, 512]]),
                                bass.AP(agX_out, rank * D * TPC + toff,
                                        [[TPC, 128], [128 * TPC, 8], [1, 512]])),
                    deps=[xtok[g_i % 2], xt_rd[g_i % 2], ag1, tw6],
                    sem=("xq0" if g_i % 2 == 0 else "xq1"))
                t_x = xtok[g_i % 2]

                def proj(bank, wtile, ncols, deps, pstart=True):
                    tk = None
                    for mb in range(8):
                        tk = E.op("pe", lambda t, mb=mb, bank=bank, wtile=wtile, ncols=ncols, xt=xt:
                            t.matmul(bank[0:ncols, 0:512],
                                     wtile[:, mb * ncols:(mb + 1) * ncols],
                                     xt[:, mb * 512:(mb + 1) * 512],
                                     start=(mb == 0 and pstart), stop=(mb == 7)),
                            deps=deps if mb == 0 else (), inc=(mb == 7))
                    return tk

                tQ = proj(pP1, wqb8, 128, [t_x, twq, evQp])
                tK = proj(pP2, wkb8, 128, [t_x, twk, evKp])
                evQ = E.op("ac", lambda a: a.activation(qw[:, 3:515], pP1[:, 0:512], AF.Copy),
                           deps=[tQ, halo_q[-1]])
                tV = proj(pP1, wvb8, 128, [evQ, twv])
                evK = E.op("ac", lambda a: a.activation(kw[:, 3:515], pP2[:, 0:512], AF.Copy),
                           deps=[tK, halo_k[-1]])
                tRa = None
                for mb in range(8):
                    tRa = E.op("pe", lambda t, mb=mb, xt=xt: t.matmul(
                        pP2[0:48, 0:512], wrq8[:, mb * 48:(mb + 1) * 48],
                        xt[:, mb * 512:(mb + 1) * 512], start=(mb == 0), stop=(mb == 7)),
                        deps=[evK, tw6] if mb == 0 else (), inc=False)
                for mb in range(8):
                    tRa = E.op("pe", lambda t, mb=mb, xt=xt: t.matmul(
                        pP2[64:112, 0:512], wrq8b[:, mb * 48:(mb + 1) * 48],
                        xt[:, mb * 512:(mb + 1) * 512], start=(mb == 0), stop=(mb == 7)),
                        deps=[tw6] if mb == 0 else (), inc=(mb == 7))
                tFL = None
                for mb in range(8):
                    tFL = E.op("pe", lambda t, mb=mb, xt=xt: t.matmul(
                        pFL[0:4, 0:512], wfl8[:, mb * 4:(mb + 1) * 4],
                        xt[:, mb * 512:(mb + 1) * 512], start=(mb == 0), stop=(mb == 7)),
                        deps=[tw6, fFL] if mb == 0 else (), inc=(mb == 7))
                evV = E.op("ac", lambda a: a.activation(vw[:, 3:515], pP1[:, 0:512], AF.Copy),
                           deps=[tV, halo_v[-1]])
                evR1 = E.op("dv", lambda v: v.tensor_copy(rqsb[:, 0:512], pP2[:, 0:512]),
                            deps=[tRa])
                evFL = E.op("dv", lambda v: v.tensor_scalar(flst[0:4, 0:512],
                    pFL[0:4, 0:512], bfbl4[0:4, 0:1], None, op0=ALU.add),
                    deps=[tFL, zstage, sm_last])
                fFL = evFL
                zstage = E.op("gp", lambda g, g_i=g_i: g.dma_start(
                    bass.AP(z_hbm, g_i * 512, [[S, 4], [1, 512]]), flst[0:4, :]),
                    deps=[evFL], sem="zq")
                tRk = None
                for mb in range(8):
                    tRk = E.op("pe", lambda t, mb=mb, xt=xt: t.matmul(
                        pP1[0:48, 0:512], wrk8[:, mb * 48:(mb + 1) * 48],
                        xt[:, mb * 512:(mb + 1) * 512], start=(mb == 0), stop=(mb == 7)),
                        deps=[evV, tw6] if mb == 0 else (), inc=False)
                for mb in range(8):
                    tRk = E.op("pe", lambda t, mb=mb, xt=xt: t.matmul(
                        pP1[64:112, 0:512], wrk8b[:, mb * 48:(mb + 1) * 48],
                        xt[:, mb * 512:(mb + 1) * 512], start=(mb == 0), stop=(mb == 7)),
                        deps=[tw6] if mb == 0 else (), inc=(mb == 7))
                evR2 = E.op("dv", lambda v: v.tensor_copy(rksb[:, 0:512], pP1[0:128, 0:512]),
                            deps=[tRk])
                xt_rd[g_i % 2] = tRk
                evQp = evR2
                evKp = [evR1, evFL]

                def conv(dst, src, wcol, dep0, dfree=None):
                    t1 = E.op("dv", lambda v: v.tensor_scalar(dst[:, 0:512], src[:, 0:512],
                        wcol[:, 0:1], None, op0=ALU.mult), deps=[dep0, sm_last, dfree])
                    for j in range(1, 4):
                        t1 = E.op("dv", lambda v, j=j: v.scalar_tensor_tensor(
                            dst[:, 0:512], src[:, j:512 + j], wcol[:, j:j + 1], dst[:, 0:512],
                            op0=ALU.mult, op1=ALU.add), deps=[t1])
                    return t1
                tcq = conv(cq, qw, cvq_c, evQ, fcq)
                tck = conv(ck, kw, cvk_c, evK, fck)
                tcv = conv(cv, vw, cvv_c, evV, fcv)
                halo_q.append(E.op("dv", lambda v: v.tensor_copy(qw[:, 0:3], qw[:, 512:515]),
                                   deps=[tcq]))
                halo_k.append(E.op("dv", lambda v: v.tensor_copy(kw[:, 0:3], kw[:, 512:515]),
                                   deps=[tck]))
                halo_v.append(E.op("dv", lambda v: v.tensor_copy(vw[:, 0:3], vw[:, 512:515]),
                                   deps=[tcv]))

                tq2 = E.op("ac", lambda a: a.activation(q2t[:], cq[:], AF.Square),
                           deps=[tcq, fq2])
                trq = E.op("pe", lambda t: t.matmul(pRED[0:2, 0:512], bdones[:], q2t[:],
                    start=True, stop=True), deps=[tq2, cst1, fpRED])
                tl1 = E.op("ac", lambda a: a.activation(rr1[:], pRED[0:2, 0:512], AF.Ln,
                    scale=1.0 / dH, bias=ceps6[0:2, 0:1]), deps=[trq, rr_free[0]])
                tl2 = E.op("ac", lambda a: a.activation(rr1[:], rr1[:], AF.Exp, scale=-0.5),
                           deps=[tl1])
                tbq = E.op("pe", lambda t: t.matmul(pBC1[:, 0:512], self2f[:], rr1[:],
                    start=True, stop=True), deps=[tl2, bc1_free])
                rr_free[0] = tbq
                tqa = E.op("dv", lambda v, gsl=gsl: v.scalar_tensor_tensor(
                    Qch[:, gsl], cq[:], qnw_c[:, 0:1], pBC1[:, 0:512],
                    op0=ALU.mult, op1=ALU.mult), deps=[tbq])
                tk2 = E.op("ac", lambda a: a.activation(k2t[:], ck[:], AF.Square),
                           deps=[tck, fk2])
                trk = E.op("pe", lambda t: t.matmul(pRED[32:34, 0:512], bdones[:], k2t[:],
                    start=True, stop=True), deps=[tk2, tl1], inc=True)
                tl3 = E.op("ac", lambda a: a.activation(rr2[:], pRED[32:34, 0:512], AF.Ln,
                    scale=1.0 / dH, bias=ceps6[0:2, 0:1]), deps=[trk, rr_free[1]])
                tl4 = E.op("ac", lambda a: a.activation(rr2[:], rr2[:], AF.Exp, scale=-0.5),
                           deps=[tl3])
                tbk = E.op("pe", lambda t: t.matmul(pBC2[:, 0:512], self2f[:], rr2[:],
                    start=True, stop=True), deps=[tl4, bc2_free])
                rr_free[1] = tbk
                tka = E.op("dv", lambda v: v.scalar_tensor_tensor(
                    ktmp[:], ck[:], knw_c[:, 0:1], pBC2[:, 0:512],
                    op0=ALU.mult, op1=ALU.mult), deps=[tbk, fktmp])
                bc2_free = tka
                tqb1 = E.op("dv", lambda v, gsl=gsl: v.scalar_tensor_tensor(
                    Qch[0:48, gsl], rqsb[0:48, :], lam_c[0:48, 0:1], Qch[0:48, gsl],
                    op0=ALU.mult, op1=ALU.add), deps=[tqa, evR1, t_lam])
                tqb2 = E.op("dv", lambda v, gsl=gsl: v.scalar_tensor_tensor(
                    Qch[64:112, gsl], rqsb[64:112, :], lam_c[64:112, 0:1], Qch[64:112, gsl],
                    op0=ALU.mult, op1=ALU.add), deps=[tqb1])
                tkb1 = E.op("dv", lambda v: v.scalar_tensor_tensor(
                    ktmp[0:48, :], rksb[0:48, :], lam_c[0:48, 0:1], ktmp[0:48, :],
                    op0=ALU.mult, op1=ALU.add), deps=[tka, evR2])
                tkb2 = E.op("dv", lambda v: v.scalar_tensor_tensor(
                    ktmp[64:112, :], rksb[64:112, :], lam_c[64:112, 0:1], ktmp[64:112, :],
                    op0=ALU.mult, op1=ALU.add), deps=[tkb1])
                tk2b = E.op("ac", lambda a: a.activation(k2t[:], ktmp[:], AF.Square),
                            deps=[tkb2, trk])
                trk2 = E.op("pe", lambda t: t.matmul(pRED[64:66, 0:512], bdones[:], k2t[:],
                    start=True, stop=True), deps=[tk2b, tl3], inc=True)
                tl5 = E.op("ac", lambda a: a.activation(rr3[:], pRED[64:66, 0:512], AF.Ln,
                    scale=1.0, bias=ceps24[0:2, 0:1]), deps=[trk2, rr_free[2]])
                tl6 = E.op("ac", lambda a: a.activation(rr3[:], rr3[:], AF.Exp, scale=-0.5),
                           deps=[tl5])
                tbk2 = E.op("pe", lambda t: t.matmul(pBC1[:, 0:512], self2f[:], rr3[:],
                    start=True, stop=True), deps=[tl6, tqa])
                rr_free[2] = tbk2
                tkn = E.op("dv", lambda v, gsl=gsl: v.tensor_mul(Kch[:, gsl], ktmp[:],
                    pBC1[:, 0:512]), deps=[tbk2])
                bc1_free = tkn
                ttk = None
                for j in range(4):
                    ttk = E.op("pe", lambda t, j=j, g_i=g_i: t.transpose(
                        pT1[:, j * 128:(j + 1) * 128],
                        Kch[:, g_i * 512 + j * 128: g_i * 512 + (j + 1) * 128], identb[:]),
                        deps=[tkn, trK, trV] if j == 0 else (), inc=False)
                for j in range(4):
                    ttk = E.op("pe", lambda t, j=j: t.transpose(
                        pT1[:, 512 + j * 128: 512 + (j + 1) * 128],
                        cv[:, j * 128:(j + 1) * 128], identb[:]),
                        deps=[tcv] if j == 0 else (), inc=(j == 3))
                trK = E.op("dv", lambda v, gsl=gsl: v.tensor_copy(Ktok[:, gsl], pT1[:, 0:512]),
                           deps=[ttk])
                trV = E.op("dv", lambda v, gsl=gsl: v.tensor_copy(Vtok[:, gsl],
                    pT1[:, 512:1024]), deps=[ttk])
                fcq = tq2
                fck = tk2
                fcv = ttk
                fq2 = trq
                fk2 = trk2
                fktmp = tk2b
                fpRED = tl5

            spc = sb1("spc", [128, 256], F32)
            spzlc = sb1("spzlc", [128, 256], F32)
            sp_t = sb1("sp_t", [128, 256], F32)
            spzl_t = sb1("spzl_t", [128, 256], F32)
            lcum_t = sb1("lcum_t", [128, 256], F32)
            lcum_c = sb1("lcum_c", [128, 256], F32)
            npos_c = sb1("npos_c", [128, 256], F32)
            b_t = sb1("b_t", [128, 256], F32)
            c2_t = sb1("c2_t", [128, 256], F32)
            d2_t = sb1("d2_t", [128, 256], F32)
            gC2 = sb1("gC2", [128, 128], F32)
            tmp_t = sb1("tmp_t", [128, 256], F32)
            rbufL = sb1("rbufL", [1, 256], F32)

            tr_sp = []
            for r, dst, hcol in ((0, spc, 0), (1, spc, 1), (2, spzlc, 0), (3, spzlc, 1)):
                tr_sp.append(E.op("gp", lambda g, r=r, dst=dst, hcol=hcol: g.dma_start(
                    dst[:, hcol * 128:(hcol + 1) * 128],
                    bass.AP(z_hbm, r * S, [[128, 128], [1, 128]])),
                    deps=[zstage]))
            t_spa = E.op("ac", lambda a: a.activation(spc[:], spc[:], AF.Exp, scale=-1.0),
                         deps=tr_sp)
            t_spb = E.op("ac", lambda a: a.activation(spc[:], spc[:], AF.Ln, bias=1.0),
                         deps=[t_spa])
            t_spc2 = E.op("ac", lambda a: a.activation(spzlc[:], spzlc[:], AF.Exp, scale=-1.0),
                          deps=[t_spb])
            t_sp2 = E.op("ac", lambda a: a.activation(spzlc[:], spzlc[:], AF.Ln, bias=1.0),
                         deps=[t_spc2])
            tt1 = E.op("pe", lambda t: t.transpose(pBC1[:, 0:128], spc[:, 0:128], identf[:]),
                       deps=[t_spb, bc1_free, tkn], inc=False)
            tt2 = E.op("pe", lambda t: t.transpose(pBC1[:, 128:256], spc[:, 128:256], identf[:]),
                       inc=True)
            t_spt = E.op("dv", lambda v: v.tensor_copy(sp_t[:], pBC1[:, 0:256]), deps=[tt2])
            tt3 = E.op("pe", lambda t: t.transpose(pBC1[:, 256:384], spzlc[:, 0:128], identf[:]),
                       deps=[t_spt, t_sp2], inc=False)
            tt4 = E.op("pe", lambda t: t.transpose(pBC1[:, 384:512], spzlc[:, 128:256], identf[:]),
                       inc=True)
            t_szt = E.op("dv", lambda v: v.tensor_copy(spzl_t[:], pBC1[:, 256:512]), deps=[tt4])
            t_lc0 = E.op("pe", lambda t: t.matmul(pBC2[:, 0:256], triu_f[:], sp_t[:],
                start=True, stop=True), deps=[t_spt, bc2_free, cst1])
            t_lct = E.op("dv", lambda v: v.tensor_copy(lcum_t[:], pBC2[:, 0:256]), deps=[t_lc0])
            tt5 = E.op("pe", lambda t: t.transpose(pBC1[:, 0:128], lcum_t[:, 0:128], identf[:]),
                       deps=[t_lct, t_spt, t_szt], inc=False)
            tt6 = E.op("pe", lambda t: t.transpose(pBC1[:, 128:256], lcum_t[:, 128:256], identf[:]),
                       inc=True)
            t_lcc = E.op("dv", lambda v: v.tensor_copy(lcum_c[:], pBC1[:, 0:256]), deps=[tt6])
            t_np1 = E.op("dv", lambda v: v.tensor_sub(npos_c[:], lcum_c[:], spc[:]), deps=[t_lcc])
            t_np2 = E.op("dv", lambda v: v.tensor_add(npos_c[:], npos_c[:], spzlc[:]),
                         deps=[t_np1])
            t_bt = E.op("ac", lambda a: a.activation(b_t[:], spzl_t[:], AF.Exp, scale=-1.0),
                        deps=[t_szt])
            t_c21 = E.op("dv", lambda v: v.tensor_sub(tmp_t[:], sp_t[:], spzl_t[:]),
                         deps=[t_szt, t_spt])
            t_c22 = E.op("dv", lambda v: v.tensor_sub(tmp_t[:], tmp_t[:], lcum_t[:]),
                         deps=[t_c21, t_lct])
            t_c2t = E.op("ac", lambda a: a.activation(c2_t[:], tmp_t[:], AF.Exp), deps=[t_c22])
            t_rb = E.op("sp", lambda s: s.dma_start(rbufL[0:1, :], lcum_t[127:128, 0:256]),
                        deps=[t_lct])
            t_bcL = E.op("pe", lambda t: t.matmul(pBC2[:, 0:256], ones1f[:], rbufL[0:1, :],
                start=True, stop=True), deps=[t_rb, t_lcc])
            t_d21 = E.op("dv", lambda v: v.tensor_sub(tmp_t[:], lcum_t[:], pBC2[:, 0:256]),
                         deps=[t_bcL, t_c2t])
            t_d2t = E.op("ac", lambda a: a.activation(d2_t[:], tmp_t[:], AF.Exp), deps=[t_d21])
            t_gcb = E.op("ac", lambda a: a.activation(tmp_t[:], pBC2[:, 0:256], AF.Exp,
                scale=-1.0), deps=[t_d21, t_d2t])
            t_gc1 = E.op("dv", lambda v: v.tensor_copy(gC2[0:64, :], tmp_t[0:64, 0:128]),
                         deps=[t_gcb])
            t_gc2 = E.op("dv", lambda v: v.tensor_copy(gC2[64:128, :], tmp_t[64:128, 128:256]),
                         deps=[t_gcb])
            if DEBUG:
                E.op("gp", lambda g: g.dma_start(dbg["d_lcum"], lcum_t[:]), deps=[t_lct])
                E.op("gp", lambda g: g.dma_start(dbg["d_bt"], b_t[:]), deps=[t_bt])
                E.op("gp", lambda g: g.dma_start(dbg["d_zpk"], spc[:]), deps=[t_spb])
                E.op("gp", lambda g: g.dma_start(dbg["d_Qch"], Qch[:]), deps=[tqb2])
                E.op("gp", lambda g: g.dma_start(dbg["d_Kch"], Kch[:]), deps=[tkn])
                E.op("gp", lambda g: g.dma_start(dbg["d_Ktok"], Ktok[:]), deps=[trK])
                E.op("gp", lambda g: g.dma_start(dbg["d_Vtok"], Vtok[:]), deps=[trV])
            E.flush()

            if PHASE == "p1":
                _dummy_out(nc, E, es, y_out)
                return nc

            NTs = sb1("NTs", [128, 256], BF16)
            Ms = sb1("Ms", [128, 256], BF16)
            Wsol = sb1("Wsol", [128, 256], BF16)
            EMT = sb1("EMT", [128, 256], BF16)
            expsb = sb1("expsb", [128, 512], BF16)
            gbsb = sb1("gbsb", [128, 256], BF16)
            argb = sb1("argb", [128, 256], F32)
            argb2 = sb1("argb2", [128, 256], F32)
            d2buf = sb1("d2buf", [128, 256], BF16)
            ptmp = sb1("ptmp", [128, 128], BF16)
            PTs = sb1("PTs", [128, 128], BF16)
            STf = sb1("STf", [128, 64], F32)
            STb = sb1("STb", [128, 64], BF16)
            Mch = sb1("Mch", [128, 64], F32)
            rbAs = [sb1(f"rbA{i}", [1, 256], F32) for i in range(2)]
            rbBs = [sb1(f"rbB{i}", [1, 256], F32) for i in range(2)]

            bkA, bkB, bkC, bkD, bkE = pP1, pP2, pRED, pBC1, pBC2
            bkFO = psf1("bkFO", [128, 512])
            bkF = bkFO
            pO = pFL
            bkT = pT1

            E.sep_fn = lambda t: t.ldweights(identb[:])
            t_s0 = E.gpc(lambda g: g.memset(STf[:], 0.0))
            t_s1 = E.gpc(lambda g: g.memset(STb[:], 0.0))
            tok_ST, tok_STb = t_s0, t_s1

            fA = fB = fC = fD = fE = fF = fO = fT = None
            fNT = fMs = fW = fEMT = fgb = farg = [None]
            farg_e = farg_n = fexp_e = fexp_n = None
            fd2 = fpt = fmc = None
            rb_free = [None] * 2
            ow_tok = [None, None]
            t_oev = None

            for c in range(NCH_RUN):
                par = c % 2
                rbA = rbAs[par]
                rbB = rbBs[par]
                csl = slice(c * 128, (c + 1) * 128)
                if SCAN_PART != "g2":
                    t_rba = E.op("sp", lambda s, c=c, rbA=rbA: s.dma_start(rbA[0:1, :],
                        lcum_c[c:c + 1, 0:256]), deps=[t_lcc, rb_free[par]],
                        sem=f"rbq{par}")
                    t_rbb = E.op("sp", lambda s, c=c, rbB=rbB: s.dma_start(rbB[0:1, :],
                        npos_c[c:c + 1, 0:256]), deps=[t_np2], sem=f"rbq{par}")
                else:
                    t_rba = t_rbb = None
                tG = None
                glist = [] if SCAN_PART == "g3" else [
                    (Kch, Kch, 0), (Kch, Kch, 128), (Kch, Qch, 256), (Kch, Qch, 384)]
                for i, (srcL, srcR, col) in enumerate(glist):
                    h = i % 2
                    hsl = slice(h * 64, (h + 1) * 64)
                    tG = E.op("pe", lambda t, srcL=srcL, srcR=srcR, col=col, hsl=hsl, csl=csl:
                        t.matmul(bkA[:, col:col + 128], srcL[hsl, csl], srcR[hsl, csl],
                                 start=True, stop=True),
                        deps=[fA, tkn, tqb2] if i == 0 else (), inc=(i == 3),
                        rows=(h * 64, 64))
                tBC = None
                for i in range(4 if SCAN_PART != "g2" else 0):
                    h = i % 2
                    rb = rbA if i < 2 else rbB
                    col = i * 128
                    tBC = E.op("pe", lambda t, h=h, rb=rb, col=col: t.matmul(
                        bkB[:, col:col + 128], ones1f[:], rb[0:1, h * 128:(h + 1) * 128],
                        start=True, stop=True),
                        deps=[t_rba, t_rbb, fB] if i == 0 else (), inc=(i == 3),
                        rows=(0, 1))
                if SCAN_PART in ("g", "g2", "g3"):
                    fO = tG if tG else tBC
                    ow_tok[par] = tBC if tBC else tG
                    fA = tG
                    fB = tBC
                    continue
                t_ae = E.op("dv", lambda v: v.tensor_sub(argb[:], inclm[:], bkB[:, 0:256]),
                            deps=[tBC, farg_e])
                t_an = E.op("dv", lambda v: v.tensor_sub(argb2[:], strictm[:], bkB[:, 256:512]),
                            deps=[tBC, farg_n])
                t_e = None
                for h in range(2):
                    t_e = E.op("ac", lambda a, h=h, c=c: a.activation(
                        expsb[:, h * 128:(h + 1) * 128], argb[:, h * 128:(h + 1) * 128],
                        AF.Exp, bias=lcum_t[:, h * 128 + c: h * 128 + c + 1]),
                        deps=[t_ae, fexp_e] if h == 0 else (), inc=(h == 1))
                farg_e = t_e
                t_en = None
                for h in range(2):
                    t_en = E.op("ac", lambda a, h=h, c=c: a.activation(
                        expsb[:, 256 + h * 128:256 + (h + 1) * 128],
                        argb2[:, h * 128:(h + 1) * 128], AF.Exp,
                        bias=lcum_t[:, h * 128 + c: h * 128 + c + 1]),
                        deps=[t_an, fexp_n] if h == 0 else (), inc=(h == 1))
                farg_n = t_en
                t_gb = E.op("ac", lambda a: a.activation(gbsb[:], bkB[:, 0:256], AF.Exp,
                    scale=-1.0), deps=[tBC, fgb])
                fB = [t_ae, t_an, t_gb]
                if SCAN_PART == "arg":
                    fO = t_gb
                    ow_tok[par] = t_gb
                    fA = tG
                    fexp_e = t_e
                    fexp_n = t_en
                    continue
                t_emt = E.op("dv", lambda v: v.tensor_mul(EMT[:], bkA[:, 256:512],
                    expsb[:, 0:256]), deps=[tG, t_e, fEMT])
                fexp_e = t_emt
                t_nt = E.op("dv", lambda v: v.scalar_tensor_tensor(NTs[:], bkA[:, 0:256],
                    -1.0, expsb[:, 256:512], op0=ALU.mult, op1=ALU.mult),
                    deps=[tG, t_en, fNT])
                fexp_n = t_nt
                fA = t_nt
                t_w = E.op("dv", lambda v, c=c, csl=csl: v.tensor_scalar(Wsol[:, 0:64],
                    Vtok[:, csl][:, 0:64], b_t[:, c:c + 1], None, op0=ALU.mult),
                    deps=[t_bt, fW, trV])
                t_w = E.op("dv", lambda v, c=c, csl=csl: v.tensor_scalar(Wsol[:, 128:192],
                    Vtok[:, csl][:, 64:128], b_t[:, 128 + c:129 + c], None, op0=ALU.mult),
                    deps=[t_w])
                t_w = E.op("dv", lambda v, c=c, csl=csl: v.tensor_scalar(Wsol[:, 64:128],
                    Ktok[:, csl][:, 0:64], c2_t[:, c:c + 1], None, op0=ALU.mult),
                    deps=[t_w, t_c2t, trK])
                t_w = E.op("dv", lambda v, c=c, csl=csl: v.tensor_scalar(Wsol[:, 192:256],
                    Ktok[:, csl][:, 64:128], c2_t[:, 128 + c:129 + c], None, op0=ALU.mult),
                    deps=[t_w])
                tok_W = t_w
                if SCAN_PART == "rhs":
                    fO = t_w
                    ow_tok[par] = t_w
                    fW = t_w
                    continue
                t_m0 = None
                for h in range(2):
                    t_m0 = E.op("pe", lambda t, h=h: t.transpose(
                        bkT[:, h * 128:(h + 1) * 128], NTs[:, h * 128:(h + 1) * 128], identb[:]),
                        deps=[t_nt, fT] if h == 0 else (), inc=(h == 1))
                tok_Ms = E.op("dv", lambda v: v.tensor_copy(Ms[:], bkT[:, 0:256]),
                              deps=[t_m0, fMs])
                fT = tok_Ms
                tok_NT = t_nt
                if SCAN_PART == "pro":
                    fO = tok_Ms
                    ow_tok[par] = tok_Ms
                    continue

                tdw = None
                for j in range(7):
                    old_NT = tok_NT
                    if j < 6:
                        tsq = None
                        for h in range(2):
                            tsq = E.op("pe", lambda t, h=h: t.matmul(
                                bkD[:, h * 128:(h + 1) * 128], NTs[:, h * 128:(h + 1) * 128],
                                Ms[:, h * 128:(h + 1) * 128], start=True, stop=True),
                                deps=[tok_Ms, tok_NT, fD] if h == 0 else (), inc=False)
                        for h in range(2):
                            tsq = E.op("pe", lambda t, h=h: t.matmul(
                                bkD[:, 256 + h * 128:256 + (h + 1) * 128],
                                Ms[:, h * 128:(h + 1) * 128], NTs[:, h * 128:(h + 1) * 128],
                                start=True, stop=True), deps=(), inc=(h == 1))
                    for h in range(2):
                        tdw = E.op("pe", lambda t, h=h: t.matmul(
                            bkC[:, h * 128:(h + 1) * 128], NTs[:, h * 128:(h + 1) * 128],
                            Wsol[:, h * 128:(h + 1) * 128], start=True, stop=True),
                            deps=[old_NT, tok_W, fC] if h == 0 else (), inc=(h == 1))
                    if j < 6:
                        tok_Ms = E.op("ac", lambda a: a.activation(Ms[:], bkD[:, 0:256],
                            AF.Copy), deps=[tsq])
                        tok_NT = E.op("ac", lambda a: a.activation(NTs[:], bkD[:, 256:512],
                            AF.Copy), deps=[tsq, tdw])
                        fD = tok_NT
                    tok_W = E.op("dv", lambda v: v.tensor_add(Wsol[:], Wsol[:], bkC[:, 0:256]),
                                 deps=[tdw, tok_W])
                    fC = tok_W
                fNT = tdw
                fMs = tok_Ms
                if DEBUG and c == 0:
                    E.op("gp", lambda g: g.dma_start(dbg["d_W"], Wsol[:]), deps=[tok_W])
                    E.op("gp", lambda g: g.dma_start(dbg["d_EMT"], EMT[:]), deps=[t_emt])
                    E.op("gp", lambda g: g.dma_start(dbg["d_NT"], NTs[:]), deps=[t_nt])
                    E.op("gp", lambda g: g.dma_start(dbg["d_exp"], expsb[:]), deps=[t_en, t_e])
                if SCAN_PART == "solve":
                    fO = tok_W
                    ow_tok[par] = tok_W
                    continue

                t_d = E.op("dv", lambda v, c=c: v.tensor_scalar(d2buf[:, 0:64], Wsol[:, 0:64],
                    d2_t[:, c:c + 1], None, op0=ALU.mult), deps=[tok_W, t_d2t, fd2])
                t_d = E.op("dv", lambda v, c=c: v.tensor_scalar(d2buf[:, 64:128],
                    Wsol[:, 128:192], d2_t[:, 128 + c:129 + c], None, op0=ALU.mult),
                    deps=[t_d])
                t_d = E.op("dv", lambda v, c=c, csl=csl: v.tensor_scalar(d2buf[:, 128:192],
                    Ktok[:, csl][:, 0:64], d2_t[:, c:c + 1], None, op0=ALU.mult), deps=[t_d])
                t_d = E.op("dv", lambda v, c=c, csl=csl: v.tensor_scalar(d2buf[:, 192:256],
                    Ktok[:, csl][:, 64:128], d2_t[:, 128 + c:129 + c], None, op0=ALU.mult),
                    deps=[t_d])
                tG2 = None
                for h in range(2):
                    tG2 = E.op("pe", lambda t, h=h: t.matmul(
                        bkE[h * 64:(h + 1) * 64, 0:64], Wsol[:, h * 128 + 64:h * 128 + 128],
                        d2buf[:, 128 + h * 64:128 + (h + 1) * 64], start=True, stop=True),
                        deps=[t_d, fE] if h == 0 else (), inc=(h == 1))
                tZq = None
                for h in range(2):
                    tZq = E.op("pe", lambda t, h=h: t.matmul(
                        bkE[h * 64:(h + 1) * 64, 64:192], Wsol[:, h * 128 + 64:h * 128 + 128],
                        EMT[:, h * 128:(h + 1) * 128], start=True, stop=True),
                        deps=[t_emt] if h == 0 else (), inc=(h == 1))

                t_mc = E.op("dv", lambda v, c=c: v.scalar_tensor_tensor(Mch[:], i64two[:],
                    gC2[:, c:c + 1], bkE[:, 0:64], op0=ALU.mult, op1=ALU.subtract),
                    deps=[tG2, tZq, t_gc1, t_gc2, t_i64b, fmc])
                t_p = E.op("dv", lambda v, csl=csl: v.tensor_mul(ptmp[0:64, :],
                    Qch[0:64, csl], gbsb[0:64, 0:128]), deps=[t_gb, fpt])
                t_p = E.op("dv", lambda v, csl=csl: v.tensor_mul(ptmp[64:128, :],
                    Qch[64:128, csl], gbsb[64:128, 128:256]), deps=[t_p])
                t_pt = E.op("dv", lambda v: v.tensor_sub(PTs[:], ptmp[:], bkE[:, 64:192]),
                            deps=[t_p, tZq])
                fgb = t_pt
                fE = t_pt
                if SCAN_PART == "epi1":
                    fO = t_pt
                    ow_tok[par] = t_pt
                    fF = [t_mc]
                    fd2 = tG1
                    fmc = t_mc
                    continue
                tch = None
                tG1 = None
                for h in range(2):
                    hsl = slice(h * 64, (h + 1) * 64)
                    tG1 = E.op("pe", lambda t, h=h, hsl=hsl, csl=csl: t.matmul(
                        bkF[hsl, 0:64], Ktok[:, csl][:, h * 64:(h + 1) * 64],
                        d2buf[:, h * 64:(h + 1) * 64], start=True, stop=False),
                        deps=[fF, t_mc, tok_ST] if h == 0 else (), inc=False)
                    tch = E.op("pe", lambda t, hsl=hsl: t.matmul(
                        bkF[hsl, 0:64], Mch[hsl, :], STf[hsl, :], start=False, stop=True),
                        deps=(), inc=(h == 1), rows=(h * 64, 64))
                fmc = tch
                fd2 = tch
                if DEBUG and c == 0:
                    E.op("gp", lambda g: g.dma_start(dbg["d_STb"], STb[:]), deps=[tok_STb, t_pt])
                    E.op("gp", lambda g: g.dma_start(dbg["d_PT"], PTs[:]), deps=[t_pt])
                tout = None
                for h in range(2):
                    hsl = slice(h * 64, (h + 1) * 64)
                    tout = E.op("pe", lambda t, h=h, hsl=hsl: t.matmul(
                        pO[hsl, 0:128], STb[hsl, :], PTs[hsl, :],
                        start=True, stop=False),
                        deps=[t_pt, tok_STb, fO] if h == 0 else (), inc=False,
                        rows=(h * 64, 64))
                    tout = E.op("pe", lambda t, h=h, hsl=hsl: t.matmul(
                        pO[hsl, 0:128], Wsol[:, h * 128:h * 128 + 64],
                        EMT[:, h * 128:(h + 1) * 128], start=False, stop=True),
                        deps=(), inc=(h == 1))
                fEMT = tout
                fW = tout
                fpt = tout
                obuf = obuf0 if par == 0 else obuf1
                t_oev = E.op("dv", lambda v, obuf=obuf: v.tensor_copy(obuf[:],
                    pO[:, 0:128]), deps=[tout, ow_tok[par]])
                rdest = c // 16
                ow_tok[par] = E.op("gp", lambda g, c=c, rdest=rdest, obuf=obuf: g.dma_start(
                    bass.AP(agO_in, rdest * 128 * TPC + (c % 16) * 128,
                            [[TPC, 128], [1, 128]]), obuf[:]), deps=[t_oev], sem="owq")
                if DEBUG:
                    ow_tok[par] = E.op("gp", lambda g, csl=csl, obuf=obuf: g.dma_start(
                        dbg["d_out"][:, csl], obuf[:]), deps=[t_oev], sem="owq")
                tok_ST = E.op("dv", lambda v: v.tensor_copy(STf[:], bkF[:, 0:64]),
                              deps=[tch, tout])
                tok_STb = E.op("ac", lambda a: a.activation(STb[:], bkF[:, 0:64], AF.Copy),
                               deps=[tch, tout])
                fF = [tok_ST, tok_STb]
                fO = t_oev
                rb_free[par] = tBC

            def a2a(g):
                with allow_ici_all_to_all():
                    return g.collective_compute(
                        "AllToAll", ALU.bypass, replica_groups=[list(range(NCORES))],
                        ins=[agO_in.ap().opt()], outs=[agO_out.ap().opt()])
            if PHASE == "p2x":
                ag3 = None
            else:
                ag3 = E.cc(a2a, deps=[ow_tok[0], ow_tok[1]])
            E.flush()

        E.sep_fn = None
        if PHASE in ("p2", "p2x"):
            _dummy_out(nc, E, es, y_out)
            return nc

        with ExitStack() as es4:
            sb4 = lambda n, shape, dt=F32: es4.enter_context(nc.sbuf_tensor(n, shape, dt))

            x2Thi = sb4("x2Thi", [128, 8 * TPC], BF16)
            comb = sb4("comb", [128, 128], F32)
            tg_last = None

            with ExitStack() as es4b:
                sb4b = lambda n, shape, dt=F32: es4b.enter_context(nc.sbuf_tensor(n, shape, dt))
                psf4 = lambda n, shape: es4b.enter_context(nc.psum_tensor(n, shape, F32))
                psb4 = lambda n, shape: es4b.enter_context(nc.psum_tensor(n, shape, BF16))
                OUTSl = sb4b("OUTSl", [128, 8 * TPC], BF16)
                x2Tlo = sb4b("x2Tlo", [128, 8 * TPC], BF16)
                wob = sb4b("wob", [128, 8 * 1024], BF16)
                normed = sb4b("normed", [128, 4096], BF16)
                sqb = sb4b("sqb", [128, 512], BF16)
                rstd4 = sb4b("rstd4", [1, 512], F32)
                xrow = sb4b("xrow", [128, 1024], F32)
                x2t4 = sb4b("x2t4", [128, 1024], F32)
                x2bf = sb4b("x2bf", [128, 1024], BF16)
                x2lo = sb4b("x2lo", [128, 1024], BF16)
                lnc1 = sb4b("lnc1", [128, 1], F32)
                lnc2 = sb4b("lnc2", [128, 1], F32)
                lnsq = sb4b("lnsq", [128, 1024], F32)
                gwf = sb4b("gwf", [128, 64], F32)
                gwhi = sb4b("gwhi", [128, 64], BF16)
                gwlo = sb4b("gwlo", [128, 64], BF16)
                gbb = sb4b("gbb", [128, 8], F32)
                glog = sb4b("glog", [128, 8], F32)
                gsc = sb4b("gsc", [128, 8], F32)
                gtmp = sb4b("gtmp", [128, 8], F32)
                gm = sb4b("gm", [128, 1], F32)
                gs2 = sb4b("gs2", [128, 1], F32)
                gwsb = sb4b("gwsb", [8, 1024], F32)

                pM1 = psf4("pM1", [128, 1024])
                pR4 = psf4("pR4", [128, 512])
                pB4 = psf4("pB4", [128, 512])
                pG4 = psf4("pG4", [128, 512])
                pT4 = psb4("pT4", [128, 1024])
                pT5 = psb4("pT5", [128, 1024])

                t_og = []
                for r in range(NCORES):
                    t_og.append(E.op("sp", lambda s, r=r: s.dma_start(
                        OUTSl[:, r * TPC:(r + 1) * TPC],
                        bass.AP(agO_out, r * 128 * TPC, [[TPC, 128], [1, TPC]])),
                        deps=[ag3], sem="oq"))
                t_wo = E.op("gp", lambda g: g.dma_start(wob[:],
                    wo.rearrange("(b p) c -> p b c", p=128)), sem="wobq")
                t_gw = E.op("gp", lambda g: g.dma_start(gwsb[:], gate_w), sem="g4q")
                t_gt = None
                for mb in range(8):
                    t_gt = E.op("pe", lambda t, mb=mb: t.transpose(
                        pG4[:, mb * 8:(mb + 1) * 8], gwsb[0:8, mb * 128:(mb + 1) * 128],
                        identf[0:8, 0:8]), deps=[t_gw] if mb == 0 else (), inc=(mb == 7))
                t_gwf = E.op("dv", lambda v: v.tensor_copy(gwf[:], pG4[:, 0:64]), deps=[t_gt])
                t_ghi = E.op("dv", lambda v: v.tensor_copy(gwhi[:], gwf[:]), deps=[t_gwf])
                t_glo = E.op("dv", lambda v: v.scalar_tensor_tensor(gwlo[:], gwhi[:], -1.0,
                    gwf[:], op0=ALU.mult, op1=ALU.add), deps=[t_ghi])
                t_gbb0 = E.op("pe", lambda t: t.matmul(pG4[:, 64:72], ones1f[:], gbrow[0:1, :],
                    start=True, stop=True), deps=[t_gt, sm_last], rows=(0, 1))
                t_gbbc = E.op("dv", lambda v: v.tensor_copy(gbb[:], pG4[:, 64:72]),
                              deps=[t_gbb0])

                fR4 = fB4 = None
                fM = [None, None]
                fth1 = fxbf = fxlo = fg3 = None
                fT4 = fT5 = None
                fnorm = None
                fx2 = None
                fglog = None
                frstd = None
                fsqb = None
                x2w_toks = []

                for tt in range(4):
                    tred = None
                    for mb in range(8):
                        tqq = E.op("ac", lambda a, mb=mb, tt=tt: a.activation(sqb[:],
                            OUTSl[:, mb * TPC + tt * 512: mb * TPC + (tt + 1) * 512],
                            AF.Square), deps=[t_og[-1], fsqb])
                        tred = E.op("pe", lambda t, mb=mb: t.matmul(pR4[0:1, 0:512],
                            onesvb[:], sqb[:], start=(mb == 0), stop=(mb == 7)),
                            deps=[tqq, fR4] if mb == 0 else [tqq], inc=True)
                        fsqb = tred
                    if DEBUG and tt == 0:
                        dsq = sb4b("dsq", [1, 512], F32)
                        tdsq = E.op("dv", lambda v: v.tensor_copy(dsq[0:1, :], pR4[0:1, 0:512]),
                                    deps=[tred])
                        E.op("gp", lambda g: g.dma_start(dbg["d_sqs"], dsq[0:1, :]), deps=[tdsq])
                    tln = E.op("ac", lambda a: a.activation(rstd4[0:1, :], pR4[0:1, 0:512],
                        AF.Ln, scale=1.0 / D, bias=ceps6[0:1, 0:1]), deps=[tred, frstd, (tdsq if (DEBUG and tt == 0) else None)])
                    tex = E.op("ac", lambda a: a.activation(rstd4[0:1, :], rstd4[0:1, :],
                        AF.Exp, scale=-0.5), deps=[tln])
                    if DEBUG and tt == 0:
                        E.op("gp", lambda g: g.dma_start(dbg["d_rstd"], rstd4[0:1, :]), deps=[tex])
                    fR4 = [tex, (tdsq if (DEBUG and tt == 0) else None)]
                    tbc = E.op("pe", lambda t: t.matmul(pB4[:, 0:512], ones1f[:],
                        rstd4[0:1, :], start=True, stop=True), deps=[tex, fB4])
                    frstd = tbc
                    tnm = None
                    for mb in range(8):
                        tnm = E.op("dv", lambda v, mb=mb, tt=tt: v.scalar_tensor_tensor(
                            normed[:, mb * 512:(mb + 1) * 512],
                            OUTSl[:, mb * TPC + tt * 512: mb * TPC + (tt + 1) * 512],
                            onw_c[:, mb:mb + 1], pB4[:, 0:512], op0=ALU.mult, op1=ALU.mult),
                            deps=[tbc, fnorm] if mb == 0 else (), inc=(mb == 7))
                    fB4 = tnm

                    for ts_i in range(4):
                        gt = tt * 4 + ts_i
                        tmix = None
                        for ch in range(2):
                            for mb in range(8):
                                tmix = E.op("pe", lambda t, ch=ch, mb=mb, ts_i=ts_i:
                                    t.matmul(pM1[:, ch * 512:(ch + 1) * 512],
                                             normed[:, mb * 512 + ts_i * 128: mb * 512 + (ts_i + 1) * 128],
                                             wob[:, mb * 1024 + ch * 512: mb * 1024 + (ch + 1) * 512],
                                             start=(mb == 0), stop=(mb == 7)),
                                    deps=[tnm, t_wo, fM[ch]] if mb == 0 else (),
                                    inc=(mb == 7))
                                fmix = tmix
                        fnorm = tmix
                        txl = E.op("sp", lambda s, gt=gt: s.dma_start(xrow[:],
                            x_rows[gt * 128:(gt + 1) * 128, :]), deps=[fx2], sem="xrq")
                        ta = E.op("dv", lambda v: v.tensor_add(x2t4[:, 0:512], xrow[:, 0:512],
                            pM1[:, 0:512]), deps=[tmix, txl, fth1])
                        fM[0] = ta
                        ta = E.op("dv", lambda v: v.tensor_add(x2t4[:, 512:1024],
                            xrow[:, 512:1024], pM1[:, 512:1024]), deps=[ta])
                        fM[1] = ta
                        u1 = E.op("dv", lambda v: v.tensor_reduce(lnc1[:, 0:1], x2t4[:],
                            axis=mybir.AxisListType.X, op=ALU.add, negate=True), deps=[ta])
                        u2 = E.op("dv", lambda v: v.tensor_scalar_mul(lnc1[:, 0:1],
                            lnc1[:, 0:1], 1.0 / D), deps=[u1])
                        u3 = E.op("dv", lambda v: v.tensor_scalar(x2t4[:], x2t4[:],
                            lnc1[:, 0:1], None, op0=ALU.add), deps=[u2])
                        u4 = E.op("ac", lambda a: a.activation(lnsq[:], x2t4[:], AF.Square),
                                  deps=[u3])
                        u5 = E.op("dv", lambda v: v.tensor_reduce(lnc2[:, 0:1], lnsq[:],
                            axis=mybir.AxisListType.X, op=ALU.add), deps=[u4])
                        u6 = E.op("ac", lambda a: a.activation(lnc2[:, 0:1], lnc2[:, 0:1],
                            AF.Ln, scale=1.0 / D, bias=ceps5[:, 0:1]), deps=[u5])
                        u7 = E.op("ac", lambda a: a.activation(lnc2[:, 0:1], lnc2[:, 0:1],
                            AF.Exp, scale=-0.5), deps=[u6])
                        u8 = E.op("dv", lambda v: v.tensor_scalar(x2t4[:], x2t4[:],
                            lnc2[:, 0:1], None, op0=ALU.mult), deps=[u7])
                        fx2 = E.op("sp", lambda s, gt=gt: s.dma_start(
                            x2_scr.ap()[gt * 128:(gt + 1) * 128, :], x2t4[:]), deps=[u8],
                            sem="x2wq")
                        x2w_toks.append(fx2)
                        th1 = E.op("ac", lambda a: a.activation(x2bf[:], x2t4[:], AF.Copy),
                                   deps=[u8, fxbf])
                        th2 = E.op("dv", lambda v: v.scalar_tensor_tensor(x2lo[:], x2bf[:],
                            -1.0, x2t4[:], op0=ALU.mult, op1=ALU.add), deps=[th1, fxlo])
                        tq1 = None
                        for mb in range(8):
                            tq1 = E.op("pe", lambda t, mb=mb: t.transpose(
                                pT4[:, mb * 128:(mb + 1) * 128],
                                x2bf[:, mb * 128:(mb + 1) * 128], identb[:]),
                                deps=[th1, fT4] if mb == 0 else (), inc=(mb == 7))
                        fT4 = E.op("dv", lambda v, gt=gt: v.tensor_copy(
                            bass.AP(x2Thi, gt * 128, [[8 * TPC, 128], [TPC, 8], [1, 128]]),
                            pT4[:, 0:1024]), deps=[tq1])
                        hi_ev = fT4
                        fth1 = th1
                        fxbf = tq1
                        tq2 = None
                        for mb in range(8):
                            tq2 = E.op("pe", lambda t, mb=mb: t.transpose(
                                pT5[:, mb * 128:(mb + 1) * 128],
                                x2lo[:, mb * 128:(mb + 1) * 128], identb[:]),
                                deps=[th2, fT5] if mb == 0 else (), inc=(mb == 7))
                        fT5 = E.op("dv", lambda v, gt=gt: v.tensor_copy(
                            bass.AP(x2Tlo, gt * 128, [[8 * TPC, 128], [TPC, 8], [1, 128]]),
                            pT5[:, 0:1024]), deps=[tq2])
                        lo_ev = fT5
                        fxlo = tq2
                        tgl = None
                        for mb in range(8):
                            tgl = E.op("pe", lambda t, mb=mb, gt=gt: t.matmul(
                                pG4[:, 128:136],
                                bass.AP(x2Thi, mb * TPC + gt * 128, [[8 * TPC, 128], [1, 128]]),
                                gwhi[:, mb * 8:(mb + 1) * 8], start=(mb == 0), stop=False),
                                deps=[hi_ev, t_ghi, t_gbbc, fglog] if mb == 0 else (), inc=False)
                        for mb in range(8):
                            tgl = E.op("pe", lambda t, mb=mb, gt=gt: t.matmul(
                                pG4[:, 128:136],
                                bass.AP(x2Thi, mb * TPC + gt * 128, [[8 * TPC, 128], [1, 128]]),
                                gwlo[:, mb * 8:(mb + 1) * 8], start=False, stop=False),
                                deps=[t_glo] if mb == 0 else (), inc=False)
                        for mb in range(8):
                            tgl = E.op("pe", lambda t, mb=mb, gt=gt: t.matmul(
                                pG4[:, 128:136],
                                bass.AP(x2Tlo, mb * TPC + gt * 128, [[8 * TPC, 128], [1, 128]]),
                                gwhi[:, mb * 8:(mb + 1) * 8], start=False, stop=(mb == 7)),
                                deps=[lo_ev] if mb == 0 else (), inc=(mb == 7))
                        tg1 = E.op("dv", lambda v: v.tensor_copy(glog[:], pG4[:, 128:136]),
                                   deps=[tgl, fg3])
                        fglog = tg1
                        tg2 = E.op("dv", lambda v: v.tensor_reduce(gm[:, 0:1], glog[:],
                            axis=mybir.AxisListType.X, op=ALU.max, negate=True), deps=[tg1])
                        tg3 = E.op("ac", lambda a: a.activation(gsc[:], glog[:], AF.Exp,
                            bias=gm[:, 0:1]), deps=[tg2])
                        fg3 = tg3
                        tg4 = E.op("dv", lambda v: v.tensor_reduce(gm[:, 0:1], gsc[:],
                            axis=mybir.AxisListType.X, op=ALU.add), deps=[tg3])
                        tg5 = E.op("dv", lambda v: v.reciprocal(gs2[:, 0:1], gm[:, 0:1]),
                                   deps=[tg4])
                        tg6 = E.op("dv", lambda v: v.tensor_scalar(gsc[:], gsc[:],
                            gs2[:, 0:1], None, op0=ALU.mult), deps=[tg5])
                        tg7 = E.op("dv", lambda v: v.tensor_add(gtmp[:], gsc[:], gbb[:]),
                                   deps=[tg6])
                        tg8 = E.op("dv", lambda v: v.tensor_reduce(gm[:, 0:1], gtmp[:],
                            axis=mybir.AxisListType.X, op=ALU.max), deps=[tg7])
                        tg9 = E.op("dv", lambda v: v.tensor_scalar(glog[:], gtmp[:],
                            gm[:, 0:1], None, op0=ALU.is_ge), deps=[tg8])
                        tg10 = E.op("dv", lambda v: v.scalar_tensor_tensor(glog[:], glog[:],
                            NEG, gtmp[:], op0=ALU.mult, op1=ALU.add), deps=[tg9])
                        tg11 = E.op("dv", lambda v: v.tensor_reduce(gm[:, 0:1], glog[:],
                            axis=mybir.AxisListType.X, op=ALU.max), deps=[tg10])
                        tg12 = E.op("dv", lambda v: v.tensor_scalar(gtmp[:], gtmp[:],
                            gm[:, 0:1], None, op0=ALU.is_ge), deps=[tg11])
                        tg13 = E.op("dv", lambda v, gt=gt: v.tensor_mul(
                            comb[:, gt * 8:(gt + 1) * 8], gtmp[:], gsc[:]), deps=[tg12])
                        tg_last = tg13
                if DEBUG:
                    E.op("gp", lambda g: g.dma_start(dbg["d_x2"], x2_scr.ap()),
                         deps=[x2w_toks[-1]])
                    E.op("gp", lambda g: g.dma_start(dbg["d_comb"], comb[:]), deps=[tg_last])
                E.flush()

            with ExitStack() as es5:
                sb5 = lambda n, shape, dt=F32: es5.enter_context(nc.sbuf_tensor(n, shape, dt))
                psf5 = lambda n, shape: es5.enter_context(nc.psum_tensor(n, shape, F32))
                yacc = sb5("yacc", [128, 16 * D], F32)
                es5b = es5.enter_context(ExitStack())
                sb5b = lambda n, shape, dt=F32: es5b.enter_context(nc.sbuf_tensor(n, shape, dt))
                h_e = sb5b("h_e", [128, 16 * TPC], BF16)
                w1b0 = sb5b("w1b0", [128, 1024], BF16)
                w3b0 = sb5b("w3b0", [128, 1024], BF16)
                w2full = sb5b("w2full", [128, 16 * 1024], BF16)
                silu_t = sb5b("silu_t", [128, 512], BF16)
                ytmp0 = sb5b("ytmp0", [128, 512], F32)
                ytmp1 = sb5b("ytmp1", [128, 512], F32)
                pH1a = psf5("pH1a", [128, 512])
                pH1b = psf5("pH1b", [128, 512])
                pH3a = psf5("pH3a", [128, 512])
                pH3b = psf5("pH3b", [128, 512])
                pY1 = psf5("pY1", [128, 512])
                pY2 = psf5("pY2", [128, 512])

                def w1_off(e, which):
                    return (e * 3 * PIECE if e < E_EXP else base_sh) + which * PIECE

                def w2_off(e):
                    return (e * 3 * PIECE if e < E_EXP else base_sh) + 2 * PIECE

                experts = [E_EXP] + list(range(E_EXP))
                w1_rd = {}
                w3_rd = {}
                w1_ld = {}
                w3_ld = {}
                prev_y_last = None
                y_free = [None, None]
                fsilu = None
                fytmp = [None, None]
                yt = {}
                ph_free = {"1a": None, "1b": None, "3a": None, "3b": None}

                for ei, e in enumerate(experts):
                    w1b = w1b0
                    w3b = w3b0
                    w2tk = []
                    for ib in range(16):
                        rk = ib // 2
                        w2tk.append(E.op("sp", lambda s, e=e, ib=ib, rk=rk: s.dma_start(
                            w2full[:, ib * 1024:(ib + 1) * 1024],
                            bass.AP(agW_out, rk * WCHUNK + w2_off(e) + (ib % 2) * 128 * 1024,
                                    [[1024, 128], [1, 1024]])),
                            deps=[ag2, prev_y_last] if ib == 0 else (), sem="w2q"))
                    hA_last = None
                    for ib in range(16):
                        prev1 = w1_rd.get((ei, ib - 1), w1_rd.get((ei - 1, 15)))
                        prev3 = w3_rd.get((ei, ib - 1), w3_rd.get((ei - 1, 15)))
                        tl1 = E.op("sp", lambda s, e=e, ib=ib, w1b=w1b: s.dma_start(
                            bass.AP(w1b, 0, [[1024, 128], [128, 8], [1, 128]]),
                            bass.AP(agW_out, w1_off(e, 0) + ib * 128,
                                    [[2048, 128], [WCHUNK, 8], [1, 128]])),
                            deps=[ag2, prev1], sem="w1q")
                        w1_ld[(ei, ib)] = tl1
                        tl3 = E.op("sp", lambda s, e=e, ib=ib, w3b=w3b: s.dma_start(
                            bass.AP(w3b, 0, [[1024, 128], [128, 8], [1, 128]]),
                            bass.AP(agW_out, w1_off(e, 1) + ib * 128,
                                    [[2048, 128], [WCHUNK, 8], [1, 128]])),
                            deps=[prev3], sem="w3q")
                        w3_ld[(ei, ib)] = tl3
                        for tg in range(4):
                            pH1 = pH1a if tg % 2 == 0 else pH1b
                            pH3 = pH3a if tg % 2 == 0 else pH3b
                            k1 = "1a" if tg % 2 == 0 else "1b"
                            k3 = "3a" if tg % 2 == 0 else "3b"
                            tm1 = None
                            for mb in range(8):
                                tm1 = E.op("pe", lambda t, mb=mb, tg=tg, pH1=pH1, w1b=w1b:
                                    t.matmul(pH1[:, 0:512], w1b[:, mb * 128:(mb + 1) * 128],
                                             x2Thi[:, mb * TPC + tg * 512: mb * TPC + (tg + 1) * 512],
                                             start=(mb == 0), stop=(mb == 7)),
                                    deps=[tl1, ph_free[k1], prev_y_last, hi_ev] if mb == 0 else (),
                                    inc=(mb == 7))
                            w1_rd[(ei, ib)] = tm1
                            tm3 = None
                            for mb in range(8):
                                tm3 = E.op("pe", lambda t, mb=mb, tg=tg, pH3=pH3, w3b=w3b:
                                    t.matmul(pH3[:, 0:512], w3b[:, mb * 128:(mb + 1) * 128],
                                             x2Thi[:, mb * TPC + tg * 512: mb * TPC + (tg + 1) * 512],
                                             start=(mb == 0), stop=(mb == 7)),
                                    deps=[tl3, ph_free[k3]] if mb == 0 else (), inc=(mb == 7))
                            w3_rd[(ei, ib)] = tm3
                            tsl = E.op("ac", lambda a, pH1=pH1: a.activation(silu_t[:],
                                pH1[:, 0:512], AF.Silu), deps=[tm1, fsilu])
                            ph_free[k1] = tsl
                            tmu = E.op("dv", lambda v, ib=ib, tg=tg, pH3=pH3: v.tensor_mul(
                                h_e[:, ib * TPC + tg * 512: ib * TPC + (tg + 1) * 512],
                                silu_t[:], pH3[:, 0:512]), deps=[tsl, tm3])
                            fsilu = tmu
                            ph_free[k3] = tmu
                            yt[(ib, tg)] = tmu
                            hA_last = tmu
                    for ts_i in range(16):
                        for ch in range(2):
                            par = (ts_i * 2 + ch) % 2
                            pY = pY1 if par == 0 else pY2
                            ty = None
                            for ib in range(16):
                                ty = E.op("pe", lambda t, ib=ib, ts_i=ts_i, ch=ch, pY=pY:
                                    t.matmul(pY[:, 0:512],
                                             h_e[:, ib * TPC + ts_i * 128:ib * TPC + (ts_i + 1) * 128],
                                             w2full[:, ib * 1024 + ch * 512: ib * 1024 + (ch + 1) * 512],
                                             start=(ib == 0), stop=(ib == 15)),
                                    deps=[yt[(ib, ts_i // 4)], w2tk[-1], y_free[par]]
                                         if ib == 0 else [yt[(ib, ts_i // 4)]],
                                    inc=(ib == 15))
                            if e == E_EXP:
                                tya = E.op("ac", lambda a, ts_i=ts_i, ch=ch, pY=pY:
                                    a.activation(
                                        yacc[:, ts_i * D + ch * 512: ts_i * D + (ch + 1) * 512],
                                        pY[:, 0:512], AF.Copy), deps=[ty])
                                y_free[par] = tya
                                yt[("acc", ts_i, ch)] = tya
                            else:
                                ytmp = ytmp0 if par == 0 else ytmp1
                                tya = E.op("ac", lambda a, ts_i=ts_i, ch=ch, pY=pY, e=e, ytmp=ytmp:
                                    a.activation(ytmp[:], pY[:, 0:512], AF.Copy,
                                                 scale=comb[:, ts_i * 8 + e: ts_i * 8 + e + 1]),
                                    deps=[ty, tg_last, fytmp[par]])
                                y_free[par] = tya
                                tyb = E.op("dv", lambda v, ts_i=ts_i, ch=ch, ytmp=ytmp: v.tensor_add(
                                    yacc[:, ts_i * D + ch * 512: ts_i * D + (ch + 1) * 512],
                                    yacc[:, ts_i * D + ch * 512: ts_i * D + (ch + 1) * 512],
                                    ytmp[:]), deps=[tya, yt[("acc", ts_i, ch)]])
                                fytmp[par] = tyb
                                yt[("acc", ts_i, ch)] = tyb
                            prev_y_last = ty
                E.flush()
                es5b.close()

                x2r = sb5("x2r", [128, 1024], F32)
                o6 = sb5("o6", [128, 1024], F32)
                l6a = sb5("l6a", [128, 1], F32)
                l6b = sb5("l6b", [128, 1], F32)
                sq6 = sb5("sq6", [128, 1024], F32)
                fx2r = None
                out_toks = []
                for ts_i in range(16):
                    tx = E.op("sp", lambda s, ts_i=ts_i: s.dma_start(x2r[:],
                        x2_scr.ap()[ts_i * 128:(ts_i + 1) * 128, :]),
                        deps=[x2w_toks[-1], fx2r], sem="x2q")
                    tadd = E.op("dv", lambda v, ts_i=ts_i: v.tensor_add(o6[:], x2r[:],
                        yacc[:, ts_i * D:(ts_i + 1) * D]),
                        deps=[tx, yt[("acc", ts_i, 0)], yt[("acc", ts_i, 1)]])
                    u1 = E.op("dv", lambda v: v.tensor_reduce(l6a[:, 0:1], o6[:],
                        axis=mybir.AxisListType.X, op=ALU.add, negate=True), deps=[tadd])
                    u2 = E.op("dv", lambda v: v.tensor_scalar_mul(l6a[:, 0:1], l6a[:, 0:1],
                        1.0 / D), deps=[u1])
                    u3 = E.op("dv", lambda v: v.tensor_scalar(o6[:], o6[:], l6a[:, 0:1],
                        None, op0=ALU.add), deps=[u2])
                    u4 = E.op("ac", lambda a: a.activation(sq6[:], o6[:], AF.Square), deps=[u3])
                    u5 = E.op("dv", lambda v: v.tensor_reduce(l6b[:, 0:1], sq6[:],
                        axis=mybir.AxisListType.X, op=ALU.add), deps=[u4])
                    u6 = E.op("ac", lambda a: a.activation(l6b[:, 0:1], l6b[:, 0:1], AF.Ln,
                        scale=1.0 / D, bias=ceps5[:, 0:1]), deps=[u5])
                    u7 = E.op("ac", lambda a: a.activation(l6b[:, 0:1], l6b[:, 0:1], AF.Exp,
                        scale=-0.5), deps=[u6])
                    u8 = E.op("dv", lambda v: v.tensor_scalar(o6[:], o6[:], l6b[:, 0:1],
                        None, op0=ALU.mult), deps=[u7])
                    tout6 = E.op("sp", lambda s, ts_i=ts_i: s.dma_start(
                        y_out[ts_i * 128:(ts_i + 1) * 128, :], o6[:]), deps=[u8], sem="yq")
                    out_toks.append(tout6)
                    fx2r = tout6

                fin = out_toks[-1]
                E.raw("gp", lambda g, fin=fin: g.wait_ge(E.sem["yq"], fin[1]))
                E.raw("sp", lambda s, fin=fin: s.wait_ge(E.sem["yq"], fin[1]))
                E.flush()

    return nc


def kernel(**inputs):
    x = np.asarray(inputs["L0_attn_x"])[0]
    get = lambda n: np.asarray(inputs[n])
    in_maps = []
    for c in range(NCORES):
        hs = slice(2 * c, 2 * c + 2)
        m = {
            "x_rows": np.ascontiguousarray(x[c * TPC:(c + 1) * TPC]),
            "wq_s": np.ascontiguousarray(get("L0_attn_Wq")[:, c * 128:(c + 1) * 128]),
            "wk_s": np.ascontiguousarray(get("L0_attn_Wk")[:, c * 128:(c + 1) * 128]),
            "wv_s": np.ascontiguousarray(get("L0_attn_Wv")[:, c * 128:(c + 1) * 128]),
            "wrq_s": np.ascontiguousarray(get("L0_attn_Wrq")[:, c * 96:(c + 1) * 96]),
            "wrk_s": np.ascontiguousarray(get("L0_attn_Wrk")[:, c * 96:(c + 1) * 96]),
            "convq_s": np.ascontiguousarray(get("L0_attn_conv_q")[c * 128:(c + 1) * 128]),
            "convk_s": np.ascontiguousarray(get("L0_attn_conv_k")[c * 128:(c + 1) * 128]),
            "convv_s": np.ascontiguousarray(get("L0_attn_conv_v")[c * 128:(c + 1) * 128]),
            "wf_s": np.ascontiguousarray(get("L0_attn_Wf")[:, hs]),
            "wl_s": np.ascontiguousarray(get("L0_attn_Wl")[:, hs]),
            "bfbl": np.ascontiguousarray(np.concatenate(
                [get("L0_attn_bf")[hs], get("L0_attn_bl")[hs]])),
            "gs_s": np.ascontiguousarray(get("L0_attn_graph_scale")[hs]),
            "qn_w": np.ascontiguousarray(get("L0_attn_qn_w")),
            "kn_w": np.ascontiguousarray(get("L0_attn_kn_w")),
            "on_w": np.ascontiguousarray(get("L0_attn_on_w")),
            "wo": np.ascontiguousarray(get("L0_attn_Wo")),
            "gate_w": np.ascontiguousarray(get("L0_ffn_gate_w")),
            "gate_b": np.ascontiguousarray(get("L0_ffn_gate_bias")),
            "w1p": np.ascontiguousarray(get("L0_ffn_w1")[:, :, c * 128:(c + 1) * 128]),
            "w3p": np.ascontiguousarray(get("L0_ffn_w3")[:, :, c * 128:(c + 1) * 128]),
            "w2p": np.ascontiguousarray(get("L0_ffn_w2")[:, :, c * 256:(c + 1) * 256]),
            "sw1p": np.ascontiguousarray(get("L0_ffn_sw1")[:, c * 128:(c + 1) * 128]),
            "sw3p": np.ascontiguousarray(get("L0_ffn_sw3")[:, c * 128:(c + 1) * 128]),
            "sw2p": np.ascontiguousarray(get("L0_ffn_sw2")[:, c * 256:(c + 1) * 256]),
        }
        in_maps.append(m)
    nc = build_program()
    res = run_bass_kernel_spmd(nc, in_maps, core_ids=list(range(NCORES)))
    out = np.concatenate([np.asarray(res.results[c]["y"]) for c in range(NCORES)], axis=0)
    return out.reshape(1, S, D).astype(np.float32)


